# version 09
# baseline (speedup 1.0000x reference)
import numpy as np
from contextlib import ExitStack

import concourse.bass as bass
import concourse.mybir as mybir
from concourse.bass_utils import run_bass_kernel_spmd

import bass_rust as _br
F32 = mybir.dt.float32
BF16 = mybir.dt.bfloat16
I32 = mybir.dt.int32
IOA = _br.IndirectOffsetOnAxis
AF = mybir.ActivationFunctionType
ALU = mybir.AluOpType
AX = mybir.AxisListType

D = 1024
T = 4096
TOK = 2048
NB = 16
NBF = 32
NHL = 8
HS = 64
HID = 2816
HC = 22
NE = 8
LOGSCALE = float(np.log(1.0 / 8.0))
EPS_ATT = 1e-7
LN_EPS = 1e-5
RMS_EPS = 1e-5
NEGBIG = -1.0e9
BIG = 1000.0
GROUPS2 = [[0, 1], [2, 3], [4, 5], [6, 7]]

_PENDQ = {"qs": ("PEND_qs", 0), "ge": ("PEND_ge", 0)}


class Buf:
    __slots__ = ("w", "r", "name")

    def __init__(self, name=""):
        self.w = None
        self.r = []
        self.name = name


class Prog:
    def __init__(self, nc, es):
        self.nc = nc
        self.E = {"te": nc.tensor, "ve": nc.vector, "se": nc.scalar,
                  "ge": nc.gpsimd, "qs": nc.sync}
        self.sem = {}
        for name in ("te", "ve", "se", "ge", "qsd", "ged"):
            self.sem[name] = es.enter_context(nc.semaphore("S_" + name))
        self.cnt = {"te": 0, "ve": 0, "se": 0, "ge": 0}
        self.dcnt = {"qs": 0, "ge": 0}
        self.pend = {"qs": [], "ge": []}
        self.waited = {k: {} for k in self.E}

    def _wait(self, ekey, toks):
        w = self.waited[ekey]
        need = {}
        for t in toks:
            if t is None:
                continue
            s, v = t
            if s.startswith("PEND"):
                if s == "PEND_" + ekey:
                    continue
                raise RuntimeError("buffer consumed before dma flush (%s)" % s)
            if s == "te" and ekey == "te":
                continue
            if v > w.get(s, 0) and v > need.get(s, 0):
                need[s] = v
        for s, v in sorted(need.items()):
            self.E[ekey].wait_ge(self.sem[s], v)
            w[s] = v

    def _collect(self, r, w):
        toks = []
        for b in r:
            toks.append(b.w)
        for b in w:
            toks.append(b.w)
            toks.extend(b.r)
        return toks

    def op(self, ekey, fn, r=(), w=()):
        self._wait(ekey, self._collect(r, w))
        inst = fn(self.E[ekey])
        self.cnt[ekey] += 1
        inst.then_inc(self.sem[ekey], 1)
        t = (ekey, self.cnt[ekey])
        for b in r:
            b.r.append(t)
        for b in w:
            b.w = t
            b.r = []
        return t

    def dma(self, q, out_ap, in_ap, r=(), w=(), indirect=None, indirect_in=None, **kw):
        self._wait(q, self._collect(r, w))
        if indirect is not None:
            idx, ax = indirect
            inst = self.E[q].indirect_dma_start(out_ap, IOA(idx[:], ax), in_ap, None,
                                                oob_is_err=False)
        elif indirect_in is not None:
            idx, ax = indirect_in
            inst = self.E[q].indirect_dma_start(out_ap, None, in_ap, IOA(idx[:], ax),
                                                oob_is_err=False)
        else:
            inst = self.E[q].dma_start(out=out_ap, in_=in_ap, **kw)
        dsem = "qsd" if q == "qs" else "ged"
        inst.then_inc(self.sem[dsem], 16)
        self.dcnt[q] += 16
        pt = _PENDQ[q]
        for b in w:
            b.w = pt
            b.r = []
        for b in r:
            b.r.append(pt)
        self.pend[q].append((r, w))

    def flush(self, q):
        if not self.pend[q]:
            return
        dsem = "qsd" if q == "qs" else "ged"
        v = self.dcnt[q]
        self.E[q].wait_ge(self.sem[dsem], v)
        self.waited[q][dsem] = v
        t = (dsem, v)
        pt = _PENDQ[q]
        for r, w in self.pend[q]:
            for b in w:
                if b.w == pt:
                    b.w = t
            for b in r:
                b.r = [t if x == pt else x for x in b.r]
        self.pend[q] = []

    def cc(self, kind, aluop, in_ap, out_ap, r=(), w=()):
        return self.op("ge", lambda g: g.collective_compute(
            kind, aluop, replica_groups=GROUPS2,
            ins=[in_ap], outs=[out_ap]), r=r, w=w)

    def barrier(self):
        self.flush("qs")
        self.flush("ge")
        toks = [(e, self.cnt[e]) for e in self.cnt]
        toks += [("qsd", self.dcnt["qs"]), ("ged", self.dcnt["ge"])]
        for ekey in self.E:
            self._wait(ekey, toks)


def _sb(es, nc, name, shape, dtype):
    h = es.enter_context(nc.sbuf_tensor(name, shape, dtype))
    return h, Buf(name)


def _ps(es, nc, name, shape, dtype):
    esz = mybir.dt.size(dtype)
    n = 1
    for d in shape[1:]:
        n *= d
    per = n * esz
    pad = (-per) % 2048
    if pad == 0:
        h = es.enter_context(nc.psum_tensor(name, shape, dtype))
        return h, Buf(name)
    flat = n + pad // esz
    h = es.enter_context(nc.psum_tensor(name, [shape[0], flat], dtype))
    ap = h[:, 0:n]
    if len(shape) == 3:
        ap = ap.rearrange("p (a b) -> p a b", a=shape[1])
    return ap, Buf(name)


def emit_ln(p, src_ap, src_bufs, out_ap, out_buf, sq, sqb, st, stb):
    p.op("ve", lambda v: v.reduce_sum(st[:, 0:1], src_ap, axis=AX.X), r=src_bufs, w=[stb])
    p.op("se", lambda s: s.activation(sq[:], src_ap, AF.Square), r=src_bufs, w=[sqb])
    p.op("ve", lambda v: v.reduce_sum(st[:, 1:2], sq[:], axis=AX.X), r=[sqb], w=[stb])
    p.op("ve", lambda v: v.tensor_scalar_mul(st[:, 2:3], st[:, 0:1], 1.0 / D), r=[stb], w=[stb])
    p.op("ve", lambda v: v.tensor_tensor(st[:, 3:4], st[:, 2:3], st[:, 2:3], ALU.mult), r=[stb], w=[stb])
    p.op("ve", lambda v: v.tensor_scalar_mul(st[:, 4:5], st[:, 1:2], 1.0 / D), r=[stb], w=[stb])
    p.op("ve", lambda v: v.tensor_tensor(st[:, 5:6], st[:, 4:5], st[:, 3:4], ALU.subtract), r=[stb], w=[stb])
    p.op("se", lambda s: s.activation(st[:, 6:7], st[:, 5:6], AF.Ln, bias=LN_EPS), r=[stb], w=[stb])
    p.op("se", lambda s: s.activation(st[:, 7:8], st[:, 6:7], AF.Exp, scale=-0.5), r=[stb], w=[stb])
    p.op("ve", lambda v: v.tensor_scalar(out_ap, src_ap, st[:, 2:3], st[:, 7:8],
                                         ALU.subtract, ALU.mult),
         r=list(src_bufs) + [stb], w=[out_buf])


def build_nc(dump=()):
    nc = bass.Bass("TRN2", target_bir_lowering=False)

    x_own = nc.dram_tensor("x_own", [TOK, D], F32, kind="ExternalInput").ap()
    x_full = nc.dram_tensor("x_full", [T, D], F32, kind="ExternalInput").ap()
    w_qkvT = nc.dram_tensor("w_qkvT", [D, 1544], F32, kind="ExternalInput").ap()
    wi = nc.dram_tensor("wi", [8, 128, 8, 128], F32, kind="ExternalInput").ap()
    wf = nc.dram_tensor("wf", [8, 128, 8, 128], F32, kind="ExternalInput").ap()
    wg = nc.dram_tensor("wg", [8, 128, 8, 128], F32, kind="ExternalInput").ap()
    wo = nc.dram_tensor("wo", [8, 128, 8, 128], F32, kind="ExternalInput").ap()
    conv_i = nc.dram_tensor("conv_i", [128, 8, 4], F32, kind="ExternalInput").ap()
    conv_f = nc.dram_tensor("conv_f", [128, 8, 4], F32, kind="ExternalInput").ap()
    gnw = nc.dram_tensor("gnw", [128, 8], F32, kind="ExternalInput").ap()
    gwt = nc.dram_tensor("gwt", [128, 8, 8], F32, kind="ExternalInput").ap()
    ebias = nc.dram_tensor("ebias", [1, 8], F32, kind="ExternalInput").ap()
    w1 = nc.dram_tensor("w1", [NE, HC, 128, D], F32, kind="ExternalInput").ap()
    w2 = nc.dram_tensor("w2", [NE, 2, 128, 11, D], F32, kind="ExternalInput").ap()
    w3 = nc.dram_tensor("w3", [NE, HC, 128, D], F32, kind="ExternalInput").ap()
    mA = nc.dram_tensor("mA", [128, 1], F32, kind="ExternalInput").ap()
    mB = nc.dram_tensor("mB", [128, 1], F32, kind="ExternalInput").ap()
    y_out = nc.dram_tensor("y_out", [TOK, D], F32, kind="ExternalOutput").ap()
    dumps = {}
    for nm in dump:
        shp = {"res1": [TOK, D], "res2": [TOK, D], "res3": [TOK, D],
               "sco": [128, NB, 8], "lgg": [8, T], "yag": [8192, 512]}[nm]
        dt = BF16 if nm == "yag" else F32
        dumps[nm] = nc.dram_tensor("d_" + nm, shp, dt, kind="ExternalOutput").ap()

    y_pub = nc.dram_tensor("y_pub", [T, 512], BF16)
    y_ag = nc.dram_tensor("y_ag", [2 * T, 512], BF16)
    b_pub = nc.dram_tensor("b_pub", [128, 24], BF16)
    b_ag = nc.dram_tensor("b_ag", [256, 24], BF16)
    c_pub = nc.dram_tensor("c_pub", [128, 8], F32)
    c_ag = nc.dram_tensor("c_ag", [256, 8], F32)
    w13b = nc.dram_tensor("w13b", [NE, 2, 128, 2, 11, D], BF16)
    w2b = nc.dram_tensor("w2b", [NE, 2, 128, 11, D], BF16)
    xl_d = nc.dram_tensor("xl_d", [TOK, D], BF16)
    tsl_d = nc.dram_tensor("tsl_d", [5120, 2], F32)
    osl_d = nc.dram_tensor("osl_d", [5120, D], BF16)
    yb = {n: Buf(n) for n in ("y_pub", "y_ag", "b_pub", "b_ag", "c_pub", "c_ag",
                              "w13b", "w2b", "xl_d", "tsl_d", "osl_d")}

    es = ExitStack()
    with es:
        p = Prog(nc, es)

        identf, identf_b = _sb(es, nc, "identf", [128, 128], F32)
        identb, identb_b = _sb(es, nc, "identb", [128, 128], BF16)
        onesf, onesf_b = _sb(es, nc, "onesf", [1, 128], F32)
        onescb, onescb_b = _sb(es, nc, "onescb", [128, 1], BF16)
        iota8, iota8_b = _sb(es, nc, "iota8", [128, 8], F32)
        iotaB8, iotaB8_b = _sb(es, nc, "iotaB8", [128, 8], F32)
        ebc, ebc_b = _sb(es, nc, "ebc", [128, 8], F32)
        ebrow, ebrow_b = _sb(es, nc, "ebrow", [1, 8], F32)
        cwi, cwi_b = _sb(es, nc, "cwi", [128, 8, 4], F32)
        cwf, cwf_b = _sb(es, nc, "cwf", [128, 8, 4], F32)
        gnws, gnws_b = _sb(es, nc, "gnws", [128, 8], F32)
        gwts, gwts_b = _sb(es, nc, "gwts", [128, 8, 8], BF16)
        mAs, mAs_b = _sb(es, nc, "mAs", [128, 1], F32)
        mBs, mBs_b = _sb(es, nc, "mBs", [128, 1], F32)
        iosc, iosc_b = _sb(es, nc, "iosc", [128, 128], F32)
        trix, trix_b = _sb(es, nc, "trix", [128, 128], BF16)
        iotac, iotac_b = _sb(es, nc, "iotac", [128, 1], F32)
        iota40, iota40_b = _sb(es, nc, "iota40", [8, 40], F32)
        iotok, iotok_b = _sb(es, nc, "iotok", [128, 16], F32)
        os18, os18_b = _sb(es, nc, "os18", [1, 8], F32)
        wst = [_sb(es, nc, "wst%d" % i, [128, D], BF16) for i in range(4)]
        cst, cst_b = _sb(es, nc, "cst", [128, 4], F32)
        for i, val in enumerate((LN_EPS, EPS_ATT, 1.0, RMS_EPS)):
            p.op("ve", lambda v, i=i, val=val: v.memset(cst[:, i:i + 1], val), w=[cst_b])
            nc.const_aps.register(F32, val, cst[:, i:i + 1])

        p.op("ge", lambda g: g.iota(iosc[:], pattern=[[1, 128]], base=0,
                                    channel_multiplier=-1,
                                    allow_small_or_imprecise_dtypes=True), w=[iosc_b])
        p.op("ve", lambda v: v.tensor_single_scalar(identf[:], iosc[:], 0.0, ALU.is_equal),
             r=[iosc_b], w=[identf_b])
        p.op("ve", lambda v: v.tensor_copy(identb[:], identf[:]), r=[identf_b], w=[identb_b])
        p.op("ve", lambda v: v.memset(onesf[:], 1.0), w=[onesf_b])
        p.op("ve", lambda v: v.memset(onescb[:], 1.0), w=[onescb_b])
        p.op("ge", lambda g: g.iota(iota8[:], pattern=[[1, 8]], base=0,
                                    channel_multiplier=0,
                                    allow_small_or_imprecise_dtypes=True), w=[iota8_b])
        p.op("ve", lambda v: v.tensor_single_scalar(trix[:], iosc[:], 0.0, ALU.is_gt),
             r=[iosc_b], w=[trix_b])
        p.op("ge", lambda g: g.iota(iotac[:], pattern=[[1, 1]], base=0,
                                    channel_multiplier=1,
                                    allow_small_or_imprecise_dtypes=True), w=[iotac_b])
        p.op("ge", lambda g: g.iota(iota40[:], pattern=[[128, 40]], base=0,
                                    channel_multiplier=0,
                                    allow_small_or_imprecise_dtypes=True), w=[iota40_b])
        p.op("ge", lambda g: g.iota(iotok[:], pattern=[[128, 16]], base=0,
                                    channel_multiplier=1,
                                    allow_small_or_imprecise_dtypes=True), w=[iotok_b])
        p.op("ve", lambda v: v.memset(os18[:], 1.0), w=[os18_b])
        p.op("ve", lambda v: v.tensor_single_scalar(iotaB8[:], iota8[:], -BIG, ALU.add),
             r=[iota8_b], w=[iotaB8_b])
        p.dma("qs", cwi[:], conv_i, w=[cwi_b])
        p.dma("qs", cwf[:], conv_f, w=[cwf_b])
        p.dma("qs", gnws[:], gnw, w=[gnws_b])
        p.dma("qs", mAs[:], mA, w=[mAs_b])
        p.dma("qs", mBs[:], mB, w=[mBs_b])
        p.dma("qs", ebrow[:], ebias, w=[ebrow_b])
        p.dma("ge", gwts[:], gwt, w=[gwts_b])
        p.flush("qs")
        p.flush("ge")

        with nc.psum_tensor("ebp", [128, 512], F32) as ebp:
            ebp_b = Buf("ebp")
            p.op("te", lambda t: t.matmul(ebp[:, 0:8], onesf[:], ebrow[:], start=True, stop=True),
                 r=[onesf_b, ebrow_b], w=[ebp_b])
            p.op("ve", lambda v: v.tensor_copy(ebc[:], ebp[:, 0:8]), r=[ebp_b], w=[ebc_b])
        p.barrier()

        wc_steps = []

        def _wc_pair(srcs):
            def go():
                for i, (sap, dap) in enumerate(srcs):
                    sbh, sbb = wst[i]
                    p.dma("ge", sbh[:], sap, w=[sbb])
                p.flush("ge")
                for i, (sap, dap) in enumerate(srcs):
                    sbh, sbb = wst[i]
                    p.dma("ge", dap, sbh[:], r=[sbb], w=[yb["w13b"], yb["w2b"]])
                p.flush("ge")
            return go

        _grp = []
        for e in range(NE):
            for hc in range(HC):
                half, hl = divmod(hc, 11)
                _grp.append((w1[e, hc], w13b.ap()[e, half, :, 0, hl, :]))
                _grp.append((w3[e, hc], w13b.ap()[e, half, :, 1, hl, :]))
                _grp.append((w2[e, half, :, hl, :], w2b.ap()[e, half, :, hl, :]))
                if len(_grp) >= 4:
                    wc_steps.append(_wc_pair(_grp[:4]))
                    _grp = _grp[4:]
        if _grp:
            wc_steps.append(_wc_pair(_grp))
        wc_pos = [0]

        def wcast_emit(k):
            n = min(k, len(wc_steps) - wc_pos[0])
            for _ in range(n):
                wc_steps[wc_pos[0]]()
                wc_pos[0] += 1

        with ExitStack() as s1:
            s1.enter_context(nc.named_scope("p1_attn"))
            XT, XT_b = _sb(s1, nc, "XT", [128, 8, T], BF16)
            wall, wall_b = _sb(s1, nc, "wall", [128, 8, 1544], BF16)
            lgG8, lgG8_b = _sb(s1, nc, "lgG8", [8, T], F32)
            lgGn, lgGn_b = _sb(s1, nc, "lgGn", [128, 32, 8], F32)

            p.dma("ge", wall[:], w_qkvT.rearrange("(dc p) o -> p dc o", p=128), w=[wall_b])
            p.flush("ge")
            wcast_emit(2)

            with ExitStack() as sA:
                xs = [_sb(sA, nc, "p1xs%d" % i, [128, D], F32) for i in range(2)]
                xln = [_sb(sA, nc, "p1xl%d" % i, [128, D], BF16) for i in range(2)]
                sq, sq_b = _sb(sA, nc, "p1sq", [128, D], F32)
                st, st_b = _sb(sA, nc, "p1st", [128, 8], F32)
                g8, g8_b = _sb(sA, nc, "p1g8", [8, T], F32)
                gs1, gs1_b = _sb(sA, nc, "p1gs1", [8, 512], F32)
                gs2, gs2_b = _sb(sA, nc, "p1gs2", [8, 512], F32)
                psb, psb_b = _ps(sA, nc, "p1psb", [128, 128], BF16)
                gp, gp_b = _ps(sA, nc, "p1gp", [8, 512], F32)
                tpp, tpp_b = _ps(sA, nc, "p1tpp", [128, 8], F32)

                for blk in range(NBF):
                    pb = blk % 2
                    xsh, xsb = xs[pb]
                    xlh, xlb = xln[pb]
                    p.dma("qs", xsh[:], x_full[blk * 128:(blk + 1) * 128, :], w=[xsb])
                    p.flush("qs")
                    emit_ln(p, xsh[:], [xsb], xlh[:], xlb, sq, sq_b, st, st_b)
                    for dc in range(8):
                        p.op("te", lambda t, dc=dc, xlh=xlh: t.transpose(
                            psb[:], xlh[:, dc * 128:(dc + 1) * 128], identb[:]),
                            r=[xlb, identb_b], w=[psb_b])
                        p.op("se", lambda s, dc=dc, blk=blk: s.copy(
                            XT[:, dc, blk * 128:(blk + 1) * 128], psb[:]),
                            r=[psb_b], w=[XT_b])
                    if blk % 4 == 3:
                        wcast_emit(1)

                for s8 in range(8):
                    for dc in range(8):
                        p.op("te", lambda t, dc=dc, s8=s8: t.matmul(
                            gp[:], wall[:, dc, 1536:1544], XT[:, dc, s8 * 512:(s8 + 1) * 512],
                            start=(dc == 0), stop=(dc == 7)),
                            r=[wall_b, XT_b], w=[gp_b])
                    p.op("se", lambda s, s8=s8: s.copy(g8[:, s8 * 512:(s8 + 1) * 512], gp[:]),
                         r=[gp_b], w=[g8_b])
                for s8 in range(8):
                    sl = slice(s8 * 512, (s8 + 1) * 512)
                    p.op("se", lambda s, sl=sl: s.activation(
                        gs1[:], g8[:, sl], AF.Abs), r=[g8_b], w=[gs1_b])
                    p.op("se", lambda s: s.activation(gs1[:], gs1[:], AF.Exp, scale=-1.0),
                         r=[gs1_b], w=[gs1_b])
                    p.op("se", lambda s: s.activation(gs1[:], gs1[:], AF.Ln, bias=1.0),
                         r=[gs1_b], w=[gs1_b])
                    p.op("ve", lambda v, sl=sl: v.tensor_single_scalar(
                        gs2[:], g8[:, sl], 0.0, ALU.min), r=[g8_b], w=[gs2_b])
                    p.op("ve", lambda v, sl=sl: v.tensor_tensor(
                        g8[:, sl], gs2[:], gs1[:], ALU.subtract),
                        r=[gs2_b, gs1_b], w=[g8_b])
                p.op("ve", lambda v: v.memset(lgG8[:], 1.0), w=[lgG8_b])
                p.op("ve", lambda v: v.tensor_tensor_scan(
                    lgG8[:], lgG8[:], g8[:], 0.0, ALU.mult, ALU.add),
                    r=[g8_b], w=[lgG8_b])
                if "lgg" in dumps:
                    p.dma("qs", dumps["lgg"], lgG8[:], r=[lgG8_b])
                    p.flush("qs")
                for j in range(32):
                    p.op("te", lambda t, j=j: t.transpose(
                        tpp[:], lgG8[:, j * 128:(j + 1) * 128], identf[0:8, 0:8]),
                        r=[lgG8_b, identf_b], w=[tpp_b])
                    p.op("se", lambda s, j=j: s.activation(
                        lgGn[:, j, :], tpp[:], AF.Copy, scale=-1.0),
                        r=[tpp_b], w=[lgGn_b])
                wcast_emit(2)
            p.barrier()

            with ExitStack() as sB:
                KTt = es_kt = sB.enter_context(nc.sbuf_tensor("KTt", [128, T], BF16))
                QTt = sB.enter_context(nc.sbuf_tensor("QTt", [128, T], BF16))
                KT = [(KTt[0:64, :], Buf("KT0")), (KTt[64:128, :], Buf("KT1"))]
                QT = [(QTt[0:64, :], Buf("QT0")), (QTt[64:128, :], Buf("QT1"))]
                Vsb = [_sb(sB, nc, "Vsb%d" % i, [128, 32, 65], BF16) for i in range(2)]
                VTs, VTs_b = _sb(sB, nc, "VTs", [64, T], BF16)
                y2, y2_b = _sb(sB, nc, "y2", [128, 32, 128], BF16)
                As2 = [_sb(sB, nc, "As%d" % i, [128, 2, 512], F32) for i in range(2)]
                Ls2 = [_sb(sB, nc, "Ls%d" % i, [128, 2, 512], F32) for i in range(2)]
                Ps2 = [_sb(sB, nc, "Ps%d" % i, [128, 2, 512], BF16) for i in range(2)]
                Rsb2 = [_sb(sB, nc, "Rsb%d" % i, [128, 512], F32) for i in range(2)]
                rr, rr_b = _sb(sB, nc, "rr", [1, 512], F32)
                rr02 = [_sb(sB, nc, "rr0%d" % i, [1, 512], F32) for i in range(2)]
                msk4, msk4_b = _sb(sB, nc, "msk4", [128, 4, 512], BF16)
                mtmp, mtmp_b = _sb(sB, nc, "mtmp", [128, 512], F32)
                Oe, Oe_b = _sb(sB, nc, "Oe", [65, 512], F32)
                rcp, rcp_b = _sb(sB, nc, "rcp", [128, 1], F32)
                S2 = [_ps(sB, nc, "S_ps%d" % i, [128, 2, 512], F32) for i in range(2)]
                O_ps, O_ps_b = _ps(sB, nc, "O_ps", [65, 512], F32)
                mp_ps, mp_ps_b = _ps(sB, nc, "mp_ps", [128, 512], F32)
                ftr_ps, ftr_ps_b = _ps(sB, nc, "ftr_ps", [128, 65], F32)
                vtr_ps, vtr_ps_b = _ps(sB, nc, "vtr_ps", [128, 64], BF16)

                for k in range(4):
                    p.op("ge", lambda g, k=k: g.iota(
                        mtmp[:], pattern=[[1, 512]], base=-128 * k, channel_multiplier=-1,
                        allow_small_or_imprecise_dtypes=True), w=[mtmp_b])
                    p.op("ve", lambda v, k=k: v.tensor_single_scalar(
                        msk4[:, k, :], mtmp[:], 0.0, ALU.is_lt), r=[mtmp_b], w=[msk4_b])
                p.op("ve", lambda v: v.tensor_scalar_mul(msk4[:], msk4[:], NEGBIG),
                     r=[msk4_b], w=[msk4_b])
                for i in range(2):
                    p.op("ve", lambda v, i=i: v.memset(Vsb[i][0][:, :, 64:65], 1.0),
                         w=[Vsb[i][1]])

                def setup_pieces(h):
                    hb = h % 2
                    kth, ktb = KT[hb]
                    qth, qtb = QT[hb]
                    vh, vb2 = Vsb[hb]
                    pieces = []
                    for s8 in range(8):
                        def go(s8=s8, kth=kth, ktb=ktb, qth=qth, qtb=qtb, vh=vh, vb2=vb2, h=h):
                            for (dst, dstb, off) in ((kth, ktb, 512), (qth, qtb, 0),
                                                     (VTs, VTs_b, 1024)):
                                for dc in range(8):
                                    p.op("te", lambda t, dc=dc, s8=s8, off=off, h=h: t.matmul(
                                        mp_ps[0:64, :],
                                        wall[:, dc, off + h * 64: off + (h + 1) * 64],
                                        XT[:, dc, s8 * 512:(s8 + 1) * 512],
                                        start=(dc == 0), stop=(dc == 7)),
                                        r=[wall_b, XT_b], w=[mp_ps_b])
                                p.op("ve", lambda v, s8=s8, dst=dst: v.tensor_copy(
                                    dst[:, s8 * 512:(s8 + 1) * 512], mp_ps[0:64, :]),
                                    r=[mp_ps_b], w=[dstb])
                            for j in range(4):
                                jj = s8 * 4 + j
                                p.op("te", lambda t, jj=jj: t.transpose(
                                    vtr_ps[:], VTs[:, jj * 128:(jj + 1) * 128],
                                    identb[0:64, 0:64]),
                                    r=[VTs_b, identb_b], w=[vtr_ps_b])
                                p.op("se", lambda s, jj=jj, vh=vh: s.copy(
                                    vh[:, jj, 0:64], vtr_ps[:]),
                                    r=[vtr_ps_b], w=[vb2])
                        pieces.append(go)
                    return pieces

                def attend(h, next_pieces):
                    hb = h % 2
                    kth, ktb = KT[hb]
                    qth, qtb = QT[hb]
                    vh, vb2 = Vsb[hb]
                    pend_fin = [None]

                    def emit_qk(s8, g0, gn):
                        gp = (g0 // 2) % 2
                        Sp, Sp_b = S2[gp]
                        for k in range(gn):
                            c = g0 + k
                            p.op("te", lambda t, k=k, c=c, s8=s8, kth=kth, qth=qth, Sp=Sp: t.matmul(
                                Sp[:, k, :], kth[:, c * 128:(c + 1) * 128],
                                qth[:, s8 * 512:(s8 + 1) * 512], start=True, stop=True),
                                r=[ktb, qtb], w=[Sp_b])

                    def emit_absln(s8, g0, gn):
                        gp = (g0 // 2) % 2
                        Sp, Sp_b = S2[gp]
                        Ah, Ah_b = As2[gp]
                        Lh, Lh_b = Ls2[gp]
                        p.op("se", lambda s, gn=gn, Ah=Ah, Sp=Sp: s.activation(
                            Ah[:, 0:gn, :], Sp[:, 0:gn, :], AF.Abs),
                            r=[Sp_b], w=[Ah_b])
                        p.op("se", lambda s, gn=gn, Ah=Ah, Lh=Lh: s.activation(
                            Lh[:, 0:gn, :], Ah[:, 0:gn, :], AF.Ln, bias=EPS_ATT),
                            r=[Ah_b], w=[Lh_b])

                    def emit_elem_pv(s8, g0, gn, nch):
                        gp = (g0 // 2) % 2
                        Sp, Sp_b = S2[gp]
                        Ah, Ah_b = As2[gp]
                        Lh, Lh_b = Ls2[gp]
                        Ph, Ph_b = Ps2[gp]
                        Rsb, Rsb_b = Rsb2[s8 % 2]
                        for k in range(gn):
                            c = g0 + k
                            p.op("ve", lambda v, k=k, Ah=Ah, Lh=Lh, Rsb=Rsb: v.tensor_tensor(
                                Ah[:, k, :], Lh[:, k, :], Rsb[:], ALU.add),
                                r=[Lh_b, Rsb_b], w=[Ah_b])
                            if c >= 4 * s8:
                                p.op("ve", lambda v, k=k, c=c, s8=s8, Ah=Ah: v.tensor_tensor(
                                    Ah[:, k, :], Ah[:, k, :], msk4[:, c - 4 * s8, :],
                                    ALU.add),
                                    r=[msk4_b], w=[Ah_b])
                        for k in range(gn):
                            c = g0 + k
                            p.op("se", lambda s, k=k, c=c, h=h, Ah=Ah, Ph=Ph: s.activation(
                                Ph[:, k, :], Ah[:, k, :], AF.Exp, scale=2.0,
                                bias=lgGn[:, c, h:h + 1]),
                                r=[Ah_b, lgGn_b], w=[Ph_b])
                        for k in range(gn):
                            c = g0 + k
                            p.op("te", lambda t, k=k, c=c, vh=vh, Ph=Ph: t.matmul(
                                O_ps[:], vh[:, c, :], Ph[:, k, :],
                                start=(c == 0), stop=(c == nch - 1)),
                                r=[vb2, Ph_b], w=[O_ps_b])

                    def make_fin(s8, h):
                        def fin():
                            p.op("se", lambda s: s.copy(Oe[:], O_ps[:]), r=[O_ps_b], w=[Oe_b])
                            for j in range(4):
                                p.op("te", lambda t, j=j: t.transpose(
                                    ftr_ps[:], Oe[:, j * 128:(j + 1) * 128],
                                    identf[0:65, 0:65]),
                                    r=[Oe_b, identf_b], w=[ftr_ps_b])
                                p.op("ve", lambda v: v.reciprocal(rcp[:], ftr_ps[:, 64:65]),
                                     r=[ftr_ps_b], w=[rcp_b])
                                p.op("ve", lambda v, j=j, s8=s8, h=h: v.tensor_scalar_mul(
                                    y2[:, s8 * 4 + j, (h % 2) * 64:(h % 2) * 64 + 64],
                                    ftr_ps[:, 0:64], rcp[:]),
                                    r=[ftr_ps_b, rcp_b], w=[y2_b])
                        return fin

                    rh0, rb0 = rr02[0]
                    p.dma("qs", rh0[:], lgG8[h:h + 1, 0:512], r=[lgG8_b], w=[rb0])
                    for s8 in range(8):
                        rpar = s8 % 2
                        rr0h, rr0b = rr02[rpar]
                        Rsb, Rsb_b = Rsb2[rpar]
                        nch = 4 * (s8 + 1)
                        emit_qk(s8, 0, 2)
                        if pend_fin[0] is not None:
                            pend_fin[0]()
                            pend_fin[0] = None
                        p.flush("qs")
                        if s8 < 7:
                            nh_, nb_ = rr02[(s8 + 1) % 2]
                            p.dma("qs", nh_[:], lgG8[h:h + 1, (s8 + 1) * 512:(s8 + 2) * 512],
                                  r=[lgG8_b], w=[nb_])
                        p.op("ve", lambda v, rr0h=rr0h: v.tensor_scalar(
                            rr[:], rr0h[:], 0.5, LOGSCALE / 2.0, ALU.mult, ALU.add),
                            r=[rr0b], w=[rr_b])
                        p.op("te", lambda t: t.matmul(mp_ps[:], onesf[:], rr[:],
                                                      start=True, stop=True),
                             r=[onesf_b, rr_b], w=[mp_ps_b])
                        p.op("ve", lambda v, Rsb=Rsb: v.tensor_copy(Rsb[:], mp_ps[:]),
                             r=[mp_ps_b], w=[Rsb_b])
                        groups = list(range(0, nch, 2))
                        emit_absln(s8, 0, 2)
                        for gi, g0 in enumerate(groups):
                            gn = min(2, nch - g0)
                            if gi + 1 < len(groups):
                                g1 = groups[gi + 1]
                                gn1 = min(2, nch - g1)
                                emit_qk(s8, g1, gn1)
                                emit_absln(s8, g1, gn1)
                            elif s8 < 4 and next_pieces:
                                next_pieces.pop(0)()
                            emit_elem_pv(s8, g0, gn, nch)
                        if s8 >= 4 and next_pieces:
                            next_pieces.pop(0)()
                        pend_fin[0] = make_fin(s8, h)
                    pend_fin[0]()
                    while next_pieces:
                        next_pieces.pop(0)()
                    if h % 2 == 1:
                        h2 = h // 2
                        for q4 in range(4):
                            p.dma("qs", y_pub.ap().rearrange("(j p) c -> p j c", p=128)
                                  [:, q4 * 8:(q4 + 1) * 8, h2 * 128:(h2 + 1) * 128],
                                  y2[:, q4 * 8:(q4 + 1) * 8, :], r=[y2_b], w=[yb["y_pub"]])
                        p.flush("qs")

                for piece in setup_pieces(0):
                    piece()
                for h in range(NHL):
                    wcast_emit(17)
                    np_ = setup_pieces(h + 1) if h + 1 < NHL else []
                    attend(h, np_)
                wcast_emit(len(wc_steps))
                p.cc("AllGather", ALU.bypass, y_pub.ap().opt(), y_ag.ap().opt(),
                     r=[yb["y_pub"]], w=[yb["y_ag"]])
                if "yag" in dumps:
                    p.dma("qs", dumps["yag"], y_ag.ap(), r=[yb["y_ag"]])
                    p.flush("qs")
            p.barrier()

        with ExitStack() as sR:
            res, res_b = _sb(sR, nc, "res", [128, NB, D], F32)

            with ExitStack() as sRB:
                sRB.enter_context(nc.named_scope("res_build"))
                xs0, xs0_b = _sb(sRB, nc, "rbx", [128, D], F32)
                ya, ya_b = _sb(sRB, nc, "rbya", [128, 512], BF16)
                yB2, yB2_b = _sb(sRB, nc, "rbyb", [128, 512], BF16)
                for j in range(NB):
                    p.dma("qs", xs0[:], x_own[j * 128:(j + 1) * 128, :], w=[xs0_b])
                    p.flush("qs")
                    p.op("ve", lambda v, j=j: v.tensor_copy(res[:, j, :], xs0[:]),
                         r=[xs0_b], w=[res_b])
                    for slot in range(2):
                        p.dma("qs", ya[:], y_ag.ap()[slot * T + j * 128: slot * T + (j + 1) * 128, :],
                              r=[yb["y_ag"]], w=[ya_b])
                        p.dma("qs", yB2[:], y_ag.ap()[slot * T + TOK + j * 128: slot * T + TOK + (j + 1) * 128, :],
                              r=[yb["y_ag"]], w=[yB2_b])
                        p.flush("qs")
                        sl = slice(slot * 512, (slot + 1) * 512)
                        p.op("ve", lambda v, j=j, sl=sl: v.scalar_tensor_tensor(
                            res[:, j, sl], ya[:], mAs[:], res[:, j, sl], ALU.mult, ALU.add),
                            r=[ya_b, mAs_b], w=[res_b])
                        p.op("ve", lambda v, j=j, sl=sl: v.scalar_tensor_tensor(
                            res[:, j, sl], yB2[:], mBs[:], res[:, j, sl], ALU.mult, ALU.add),
                            r=[yB2_b, mBs_b], w=[res_b])
            if "res1" in dumps:
                p.dma("qs", dumps["res1"].rearrange("(j p) d -> p j d", p=128), res[:],
                      r=[res_b])
                p.flush("qs")
            p.barrier()

            def emit_ffn(tag, dump_sco=False):
                with ExitStack() as sF:
                    sF.enter_context(nc.named_scope(tag + "_ffn"))
                    sctm, sctm_b = _sb(sF, nc, tag + "sctm", [128, NB, 8], F32)
                    t8a, t8a_b = _sb(sF, nc, tag + "t8a", [128, 8], F32)
                    t8b, t8b_b = _sb(sF, nc, tag + "t8b", [128, 8], F32)
                    t8c, t8c_b = _sb(sF, nc, tag + "t8c", [128, 8], F32)
                    t1c, t1c_b = _sb(sF, nc, tag + "t1c", [128, 1], F32)
                    mkb, mkb_b = _sb(sF, nc, tag + "mkb", [128, NB, 8], BF16)
                    m1t, m1t_b = _sb(sF, nc, tag + "m1t", [128, NB, 8], F32)
                    m2t, m2t_b = _sb(sF, nc, tag + "m2t", [128, NB, 8], F32)
                    sr1, sr1_b = _sb(sF, nc, tag + "sr1", [128, NB], F32)
                    sr2, sr2_b = _sb(sF, nc, tag + "sr2", [128, NB], F32)
                    cntA, cntA_b = _sb(sF, nc, tag + "cntA", [1, 128], F32)
                    r8 = {}
                    for nm in ("cnt", "al", "incl", "starts", "run", "sbrow", "tmp8"):
                        r8[nm] = _sb(sF, nc, tag + nm, [1, 8], F32)
                    blk2e, blk2e_b = _sb(sF, nc, tag + "blk2e", [1, 40], F32)
                    widx = [_sb(sF, nc, tag + "widx%d" % i, [128, 40], I32) for i in range(2)]
                    slotf, slotf_b = _sb(sF, nc, tag + "slotf", [128, 8], F32)
                    sfa, sfa_b = _sb(sF, nc, tag + "sfa", [128, 8], F32)
                    sfb, sfb_b = _sb(sF, nc, tag + "sfb", [128, 8], F32)
                    idxs4 = [_sb(sF, nc, tag + "idxs%d" % i, [128, 1], I32) for i in range(4)]
                    scs4 = [_sb(sF, nc, tag + "scs%d" % i, [128, 2], F32) for i in range(4)]
                    scsrc, scsrc_b = scs4[0]
                    inclT, inclT_b = _sb(sF, nc, tag + "inclT", [8, 1], F32)
                    cmpb, cmpb_b = _sb(sF, nc, tag + "cmpb", [8, 40], BF16)
                    with ExitStack() as sFP:
                        XTf, XTf_b = _sb(sFP, nc, tag + "XTf", [128, 8, TOK], BF16)
                        xlf, xlf_b = _sb(sFP, nc, tag + "xlf", [128, D], BF16)
                        sqf, sqf_b = _sb(sFP, nc, tag + "sqf", [128, D], F32)
                        stf, stf_b = _sb(sFP, nc, tag + "stf", [128, 8], F32)
                        lgt, lgt_b = _sb(sFP, nc, tag + "lgt", [8, 512], F32)
                        ltm, ltm_b = _sb(sFP, nc, tag + "ltm", [128, NB, 8], F32)
                        psbF, psbF_b = _ps(sFP, nc, tag + "psbF", [128, 128], BF16)
                        lgp, lgp_b = _ps(sFP, nc, tag + "lgp", [8, 512], F32)
                        tpf, tpf_b = _ps(sFP, nc, tag + "tpf", [128, 8], F32)

                        for j in range(NB):
                            emit_ln(p, res[:, j, :], [res_b], xlf[:], xlf_b,
                                    sqf, sqf_b, stf, stf_b)
                            p.dma("qs", xl_d.ap()[j * 128:(j + 1) * 128, :], xlf[:],
                                  r=[xlf_b], w=[yb["xl_d"]])
                            p.flush("qs")
                            for dc in range(8):
                                p.op("te", lambda t, dc=dc: t.transpose(
                                    psbF[:], xlf[:, dc * 128:(dc + 1) * 128], identb[:]),
                                    r=[xlf_b, identb_b], w=[psbF_b])
                                p.op("se", lambda s, dc=dc, j=j: s.copy(
                                    XTf[:, dc, j * 128:(j + 1) * 128], psbF[:]),
                                    r=[psbF_b], w=[XTf_b])
                        for s4 in range(4):
                            for dc in range(8):
                                p.op("te", lambda t, dc=dc, s4=s4: t.matmul(
                                    lgp[:], gwts[:, dc, :], XTf[:, dc, s4 * 512:(s4 + 1) * 512],
                                    start=(dc == 0), stop=(dc == 7)),
                                    r=[gwts_b, XTf_b], w=[lgp_b])
                            p.op("se", lambda s: s.copy(lgt[:], lgp[:]), r=[lgp_b], w=[lgt_b])
                            for jj in range(4):
                                j = s4 * 4 + jj
                                p.op("te", lambda t, jj=jj: t.transpose(
                                    tpf[:], lgt[:, jj * 128:(jj + 1) * 128], identf[0:8, 0:8]),
                                    r=[lgt_b, identf_b], w=[tpf_b])
                                p.op("ve", lambda v, j=j: v.tensor_copy(ltm[:, j, :], tpf[:]),
                                     r=[tpf_b], w=[ltm_b])
                        for j in range(NB):
                            jj = j
                            p.op("ve", lambda v: v.reduce_max(t1c[:], ltm[:, jj, :], axis=AX.X),
                                 r=[ltm_b], w=[t1c_b])
                            p.op("ve", lambda v: v.tensor_scalar_mul(t1c[:], t1c[:], -1.0),
                                 r=[t1c_b], w=[t1c_b])
                            p.op("se", lambda s: s.activation(t8a[:], ltm[:, jj, :], AF.Exp,
                                                              bias=t1c[:]),
                                 r=[ltm_b, t1c_b], w=[t8a_b])
                            p.op("ve", lambda v: v.reduce_sum(t1c[:], t8a[:], axis=AX.X),
                                 r=[t8a_b], w=[t1c_b])
                            p.op("ve", lambda v: v.reciprocal(t1c[:], t1c[:]),
                                 r=[t1c_b], w=[t1c_b])
                            p.op("ve", lambda v: v.tensor_scalar_mul(sctm[:, jj, :], t8a[:], t1c[:]),
                                 r=[t8a_b, t1c_b], w=[sctm_b])
                            p.op("ve", lambda v: v.tensor_tensor(t8a[:], sctm[:, jj, :], ebc[:],
                                                                 ALU.add),
                                 r=[sctm_b, ebc_b], w=[t8a_b])
                            p.op("ve", lambda v: v.reduce_max(t1c[:], t8a[:], axis=AX.X),
                                 r=[t8a_b], w=[t1c_b])
                            p.op("ve", lambda v: v.tensor_scalar(t8b[:], t8a[:], t1c[:], None,
                                                                 ALU.is_equal),
                                 r=[t8a_b, t1c_b], w=[t8b_b])
                            p.op("ve", lambda v: v.tensor_tensor(t8b[:], t8b[:], iotaB8[:],
                                                                 ALU.mult),
                                 r=[iotaB8_b], w=[t8b_b])
                            p.op("ve", lambda v: v.tensor_single_scalar(t8b[:], t8b[:], BIG,
                                                                        ALU.add),
                                 r=[t8b_b], w=[t8b_b])
                            p.op("ve", lambda v: v.tensor_reduce(t1c[:], t8b[:], axis=AX.X,
                                                                 op=ALU.min),
                                 r=[t8b_b], w=[t1c_b])
                            p.op("ve", lambda v: v.tensor_scalar(t8b[:], iota8[:], t1c[:], None,
                                                                 ALU.is_equal),
                                 r=[iota8_b, t1c_b], w=[t8b_b])
                            p.op("ve", lambda v: v.tensor_copy(m1t[:, jj, :], t8b[:]),
                                 r=[t8b_b], w=[m1t_b])
                            p.op("ve", lambda v: v.scalar_tensor_tensor(
                                t8c[:], t8b[:], -BIG, t8a[:], ALU.mult, ALU.add),
                                r=[t8b_b, t8a_b], w=[t8c_b])
                            p.op("ve", lambda v: v.reduce_max(t1c[:], t8c[:], axis=AX.X),
                                 r=[t8c_b], w=[t1c_b])
                            p.op("ve", lambda v: v.tensor_scalar(t8c[:], t8c[:], t1c[:], None,
                                                                 ALU.is_equal),
                                 r=[t1c_b], w=[t8c_b])
                            p.op("ve", lambda v: v.tensor_tensor(t8c[:], t8c[:], iotaB8[:],
                                                                 ALU.mult),
                                 r=[iotaB8_b], w=[t8c_b])
                            p.op("ve", lambda v: v.tensor_single_scalar(t8c[:], t8c[:], BIG,
                                                                        ALU.add),
                                 r=[t8c_b], w=[t8c_b])
                            p.op("ve", lambda v: v.tensor_reduce(t1c[:], t8c[:], axis=AX.X,
                                                                 op=ALU.min),
                                 r=[t8c_b], w=[t1c_b])
                            p.op("ve", lambda v: v.tensor_scalar(t8c[:], iota8[:], t1c[:], None,
                                                                 ALU.is_equal),
                                 r=[iota8_b, t1c_b], w=[t8c_b])
                            p.op("ve", lambda v: v.tensor_copy(m2t[:, jj, :], t8c[:]),
                                 r=[t8c_b], w=[m2t_b])
                            p.op("ve", lambda v: v.tensor_tensor(t8b[:], t8b[:], t8c[:], ALU.add),
                                 r=[t8c_b], w=[t8b_b])
                            p.op("ve", lambda v: v.tensor_copy(mkb[:, jj, :], t8b[:]),
                                 r=[t8b_b], w=[mkb_b])
                            p.op("ve", lambda v: v.tensor_tensor(sctm[:, jj, :], sctm[:, jj, :],
                                                                 t8b[:], ALU.mult),
                                 r=[t8b_b], w=[sctm_b])
                        if dump_sco and "sco" in dumps:
                            p.dma("qs", dumps["sco"], sctm[:], r=[sctm_b])
                            p.flush("qs")
                    p.barrier()

                    with ExitStack() as sRT:
                        cntp, cntp_b = _ps(sRT, nc, tag + "cntp", [1, 8], F32)
                        prefp, prefp_b = _ps(sRT, nc, tag + "prefp", [128, 8], F32)
                        sbbcp, sbbcp_b = _ps(sRT, nc, tag + "sbbcp", [128, 8], F32)
                        tixp, tixp_b = _ps(sRT, nc, tag + "tixp", [8, 1], F32)
                        b2ep, b2ep_b = _ps(sRT, nc, tag + "b2ep", [1, 40], F32)
                        e40p, e40p_b = _ps(sRT, nc, tag + "e40p", [128, 40], F32)

                        p.op("ve", lambda v: v.memset(r8["run"][0][:], 0.0), w=[r8["run"][1]])
                        for j in range(NB):
                            p.op("te", lambda t, j=j: t.matmul(
                                cntp[:], onescb[:], mkb[:, j, :], start=True, stop=True),
                                r=[onescb_b, mkb_b], w=[cntp_b])
                            p.op("ve", lambda v, j=j: v.tensor_copy(
                                cntA[0:1, j * 8:(j + 1) * 8], cntp[:]), r=[cntp_b], w=[cntA_b])
                            p.op("ve", lambda v, j=j: v.tensor_tensor(
                                r8["run"][0][:], r8["run"][0][:], cntA[0:1, j * 8:(j + 1) * 8],
                                ALU.add), r=[cntA_b], w=[r8["run"][1]])
                        p.op("ve", lambda v: v.tensor_copy(r8["cnt"][0][:], r8["run"][0][:]),
                             r=[r8["run"][1]], w=[r8["cnt"][1]])
                        p.op("ve", lambda v: v.memset(r8["al"][0][:], 0.0), w=[r8["al"][1]])
                        for m in range(16):
                            p.op("ve", lambda v, m=m: v.tensor_single_scalar(
                                r8["tmp8"][0][:], r8["cnt"][0][:], float(m * 128), ALU.is_gt),
                                r=[r8["cnt"][1]], w=[r8["tmp8"][1]])
                            p.op("ve", lambda v: v.scalar_tensor_tensor(
                                r8["al"][0][:], r8["tmp8"][0][:], 128.0, r8["al"][0][:],
                                ALU.mult, ALU.add), r=[r8["tmp8"][1]], w=[r8["al"][1]])
                        p.op("ve", lambda v: v.tensor_tensor_scan(
                            r8["incl"][0][:], os18[:], r8["al"][0][:], 0.0, ALU.mult, ALU.add),
                            r=[os18_b, r8["al"][1]], w=[r8["incl"][1]])
                        p.op("ve", lambda v: v.tensor_tensor(
                            r8["starts"][0][:], r8["incl"][0][:], r8["al"][0][:], ALU.subtract),
                            r=[r8["incl"][1], r8["al"][1]], w=[r8["starts"][1]])
                        p.op("te", lambda t: t.transpose(tixp[:], r8["incl"][0][:],
                                                         identf[0:1, 0:1]),
                             r=[r8["incl"][1], identf_b], w=[tixp_b])
                        p.op("ve", lambda v: v.tensor_copy(inclT[:], tixp[:]),
                             r=[tixp_b], w=[inclT_b])
                        p.op("ve", lambda v: v.tensor_scalar(
                            cmpb[:], iota40[:], inclT[:], None, ALU.is_ge),
                            r=[iota40_b, inclT_b], w=[cmpb_b])
                        p.op("te", lambda t: t.matmul(b2ep[:], onescb[0:8, :], cmpb[:],
                                                      start=True, stop=True),
                             r=[onescb_b, cmpb_b], w=[b2ep_b])
                        p.op("ve", lambda v: v.tensor_copy(blk2e[:], b2ep[:]),
                             r=[b2ep_b], w=[blk2e_b])
                        p.op("te", lambda t: t.matmul(e40p[:], onesf[:], blk2e[:],
                                                      start=True, stop=True),
                             r=[onesf_b, blk2e_b], w=[e40p_b])
                        for hh in range(2):
                            wih, wib = widx[hh]
                            p.op("ve", lambda v, wih=wih, hh=hh: v.tensor_scalar(
                                wih[:], e40p[:], 256.0, float(hh * 128), ALU.mult, ALU.add),
                                r=[e40p_b], w=[wib])
                            p.op("ve", lambda v, wih=wih: v.tensor_scalar(
                                wih[:], wih[:], iotac[:], None, ALU.add),
                                r=[iotac_b], w=[wib])

                        p.op("ve", lambda v: v.memset(scsrc[:], 0.0), w=[scsrc_b])
                        for blk in range(40):
                            p.dma("qs", tsl_d.ap()[blk * 128:(blk + 1) * 128, :], scsrc[:],
                                  r=[scsrc_b], w=[yb["tsl_d"]])
                        p.flush("qs")

                        p.op("ve", lambda v: v.memset(r8["run"][0][:], 0.0), w=[r8["run"][1]])
                        for j in range(NB):
                            p.op("ve", lambda v, j=j: v.tensor_tensor(
                                r8["sbrow"][0][:], r8["starts"][0][:], r8["run"][0][:], ALU.add),
                                r=[r8["starts"][1], r8["run"][1]], w=[r8["sbrow"][1]])
                            p.op("ve", lambda v, j=j: v.tensor_tensor(
                                r8["run"][0][:], r8["run"][0][:], cntA[0:1, j * 8:(j + 1) * 8],
                                ALU.add), r=[cntA_b], w=[r8["run"][1]])
                            p.op("te", lambda t, j=j: t.matmul(
                                prefp[:], trix[:], mkb[:, j, :], start=True, stop=True),
                                r=[trix_b, mkb_b], w=[prefp_b])
                            p.op("te", lambda t: t.matmul(
                                sbbcp[:], onesf[:], r8["sbrow"][0][:], start=True, stop=True),
                                r=[onesf_b, r8["sbrow"][1]], w=[sbbcp_b])
                            p.op("ve", lambda v, j=j: v.tensor_tensor(
                                sfa[:], prefp[:], m1t[:, j, :], ALU.mult),
                                r=[prefp_b, m1t_b], w=[sfa_b])
                            p.op("ve", lambda v, j=j: v.tensor_tensor(
                                sfa[:], prefp[:], mkb[:, j, :], ALU.mult),
                                r=[prefp_b, mkb_b], w=[sfa_b])
                            p.op("ve", lambda v, j=j: v.tensor_tensor(
                                sfb[:], sbbcp[:], mkb[:, j, :], ALU.mult),
                                r=[sbbcp_b, mkb_b], w=[sfb_b])
                            p.op("ve", lambda v: v.tensor_tensor(
                                slotf[:], sfa[:], sfb[:], ALU.add),
                                r=[sfa_b, sfb_b], w=[slotf_b])
                            p.op("ve", lambda v, j=j: v.tensor_scalar(
                                sfa[:], mkb[:, j, :], -BIG, BIG, ALU.mult, ALU.add),
                                r=[mkb_b], w=[sfa_b])
                            p.op("ve", lambda v: v.tensor_tensor(
                                slotf[:], slotf[:], sfa[:], ALU.add),
                                r=[sfa_b], w=[slotf_b])
                            p.op("ve", lambda v, j=j: v.tensor_tensor(
                                sfa[:], slotf[:], m1t[:, j, :], ALU.mult),
                                r=[slotf_b, m1t_b], w=[sfa_b])
                            p.op("ve", lambda v, j=j: v.reduce_sum(
                                sr1[:, j:j + 1], sfa[:], axis=AX.X), r=[sfa_b], w=[sr1_b])
                            p.op("ve", lambda v, j=j: v.tensor_tensor(
                                sfa[:], slotf[:], m2t[:, j, :], ALU.mult),
                                r=[slotf_b, m2t_b], w=[sfa_b])
                            p.op("ve", lambda v, j=j: v.reduce_sum(
                                sr2[:, j:j + 1], sfa[:], axis=AX.X), r=[sfa_b], w=[sr2_b])
                            for e in range(NE):
                                sch, scb = scs4[e % 4]
                                ixh, ixb = idxs4[e % 4]
                                p.op("ve", lambda v, j=j, sch=sch: v.tensor_copy(
                                    sch[:, 0:1], iotok[:, j:j + 1]), r=[iotok_b], w=[scb])
                                p.op("ve", lambda v, j=j, e=e, sch=sch: v.tensor_copy(
                                    sch[:, 1:2], sctm[:, j, e:e + 1]), r=[sctm_b], w=[scb])
                                p.op("ve", lambda v, e=e, ixh=ixh: v.tensor_copy(
                                    ixh[:], slotf[:, e:e + 1]), r=[slotf_b], w=[ixb])
                                p.dma("ge", tsl_d.ap(), sch[:], r=[scb, ixb],
                                      w=[yb["tsl_d"]],
                                      indirect=(ixh, 0))
                                if e % 4 == 3:
                                    p.flush("ge")
                        p.flush("ge")
                    p.barrier()

                    with ExitStack() as sF2:
                        w13u = [_sb(sF2, nc, tag + "w13u%d" % i, [128, 2, 11, D], BF16)
                                for i in range(2)]
                        w2u, w2u_b = _sb(sF2, nc, tag + "w2u", [128, 11, D], BF16)
                        tsb = [_sb(sF2, nc, tag + "tsb%d" % i, [128, 2], F32) for i in range(2)]
                        idxt = [_sb(sF2, nc, tag + "idxt%d" % i, [128, 1], I32) for i in range(2)]
                        idxw = [_sb(sF2, nc, tag + "idxw%d" % i, [128, 1], I32) for i in range(2)]
                        xg = [_sb(sF2, nc, tag + "xg%d" % i, [128, D], BF16) for i in range(2)]
                        xbt, xbt_b = _sb(sF2, nc, tag + "xbt", [128, 8, 128], BF16)
                        hbl, hbl_b = _sb(sF2, nc, tag + "hbl", [128, 11, 128], BF16)
                        ofm, ofm_b = _sb(sF2, nc, tag + "ofm", [128, 8, 128], F32)
                        otm, otm_b = _sb(sF2, nc, tag + "otm", [128, D], BF16)
                        s1b, s1b_b = _sb(sF2, nc, tag + "s1b", [128, 128], F32)
                        g1b, g1b_b = xg[0]
                        g2b, g2b_b = xg[1]
                        ir1, ir1_b = _sb(sF2, nc, tag + "ir1", [128, 1], I32)
                        ir2, ir2_b = _sb(sF2, nc, tag + "ir2", [128, 1], I32)
                        u1p = [_ps(sF2, nc, tag + "u1p%d" % i, [128, 128], F32) for i in range(2)]
                        u3p = [_ps(sF2, nc, tag + "u3p%d" % i, [128, 128], F32) for i in range(2)]
                        o_p = [_ps(sF2, nc, tag + "o_p%d" % i, [128, 128], F32) for i in range(2)]
                        trxp, trxp_b = _ps(sF2, nc, tag + "trxp", [128, 128], BF16)
                        trop, trop_b = _ps(sF2, nc, tag + "trop", [128, 128], F32)

                        w13view = w13b.ap().rearrange("e h p m l w -> (e h p) (m l w)")
                        w2view = w2b.ap().rearrange("e h p l w -> (e h p) (l w)")

                        idxw2 = [_sb(sF2, nc, tag + "idxw2_%d" % i, [128, 1], I32)
                                 for i in range(2)]

                        def ld_w13(blk, half):
                            up = (blk * 2 + half) % 2
                            wh, whb = w13u[up]
                            iw, iwb = idxw[up]
                            p.op("ve", lambda v, blk=blk, half=half, iw=iw: v.tensor_copy(
                                iw[:], widx[half][0][:, blk:blk + 1]),
                                r=[widx[half][1]], w=[iwb])
                            p.dma("ge", wh[:].rearrange("p a l w -> p (a l w)"), w13view,
                                  r=[iwb], w=[whb], indirect_in=(iw, 0))

                        def ld_w2(blk, half):
                            up = (blk * 2 + half) % 2
                            iw, iwb = idxw2[up]
                            p.op("ve", lambda v, blk=blk, half=half, iw=iw: v.tensor_copy(
                                iw[:], widx[half][0][:, blk:blk + 1]),
                                r=[widx[half][1]], w=[iwb])
                            p.dma("ge", w2u[:].rearrange("p l w -> p (l w)"), w2view,
                                  r=[iwb], w=[w2u_b], indirect_in=(iw, 0))

                        def ld_x(blk):
                            bp = blk % 2
                            th, tb = tsb[bp]
                            it, itb = idxt[bp]
                            xh, xb = xg[bp]
                            p.dma("qs", th[:], tsl_d.ap()[blk * 128:(blk + 1) * 128, :],
                                  r=[yb["tsl_d"]], w=[tb])
                            p.flush("qs")
                            p.op("ve", lambda v, th=th, it=it: v.tensor_copy(it[:], th[:, 0:1]),
                                 r=[tb], w=[itb])
                            p.dma("ge", xh[:], xl_d.ap(), r=[yb["xl_d"], itb], w=[xb],
                                  indirect_in=(it, 0))

                        ld_x(0)
                        ld_w13(0, 0)
                        units = [(b_, h_) for b_ in range(40) for h_ in (0, 1)]
                        for ui, (blk, half) in enumerate(units):
                            bp = blk % 2
                            th, tb = tsb[bp]
                            xh, xb = xg[bp]
                            if True:
                                up = (blk * 2 + half) % 2
                                wh, whb = w13u[up]
                                if ui + 1 < len(units):
                                    ld_w13(*units[ui + 1])
                                ld_w2(blk, half)
                                if half == 0 and blk < 39:
                                    ld_x(blk + 1)
                                p.flush("ge")
                                if half == 0:
                                    p.op("ve", lambda v, xh=xh, th=th: v.tensor_scalar_mul(
                                        xh[:], xh[:], th[:, 1:2]), r=[tb], w=[xb])
                                    for dc in range(8):
                                        p.op("te", lambda t, dc=dc, xh=xh: t.transpose(
                                            trxp[:], xh[:, dc * 128:(dc + 1) * 128], identb[:]),
                                            r=[xb, identb_b], w=[trxp_b])
                                        p.op("se", lambda s, dc=dc: s.copy(xbt[:, dc, :], trxp[:]),
                                             r=[trxp_b], w=[xbt_b])
                                for hl in range(11):
                                    pu = hl % 2
                                    up1, up1_b = u1p[pu]
                                    up3, up3_b = u3p[pu]
                                    for dc in range(8):
                                        p.op("te", lambda t, dc=dc, hl=hl, wh=wh, up1=up1: t.matmul(
                                            up1[:], wh[:, 0, hl, dc * 128:(dc + 1) * 128],
                                            xbt[:, dc, :], start=(dc == 0), stop=(dc == 7)),
                                            r=[whb, xbt_b], w=[up1_b])
                                    for dc in range(8):
                                        p.op("te", lambda t, dc=dc, hl=hl, wh=wh, up3=up3: t.matmul(
                                            up3[:], wh[:, 1, hl, dc * 128:(dc + 1) * 128],
                                            xbt[:, dc, :], start=(dc == 0), stop=(dc == 7)),
                                            r=[whb, xbt_b], w=[up3_b])
                                    p.op("se", lambda s, up1=up1: s.activation(
                                        s1b[:], up1[:], AF.Silu), r=[up1_b], w=[s1b_b])
                                    p.op("ve", lambda v, hl=hl, up3=up3: v.tensor_tensor(
                                        hbl[:, hl, :], up3[:], s1b[:], ALU.mult),
                                        r=[up3_b, s1b_b], w=[hbl_b])
                                for dc in range(8):
                                    po = dc % 2
                                    oph, opb = o_p[po]
                                    for hl in range(11):
                                        p.op("te", lambda t, hl=hl, dc=dc, oph=oph: t.matmul(
                                            oph[:], w2u[:, hl, dc * 128:(dc + 1) * 128],
                                            hbl[:, hl, :], start=(hl == 0), stop=(hl == 10)),
                                            r=[w2u_b, hbl_b], w=[opb])
                                    if half == 0:
                                        p.op("ve", lambda v, dc=dc, oph=oph: v.tensor_copy(
                                            ofm[:, dc, :], oph[:]), r=[opb], w=[ofm_b])
                                    else:
                                        p.op("ve", lambda v, dc=dc, oph=oph: v.tensor_tensor(
                                            ofm[:, dc, :], ofm[:, dc, :], oph[:], ALU.add),
                                            r=[opb], w=[ofm_b])
                            if half == 1:
                                for dc in range(8):
                                    p.op("te", lambda t, dc=dc: t.transpose(
                                        trop[:], ofm[:, dc, :], identf[:]),
                                        r=[ofm_b, identf_b], w=[trop_b])
                                    p.op("ve", lambda v, dc=dc: v.tensor_copy(
                                        otm[:, dc * 128:(dc + 1) * 128], trop[:]),
                                        r=[trop_b], w=[otm_b])
                                p.dma("qs", osl_d.ap()[blk * 128:(blk + 1) * 128, :], otm[:],
                                      r=[otm_b], w=[yb["osl_d"]])
                                p.flush("qs")
                        for j in range(NB):
                            p.op("ve", lambda v, j=j: v.tensor_copy(ir1[:], sr1[:, j:j + 1]),
                                 r=[sr1_b], w=[ir1_b])
                            p.op("ve", lambda v, j=j: v.tensor_copy(ir2[:], sr2[:, j:j + 1]),
                                 r=[sr2_b], w=[ir2_b])
                            p.dma("ge", g1b[:], osl_d.ap(), r=[yb["osl_d"], ir1_b], w=[g1b_b],
                                  indirect_in=(ir1, 0))
                            p.dma("ge", g2b[:], osl_d.ap(), r=[yb["osl_d"], ir2_b], w=[g2b_b],
                                  indirect_in=(ir2, 0))
                            p.flush("ge")
                            p.op("ve", lambda v, j=j: v.tensor_tensor(
                                res[:, j, :], res[:, j, :], g1b[:], ALU.add),
                                r=[g1b_b], w=[res_b])
                            p.op("ve", lambda v, j=j: v.tensor_tensor(
                                res[:, j, :], res[:, j, :], g2b[:], ALU.add),
                                r=[g2b_b], w=[res_b])
                p.barrier()

            emit_ffn("f1", dump_sco=True)
            if "res2" in dumps:
                p.dma("qs", dumps["res2"].rearrange("(j p) d -> p j d", p=128), res[:],
                      r=[res_b])
                p.flush("qs")
                p.barrier()

            with ExitStack() as sH:
                sH.enter_context(nc.named_scope("hgrn"))
                X3T, X3T_b = _sb(sH, nc, "X3T", [128, 8, TOK], BF16)
                ost, ost_b = _sb(sH, nc, "ost", [128, 8, TOK], BF16)
                bnd, bnd_b = _sb(sH, nc, "bnd", [128, 8, 3], BF16)
                hend, hend_b = _sb(sH, nc, "hend", [128, 8], F32)
                carry, carry_b = _sb(sH, nc, "carry", [128, 8], F32)
                cg0, cg0_b = _sb(sH, nc, "cg0", [128, 8], F32)
                bg0, bg0_b = _sb(sH, nc, "bg0", [128, 8, 3], BF16)

                with ExitStack() as sH0:
                    xlh, xlh_b = _sb(sH0, nc, "h_xl", [128, D], BF16)
                    sqh, sqh_b = _sb(sH0, nc, "h_sq", [128, D], F32)
                    sth, sth_b = _sb(sH0, nc, "h_st", [128, 8], F32)
                    psbH, psbH_b = _ps(sH0, nc, "h_psb", [128, 128], BF16)
                    for j in range(NB):
                        emit_ln(p, res[:, j, :], [res_b], xlh[:], xlh_b, sqh, sqh_b, sth, sth_b)
                        for dc in range(8):
                            p.op("te", lambda t, dc=dc: t.transpose(
                                psbH[:], xlh[:, dc * 128:(dc + 1) * 128], identb[:]),
                                r=[xlh_b, identb_b], w=[psbH_b])
                            p.op("se", lambda s, dc=dc, j=j: s.copy(
                                X3T[:, dc, j * 128:(j + 1) * 128], psbH[:]),
                                r=[psbH_b], w=[X3T_b])
                    p.op("ve", lambda v: v.tensor_copy(bnd[:], X3T[:, :, TOK - 3:TOK]),
                         r=[X3T_b], w=[bnd_b])
                    p.dma("qs", b_pub.ap(), bnd[:].rearrange("p h w -> p (h w)"),
                          r=[bnd_b], w=[yb["b_pub"]])
                    p.flush("qs")
                    p.cc("AllGather", ALU.bypass, b_pub.ap().opt(), b_ag.ap().opt(),
                         r=[yb["b_pub"]], w=[yb["b_ag"]])
                    p.dma("qs", bg0[:].rearrange("p h w -> p (h w)"), b_ag.ap()[0:128, :],
                          r=[yb["b_ag"]], w=[bg0_b])
                    p.flush("qs")
                    p.op("ve", lambda v: v.tensor_scalar_mul(bnd[:], bg0[:], mBs[:]),
                         r=[bg0_b, mBs_b], w=[bnd_b])
                p.barrier()

                def hgrn_pass(pass2):
                    with ExitStack() as sP:
                        wic = [_sb(sP, nc, "hwi%d%d" % (pass2, i), [128, 8, 128], BF16)
                               for i in range(2)]
                        wfc = [_sb(sP, nc, "hwf%d%d" % (pass2, i), [128, 8, 128], BF16)
                               for i in range(2)]
                        ipd, ipd_b = _sb(sP, nc, "hip%d" % pass2, [128, TOK + 3], BF16)
                        fpd, fpd_b = _sb(sP, nc, "hfp%d" % pass2, [128, TOK + 3], BF16)
                        fgr, fgr_b = _sb(sP, nc, "hfg%d" % pass2, [128, TOK], F32)
                        iir, iir_b = _sb(sP, nc, "hii%d" % pass2, [128, TOK], F32)
                        scs, scs_b = _sb(sP, nc, "hsc%d" % pass2, [128, TOK], F32)
                        c1, c1_b = _sb(sP, nc, "hc1%d" % pass2, [128, 512], F32)
                        c2, c2_b = _sb(sP, nc, "hc2%d" % pass2, [128, 512], F32)
                        sgx, sgx_b = _sb(sP, nc, "hsg%d" % pass2, [128, 512], F32)
                        slx, slx_b = _sb(sP, nc, "hsl%d" % pass2, [128, 512], F32)
                        ipp = [_ps(sP, nc, "hipp%d%d" % (pass2, i), [128, 512], F32)
                               for i in range(2)]
                        fpp = [_ps(sP, nc, "hfpp%d%d" % (pass2, i), [128, 512], F32)
                               for i in range(2)]
                        bpp, bpp_b = _ps(sP, nc, "hbpp%d" % pass2, [128, 3], F32)

                        for oc in range(8):
                            par = oc % 2
                            wih, wib = wic[par]
                            wfh, wfb = wfc[par]
                            p.dma("ge", wih[:], wi[oc], w=[wib])
                            p.dma("ge", wfh[:], wf[oc], w=[wfb])
                            p.flush("ge")
                            for (wh, wb, pad, padb) in ((wih, wib, ipd, ipd_b),
                                                        (wfh, wfb, fpd, fpd_b)):
                                for dc in range(8):
                                    p.op("te", lambda t, dc=dc, wh=wh: t.matmul(
                                        bpp[:], wh[:, dc, :], bnd[:, dc, :],
                                        start=(dc == 0), stop=(dc == 7)),
                                        r=[wb, bnd_b], w=[bpp_b])
                                p.op("se", lambda s, pad=pad: s.copy(pad[:, 0:3], bpp[:]),
                                     r=[bpp_b], w=[padb])
                                pp = ipp if pad is ipd else fpp
                                for s4 in range(4):
                                    ph, pb = pp[s4 % 2]
                                    for dc in range(8):
                                        p.op("te", lambda t, dc=dc, s4=s4, wh=wh, ph=ph: t.matmul(
                                            ph[:], wh[:, dc, :],
                                            X3T[:, dc, s4 * 512:(s4 + 1) * 512],
                                            start=(dc == 0), stop=(dc == 7)),
                                            r=[wb, X3T_b], w=[pb])
                                    p.op("se", lambda s, s4=s4, ph=ph, pad=pad: s.copy(
                                        pad[:, 3 + s4 * 512: 3 + (s4 + 1) * 512], ph[:]),
                                        r=[pb], w=[padb])
                            for s4 in range(4):
                                sl = slice(s4 * 512, (s4 + 1) * 512)
                                p.op("ve", lambda v, s4=s4, oc=oc: v.tensor_scalar_mul(
                                    c1[:], ipd[:, s4 * 512:s4 * 512 + 512], cwi[:, oc, 0:1]),
                                    r=[ipd_b, cwi_b], w=[c1_b])
                                for k in range(1, 4):
                                    p.op("ve", lambda v, s4=s4, k=k, oc=oc: v.scalar_tensor_tensor(
                                        c1[:], ipd[:, s4 * 512 + k:s4 * 512 + k + 512],
                                        cwi[:, oc, k:k + 1], c1[:], ALU.mult, ALU.add),
                                        r=[ipd_b, cwi_b], w=[c1_b])
                                p.op("ve", lambda v, s4=s4, oc=oc: v.tensor_scalar_mul(
                                    c2[:], fpd[:, s4 * 512:s4 * 512 + 512], cwf[:, oc, 0:1]),
                                    r=[fpd_b, cwf_b], w=[c2_b])
                                for k in range(1, 4):
                                    p.op("ve", lambda v, s4=s4, k=k, oc=oc: v.scalar_tensor_tensor(
                                        c2[:], fpd[:, s4 * 512 + k:s4 * 512 + k + 512],
                                        cwf[:, oc, k:k + 1], c2[:], ALU.mult, ALU.add),
                                        r=[fpd_b, cwf_b], w=[c2_b])
                                p.op("se", lambda s, sl=sl: s.activation(
                                    fgr[:, sl], c2[:], AF.Sigmoid), r=[c2_b], w=[fgr_b])
                                p.op("se", lambda s: s.activation(
                                    sgx[:], c1[:], AF.Sigmoid), r=[c1_b], w=[sgx_b])
                                p.op("ve", lambda v: v.tensor_tensor(
                                    slx[:], c1[:], sgx[:], ALU.mult), r=[c1_b, sgx_b], w=[slx_b])
                                p.op("ve", lambda v, sl=sl: v.tensor_scalar(
                                    sgx[:], fgr[:, sl], -1.0, 1.0, ALU.mult, ALU.add),
                                    r=[fgr_b], w=[sgx_b])
                                p.op("ve", lambda v, sl=sl: v.tensor_tensor(
                                    iir[:, sl], slx[:], sgx[:], ALU.mult),
                                    r=[slx_b, sgx_b], w=[iir_b])
                            if not pass2:
                                p.op("ve", lambda v: v.tensor_tensor_scan(
                                    scs[:], fgr[:], iir[:], 0.0, ALU.mult, ALU.add),
                                    r=[fgr_b, iir_b], w=[scs_b])
                                p.op("ve", lambda v, oc=oc: v.tensor_copy(
                                    hend[:, oc:oc + 1], scs[:, TOK - 1:TOK]),
                                    r=[scs_b], w=[hend_b])
                            else:
                                p.op("ve", lambda v, oc=oc: v.tensor_tensor_scan(
                                    ost[:, oc, :], fgr[:], iir[:], carry[:, oc:oc + 1],
                                    ALU.mult, ALU.add),
                                    r=[fgr_b, iir_b, carry_b], w=[ost_b])

                hgrn_pass(False)
                p.barrier()
                p.dma("qs", c_pub.ap(), hend[:], r=[hend_b], w=[yb["c_pub"]])
                p.flush("qs")
                p.cc("AllGather", ALU.bypass, c_pub.ap().opt(), c_ag.ap().opt(),
                     r=[yb["c_pub"]], w=[yb["c_ag"]])
                p.dma("qs", cg0[:], c_ag.ap()[0:128, :], r=[yb["c_ag"]], w=[cg0_b])
                p.flush("qs")
                p.op("ve", lambda v: v.tensor_scalar_mul(carry[:], cg0[:], mBs[:]),
                     r=[cg0_b, mBs_b], w=[carry_b])
                hgrn_pass(True)
                p.barrier()

                with ExitStack() as sN:
                    o2s, o2s_b = _sb(sN, nc, "n_o2", [128, TOK], BF16)
                    rrow, rrow_b = _sb(sN, nc, "n_rr", [1, TOK], F32)
                    gss, gss_b = _sb(sN, nc, "n_gs", [128, 512], F32)
                    gsl, gsl_b = _sb(sN, nc, "n_gl", [128, 512], F32)
                    ons, ons_b = _sb(sN, nc, "n_on", [128, 512], F32)
                    wgc = [_sb(sN, nc, "n_wg%d" % i, [128, 8, 128], BF16) for i in range(2)]
                    with ExitStack() as sN1:
                        rmp, rmp_b = _ps(sN1, nc, "n_rmp", [1, 4, 512], F32)
                        for oc in range(8):
                            p.op("ve", lambda v, oc=oc: v.tensor_tensor(
                                o2s[:], ost[:, oc, :], ost[:, oc, :], ALU.mult),
                                r=[ost_b], w=[o2s_b])
                            for s4 in range(4):
                                p.op("te", lambda t, s4=s4, oc=oc: t.matmul(
                                    rmp[:, s4, :], onescb[:],
                                    o2s[:, s4 * 512:(s4 + 1) * 512],
                                    start=(oc == 0), stop=(oc == 7)),
                                    r=[onescb_b, o2s_b], w=[rmp_b])
                        p.op("se", lambda s: s.copy(rrow[:], rmp[0:1, :, :].rearrange(
                            "x a b -> x (a b)")), r=[rmp_b], w=[rrow_b])
                    p.barrier()
                    p.op("se", lambda s: s.activation(rrow[:], rrow[:], AF.Ln,
                                                      scale=1.0 / D, bias=RMS_EPS),
                         r=[rrow_b], w=[rrow_b])
                    p.op("se", lambda s: s.activation(rrow[:], rrow[:], AF.Exp, scale=-0.5),
                         r=[rrow_b], w=[rrow_b])
                    with ExitStack() as sN2:
                        rbp, rbp_b = _ps(sN2, nc, "n_rbp", [128, 4, 512], F32)
                        gpp = [_ps(sN2, nc, "n_gp%d" % i, [128, 512], F32) for i in range(2)]
                        for s4 in range(4):
                            p.op("te", lambda t, s4=s4: t.matmul(
                                rbp[:, s4, :], onesf[:], rrow[0:1, s4 * 512:(s4 + 1) * 512],
                                start=True, stop=True),
                                r=[onesf_b, rrow_b], w=[rbp_b])
                        for oc in range(8):
                            par = oc % 2
                            wgh, wgb = wgc[par]
                            p.dma("ge", wgh[:], wg[oc], w=[wgb])
                            p.flush("ge")
                            for s4 in range(4):
                                gph, gpb = gpp[s4 % 2]
                                for dc in range(8):
                                    p.op("te", lambda t, dc=dc, s4=s4, wgh=wgh, gph=gph: t.matmul(
                                        gph[:], wgh[:, dc, :],
                                        X3T[:, dc, s4 * 512:(s4 + 1) * 512],
                                        start=(dc == 0), stop=(dc == 7)),
                                        r=[wgb, X3T_b], w=[gpb])
                                p.op("se", lambda s, gph=gph: s.activation(
                                    gss[:], gph[:], AF.Sigmoid), r=[gpb], w=[gss_b])
                                p.op("ve", lambda v, gph=gph: v.tensor_tensor(
                                    gsl[:], gph[:], gss[:], ALU.mult), r=[gpb, gss_b], w=[gsl_b])
                                sl = slice(s4 * 512, (s4 + 1) * 512)
                                p.op("ve", lambda v, oc=oc, s4=s4, sl=sl: v.scalar_tensor_tensor(
                                    ons[:], ost[:, oc, sl], gnws[:, oc:oc + 1], rbp[:, s4, :],
                                    ALU.mult, ALU.mult),
                                    r=[ost_b, gnws_b, rbp_b], w=[ons_b])
                                p.op("ve", lambda v, oc=oc, sl=sl: v.tensor_tensor(
                                    ost[:, oc, sl], ons[:], gsl[:], ALU.mult),
                                    r=[ons_b, gsl_b], w=[ost_b])
                    p.barrier()
                    with ExitStack() as sN3:
                        wop = [_ps(sN3, nc, "n_wo%d" % i, [128, 512], F32) for i in range(2)]
                        trh, trh_b = _ps(sN3, nc, "n_trh", [128, 4, 128], BF16)
                        owh, owh_b = _sb(sN3, nc, "n_ow", [128, 512], BF16)
                        woc = [_sb(sN3, nc, "n_wc%d" % i, [128, 8, 128], BF16) for i in range(2)]
                        for ooc in range(8):
                            par = ooc % 2
                            wch, wcb = woc[par]
                            p.dma("ge", wch[:], wo[ooc], w=[wcb])
                            p.flush("ge")
                            for tt in range(4):
                                oph, opb = wop[tt % 2]
                                for dc in range(8):
                                    p.op("te", lambda t, dc=dc, tt=tt, wch=wch, oph=oph: t.matmul(
                                        oph[:], wch[:, dc, :],
                                        ost[:, dc, tt * 512:(tt + 1) * 512],
                                        start=(dc == 0), stop=(dc == 7)),
                                        r=[wcb, ost_b], w=[opb])
                                p.op("ve", lambda v, oph=oph: v.tensor_copy(owh[:], oph[:]),
                                     r=[opb], w=[owh_b])
                                for k in range(4):
                                    p.op("te", lambda t, k=k: t.transpose(
                                        trh[:, k, :], owh[:, k * 128:(k + 1) * 128], identb[:]),
                                        r=[owh_b, identb_b], w=[trh_b])
                                p.op("ve", lambda v, tt=tt, ooc=ooc: v.tensor_tensor(
                                    res[:, tt * 4:(tt + 1) * 4, ooc * 128:(ooc + 1) * 128],
                                    res[:, tt * 4:(tt + 1) * 4, ooc * 128:(ooc + 1) * 128],
                                    trh[:], ALU.add),
                                    r=[trh_b], w=[res_b])
            p.barrier()
            if "res3" in dumps:
                p.dma("qs", dumps["res3"].rearrange("(j p) d -> p j d", p=128), res[:],
                      r=[res_b])
                p.flush("qs")
                p.barrier()

            emit_ffn("f2")

            for j in range(NB):
                p.dma("qs", y_out[j * 128:(j + 1) * 128, :], res[:, j, :], r=[res_b])
            p.flush("qs")
        p.barrier()
    return nc


def kernel(**inputs):
    x = np.ascontiguousarray(np.asarray(inputs["L0_attn_a_x"], dtype=np.float32))
    W = np.asarray(inputs["L0_attn_a_W"], dtype=np.float32)
    Wi = np.ascontiguousarray(np.asarray(inputs["L0_attn_b_Wi"], dtype=np.float32))
    Wf = np.ascontiguousarray(np.asarray(inputs["L0_attn_b_Wf"], dtype=np.float32))
    Wg = np.ascontiguousarray(np.asarray(inputs["L0_attn_b_Wg"], dtype=np.float32))
    Wo = np.ascontiguousarray(np.asarray(inputs["L0_attn_b_Wo"], dtype=np.float32))
    ci = np.asarray(inputs["L0_attn_b_conv_i"], dtype=np.float32)
    cf = np.asarray(inputs["L0_attn_b_conv_f"], dtype=np.float32)
    gn = np.asarray(inputs["L0_attn_b_g_norm_weight"], dtype=np.float32)
    gw = np.asarray(inputs["L0_ffn_gate_w"], dtype=np.float32)
    eb = np.asarray(inputs["L0_ffn_expert_bias"], dtype=np.float32)
    w1 = np.ascontiguousarray(np.asarray(inputs["L0_ffn_w1"], dtype=np.float32))
    w2 = np.ascontiguousarray(np.asarray(inputs["L0_ffn_w2"], dtype=np.float32))
    w3 = np.ascontiguousarray(np.asarray(inputs["L0_ffn_w3"], dtype=np.float32))

    w1r = np.ascontiguousarray(w1.reshape(8, 8, 128, 22, 128).transpose(0, 3, 2, 1, 4)
                               .reshape(8, 22, 128, 1024))
    w3r = np.ascontiguousarray(w3.reshape(8, 8, 128, 22, 128).transpose(0, 3, 2, 1, 4)
                               .reshape(8, 22, 128, 1024))
    w2r = np.ascontiguousarray(w2.reshape(8, 2, 11, 128, 1024).transpose(0, 1, 3, 2, 4))
    def colmajor(m):
        return np.ascontiguousarray(m.reshape(8, 128, 8, 128).transpose(2, 1, 0, 3))
    wir, wfr, wgr_, wor = colmajor(Wi), colmajor(Wf), colmajor(Wg), colmajor(Wo)
    ci_l = np.ascontiguousarray(ci.reshape(8, 128, 4).transpose(1, 0, 2))
    cf_l = np.ascontiguousarray(cf.reshape(8, 128, 4).transpose(1, 0, 2))
    gn_l = np.ascontiguousarray(gn.reshape(8, 128).T)
    gwt_l = np.ascontiguousarray(gw.T.reshape(8, 128, 8).transpose(1, 0, 2))
    eb_l = np.ascontiguousarray(eb.reshape(1, 8))

    nc = build_nc()
    in_maps = []
    for c in range(8):
        b, hf = c // 2, c % 2
        wq = W[hf * 512:(hf + 1) * 512]
        wk = W[1024 + hf * 512:1024 + (hf + 1) * 512]
        wv = W[2048 + hf * 512:2048 + (hf + 1) * 512]
        wgr = W[3072 + hf * 8:3072 + (hf + 1) * 8]
        wqkvT = np.ascontiguousarray(np.concatenate([wq, wk, wv, wgr], axis=0).T)
        im = {
            "x_own": np.ascontiguousarray(x[b, hf * TOK:(hf + 1) * TOK]),
            "x_full": np.ascontiguousarray(x[b]),
            "w_qkvT": wqkvT,
            "wi": wir, "wf": wfr, "wg": wgr_, "wo": wor,
            "conv_i": ci_l, "conv_f": cf_l, "gnw": gn_l, "gwt": gwt_l, "ebias": eb_l,
            "w1": w1r, "w2": w2r, "w3": w3r,
            "mA": np.full((128, 1), 1.0 - hf, np.float32),
            "mB": np.full((128, 1), float(hf), np.float32),
        }
        in_maps.append(im)
    res = run_bass_kernel_spmd(nc, in_maps, core_ids=list(range(8)))
    out = np.zeros((4, T, D), np.float32)
    for c in range(8):
        b, hf = c // 2, c % 2
        out[b, hf * TOK:(hf + 1) * TOK] = res.results[c]["y_out"]
    return out


# version 10
# speedup vs baseline: 1.3443x; 1.3443x over previous
import numpy as np
from contextlib import ExitStack

import concourse.bass as bass
import concourse.mybir as mybir
from concourse.bass_utils import run_bass_kernel_spmd

import bass_rust as _br
F32 = mybir.dt.float32
BF16 = mybir.dt.bfloat16
I32 = mybir.dt.int32
IOA = _br.IndirectOffsetOnAxis
AF = mybir.ActivationFunctionType
ALU = mybir.AluOpType
AX = mybir.AxisListType

D = 1024
T = 4096
TOK = 2048
NB = 16
NBF = 32
NHL = 8
HS = 64
HID = 2816
HC = 22
NE = 8
LOGSCALE = float(np.log(1.0 / 8.0))
EPS_ATT = 1e-7
LN_EPS = 1e-5
RMS_EPS = 1e-5
NEGBIG = -1.0e9
BIG = 1000.0
GROUPS2 = [[0, 1], [2, 3], [4, 5], [6, 7]]

_PENDQ = {"qs": ("PEND_qs", 0), "ge": ("PEND_ge", 0)}


class Buf:
    __slots__ = ("w", "r", "name")

    def __init__(self, name=""):
        self.w = None
        self.r = []
        self.name = name


class Prog:
    def __init__(self, nc, es):
        self.nc = nc
        self.E = {"te": nc.tensor, "ve": nc.vector, "se": nc.scalar,
                  "ge": nc.gpsimd, "qs": nc.sync}
        self.sem = {}
        for name in ("te", "ve", "se", "ge", "qsd", "ged"):
            self.sem[name] = es.enter_context(nc.semaphore("S_" + name))
        self.cnt = {"te": 0, "ve": 0, "se": 0, "ge": 0}
        self.dcnt = {"qs": 0, "ge": 0}
        self.pend = {"qs": [], "ge": []}
        self.waited = {k: {} for k in self.E}

    def _wait(self, ekey, toks):
        w = self.waited[ekey]
        need = {}
        for t in toks:
            if t is None:
                continue
            s, v = t
            if s.startswith("PEND"):
                if s == "PEND_" + ekey:
                    continue
                raise RuntimeError("buffer consumed before dma flush (%s)" % s)
            if s == "te" and ekey == "te":
                continue
            if v > w.get(s, 0) and v > need.get(s, 0):
                need[s] = v
        for s, v in sorted(need.items()):
            self.E[ekey].wait_ge(self.sem[s], v)
            w[s] = v

    def _collect(self, r, w):
        toks = []
        for b in r:
            toks.append(b.w)
        for b in w:
            toks.append(b.w)
            toks.extend(b.r)
        return toks

    def op(self, ekey, fn, r=(), w=()):
        self._wait(ekey, self._collect(r, w))
        inst = fn(self.E[ekey])
        self.cnt[ekey] += 1
        inst.then_inc(self.sem[ekey], 1)
        t = (ekey, self.cnt[ekey])
        for b in r:
            b.r.append(t)
        for b in w:
            b.w = t
            b.r = []
        return t

    def dma(self, q, out_ap, in_ap, r=(), w=(), indirect=None, indirect_in=None, **kw):
        self._wait(q, self._collect(r, w))
        if indirect is not None:
            idx, ax = indirect
            inst = self.E[q].indirect_dma_start(out_ap, IOA(idx[:], ax), in_ap, None,
                                                oob_is_err=False)
        elif indirect_in is not None:
            idx, ax = indirect_in
            inst = self.E[q].indirect_dma_start(out_ap, None, in_ap, IOA(idx[:], ax),
                                                oob_is_err=False)
        else:
            inst = self.E[q].dma_start(out=out_ap, in_=in_ap, **kw)
        dsem = "qsd" if q == "qs" else "ged"
        inst.then_inc(self.sem[dsem], 16)
        self.dcnt[q] += 16
        pt = _PENDQ[q]
        for b in w:
            b.w = pt
            b.r = []
        for b in r:
            b.r.append(pt)
        self.pend[q].append((r, w))

    def flush(self, q):
        if not self.pend[q]:
            return
        dsem = "qsd" if q == "qs" else "ged"
        v = self.dcnt[q]
        self.E[q].wait_ge(self.sem[dsem], v)
        self.waited[q][dsem] = v
        t = (dsem, v)
        pt = _PENDQ[q]
        for r, w in self.pend[q]:
            for b in w:
                if b.w == pt:
                    b.w = t
            for b in r:
                b.r = [t if x == pt else x for x in b.r]
        self.pend[q] = []

    def cc(self, kind, aluop, in_ap, out_ap, r=(), w=()):
        return self.op("ge", lambda g: g.collective_compute(
            kind, aluop, replica_groups=GROUPS2,
            ins=[in_ap], outs=[out_ap]), r=r, w=w)

    def barrier(self):
        self.flush("qs")
        self.flush("ge")
        toks = [(e, self.cnt[e]) for e in self.cnt]
        toks += [("qsd", self.dcnt["qs"]), ("ged", self.dcnt["ge"])]
        for ekey in self.E:
            self._wait(ekey, toks)


def _sb(es, nc, name, shape, dtype):
    h = es.enter_context(nc.sbuf_tensor(name, shape, dtype))
    return h, Buf(name)


def _ps(es, nc, name, shape, dtype):
    esz = mybir.dt.size(dtype)
    n = 1
    for d in shape[1:]:
        n *= d
    per = n * esz
    pad = (-per) % 2048
    if pad == 0:
        h = es.enter_context(nc.psum_tensor(name, shape, dtype))
        return h, Buf(name)
    flat = n + pad // esz
    h = es.enter_context(nc.psum_tensor(name, [shape[0], flat], dtype))
    ap = h[:, 0:n]
    if len(shape) == 3:
        ap = ap.rearrange("p (a b) -> p a b", a=shape[1])
    return ap, Buf(name)


def emit_ln(p, src_ap, src_bufs, out_ap, out_buf, sq, sqb, st, stb):
    p.op("ve", lambda v: v.reduce_sum(st[:, 0:1], src_ap, axis=AX.X), r=src_bufs, w=[stb])
    p.op("se", lambda s: s.activation(sq[:], src_ap, AF.Square), r=src_bufs, w=[sqb])
    p.op("ve", lambda v: v.reduce_sum(st[:, 1:2], sq[:], axis=AX.X), r=[sqb], w=[stb])
    p.op("ve", lambda v: v.tensor_scalar_mul(st[:, 2:3], st[:, 0:1], 1.0 / D), r=[stb], w=[stb])
    p.op("ve", lambda v: v.tensor_tensor(st[:, 3:4], st[:, 2:3], st[:, 2:3], ALU.mult), r=[stb], w=[stb])
    p.op("ve", lambda v: v.tensor_scalar_mul(st[:, 4:5], st[:, 1:2], 1.0 / D), r=[stb], w=[stb])
    p.op("ve", lambda v: v.tensor_tensor(st[:, 5:6], st[:, 4:5], st[:, 3:4], ALU.subtract), r=[stb], w=[stb])
    p.op("se", lambda s: s.activation(st[:, 6:7], st[:, 5:6], AF.Ln, bias=LN_EPS), r=[stb], w=[stb])
    p.op("se", lambda s: s.activation(st[:, 7:8], st[:, 6:7], AF.Exp, scale=-0.5), r=[stb], w=[stb])
    p.op("ve", lambda v: v.tensor_scalar(out_ap, src_ap, st[:, 2:3], st[:, 7:8],
                                         ALU.subtract, ALU.mult),
         r=list(src_bufs) + [stb], w=[out_buf])


def build_nc(dump=()):
    nc = bass.Bass("TRN2", target_bir_lowering=False)

    x_own = nc.dram_tensor("x_own", [TOK, D], F32, kind="ExternalInput").ap()
    x_full = nc.dram_tensor("x_full", [T, D], F32, kind="ExternalInput").ap()
    w_qkvT = nc.dram_tensor("w_qkvT", [D, 1544], F32, kind="ExternalInput").ap()
    wi = nc.dram_tensor("wi", [8, 128, 8, 128], F32, kind="ExternalInput").ap()
    wf = nc.dram_tensor("wf", [8, 128, 8, 128], F32, kind="ExternalInput").ap()
    wg = nc.dram_tensor("wg", [8, 128, 8, 128], F32, kind="ExternalInput").ap()
    wo = nc.dram_tensor("wo", [8, 128, 8, 128], F32, kind="ExternalInput").ap()
    conv_i = nc.dram_tensor("conv_i", [128, 8, 4], F32, kind="ExternalInput").ap()
    conv_f = nc.dram_tensor("conv_f", [128, 8, 4], F32, kind="ExternalInput").ap()
    gnw = nc.dram_tensor("gnw", [128, 8], F32, kind="ExternalInput").ap()
    gwt = nc.dram_tensor("gwt", [128, 8, 8], F32, kind="ExternalInput").ap()
    ebias = nc.dram_tensor("ebias", [1, 8], F32, kind="ExternalInput").ap()
    w1 = nc.dram_tensor("w1", [NE, HC, 128, D], F32, kind="ExternalInput").ap()
    w2 = nc.dram_tensor("w2", [NE, 2, 128, 11, D], F32, kind="ExternalInput").ap()
    w3 = nc.dram_tensor("w3", [NE, HC, 128, D], F32, kind="ExternalInput").ap()
    mA = nc.dram_tensor("mA", [128, 1], F32, kind="ExternalInput").ap()
    mB = nc.dram_tensor("mB", [128, 1], F32, kind="ExternalInput").ap()
    y_out = nc.dram_tensor("y_out", [TOK, D], F32, kind="ExternalOutput").ap()
    dumps = {}
    for nm in dump:
        shp = {"res1": [TOK, D], "res2": [TOK, D], "res3": [TOK, D],
               "sco": [128, NB, 8], "lgg": [8, T], "yag": [8192, 512]}[nm]
        dt = BF16 if nm == "yag" else F32
        dumps[nm] = nc.dram_tensor("d_" + nm, shp, dt, kind="ExternalOutput").ap()

    y_pub = nc.dram_tensor("y_pub", [T, 512], BF16)
    y_ag = nc.dram_tensor("y_ag", [2 * T, 512], BF16)
    b_pub = nc.dram_tensor("b_pub", [128, 24], BF16)
    b_ag = nc.dram_tensor("b_ag", [256, 24], BF16)
    c_pub = nc.dram_tensor("c_pub", [128, 8], F32)
    c_ag = nc.dram_tensor("c_ag", [256, 8], F32)
    w13b = nc.dram_tensor("w13b", [NE, 2, 128, 2, 11, D], BF16)
    w2b = nc.dram_tensor("w2b", [NE, 2, 128, 11, D], BF16)
    xl_d = nc.dram_tensor("xl_d", [TOK, D], BF16)
    xsl_d = nc.dram_tensor("xsl_d", [5120, D], BF16)
    osl_d = nc.dram_tensor("osl_d", [5120, D], BF16)
    yb = {n: Buf(n) for n in ("y_pub", "y_ag", "b_pub", "b_ag", "c_pub", "c_ag",
                              "w13b", "w2b", "xl_d", "xsl_d", "osl_d")}

    es = ExitStack()
    with es:
        p = Prog(nc, es)

        identf, identf_b = _sb(es, nc, "identf", [128, 128], F32)
        identb, identb_b = _sb(es, nc, "identb", [128, 128], BF16)
        onesf, onesf_b = _sb(es, nc, "onesf", [1, 128], F32)
        onescb, onescb_b = _sb(es, nc, "onescb", [128, 1], BF16)
        iota8, iota8_b = _sb(es, nc, "iota8", [128, 8], F32)
        iotaB8, iotaB8_b = _sb(es, nc, "iotaB8", [128, 8], F32)
        ebc, ebc_b = _sb(es, nc, "ebc", [128, 8], F32)
        ebrow, ebrow_b = _sb(es, nc, "ebrow", [1, 8], F32)
        cwi, cwi_b = _sb(es, nc, "cwi", [128, 8, 4], F32)
        cwf, cwf_b = _sb(es, nc, "cwf", [128, 8, 4], F32)
        gnws, gnws_b = _sb(es, nc, "gnws", [128, 8], F32)
        gwts, gwts_b = _sb(es, nc, "gwts", [128, 8, 8], BF16)
        mAs, mAs_b = _sb(es, nc, "mAs", [128, 1], F32)
        mBs, mBs_b = _sb(es, nc, "mBs", [128, 1], F32)
        iosc, iosc_b = _sb(es, nc, "iosc", [128, 128], F32)
        trix, trix_b = _sb(es, nc, "trix", [128, 128], BF16)
        iotac, iotac_b = _sb(es, nc, "iotac", [128, 1], F32)
        iota40, iota40_b = _sb(es, nc, "iota40", [8, 40], F32)
        iotok, iotok_b = _sb(es, nc, "iotok", [128, 16], F32)
        os18, os18_b = _sb(es, nc, "os18", [1, 8], F32)
        wst = [_sb(es, nc, "wst%d" % i, [128, D], BF16) for i in range(4)]
        cst, cst_b = _sb(es, nc, "cst", [128, 4], F32)
        for i, val in enumerate((LN_EPS, EPS_ATT, 1.0, RMS_EPS)):
            p.op("ve", lambda v, i=i, val=val: v.memset(cst[:, i:i + 1], val), w=[cst_b])
            nc.const_aps.register(F32, val, cst[:, i:i + 1])

        p.op("ge", lambda g: g.iota(iosc[:], pattern=[[1, 128]], base=0,
                                    channel_multiplier=-1,
                                    allow_small_or_imprecise_dtypes=True), w=[iosc_b])
        p.op("ve", lambda v: v.tensor_single_scalar(identf[:], iosc[:], 0.0, ALU.is_equal),
             r=[iosc_b], w=[identf_b])
        p.op("ve", lambda v: v.tensor_copy(identb[:], identf[:]), r=[identf_b], w=[identb_b])
        p.op("ve", lambda v: v.memset(onesf[:], 1.0), w=[onesf_b])
        p.op("ve", lambda v: v.memset(onescb[:], 1.0), w=[onescb_b])
        p.op("ge", lambda g: g.iota(iota8[:], pattern=[[1, 8]], base=0,
                                    channel_multiplier=0,
                                    allow_small_or_imprecise_dtypes=True), w=[iota8_b])
        p.op("ve", lambda v: v.tensor_single_scalar(trix[:], iosc[:], 0.0, ALU.is_gt),
             r=[iosc_b], w=[trix_b])
        p.op("ge", lambda g: g.iota(iotac[:], pattern=[[1, 1]], base=0,
                                    channel_multiplier=1,
                                    allow_small_or_imprecise_dtypes=True), w=[iotac_b])
        p.op("ge", lambda g: g.iota(iota40[:], pattern=[[128, 40]], base=0,
                                    channel_multiplier=0,
                                    allow_small_or_imprecise_dtypes=True), w=[iota40_b])
        p.op("ge", lambda g: g.iota(iotok[:], pattern=[[128, 16]], base=0,
                                    channel_multiplier=1,
                                    allow_small_or_imprecise_dtypes=True), w=[iotok_b])
        p.op("ve", lambda v: v.memset(os18[:], 1.0), w=[os18_b])
        p.op("ve", lambda v: v.tensor_single_scalar(iotaB8[:], iota8[:], -BIG, ALU.add),
             r=[iota8_b], w=[iotaB8_b])
        p.dma("qs", cwi[:], conv_i, w=[cwi_b])
        p.dma("qs", cwf[:], conv_f, w=[cwf_b])
        p.dma("qs", gnws[:], gnw, w=[gnws_b])
        p.dma("qs", mAs[:], mA, w=[mAs_b])
        p.dma("qs", mBs[:], mB, w=[mBs_b])
        p.dma("qs", ebrow[:], ebias, w=[ebrow_b])
        p.dma("ge", gwts[:], gwt, w=[gwts_b])
        p.flush("qs")
        p.flush("ge")

        with nc.psum_tensor("ebp", [128, 512], F32) as ebp:
            ebp_b = Buf("ebp")
            p.op("te", lambda t: t.matmul(ebp[:, 0:8], onesf[:], ebrow[:], start=True, stop=True),
                 r=[onesf_b, ebrow_b], w=[ebp_b])
            p.op("ve", lambda v: v.tensor_copy(ebc[:], ebp[:, 0:8]), r=[ebp_b], w=[ebc_b])
        p.barrier()

        wc_steps = []

        def _wc_pair(srcs):
            def go():
                for i, (sap, dap) in enumerate(srcs):
                    sbh, sbb = wst[i]
                    p.dma("ge", sbh[:], sap, w=[sbb])
                p.flush("ge")
                for i, (sap, dap) in enumerate(srcs):
                    sbh, sbb = wst[i]
                    p.dma("ge", dap, sbh[:], r=[sbb], w=[yb["w13b"], yb["w2b"]])
                p.flush("ge")
            return go

        _grp = []
        for e in range(NE):
            for hc in range(HC):
                half, hl = divmod(hc, 11)
                _grp.append((w1[e, hc], w13b.ap()[e, half, :, 0, hl, :]))
                _grp.append((w3[e, hc], w13b.ap()[e, half, :, 1, hl, :]))
                _grp.append((w2[e, half, :, hl, :], w2b.ap()[e, half, :, hl, :]))
                if len(_grp) >= 4:
                    wc_steps.append(_wc_pair(_grp[:4]))
                    _grp = _grp[4:]
        if _grp:
            wc_steps.append(_wc_pair(_grp))
        wc_pos = [0]

        def wcast_emit(k):
            n = min(k, len(wc_steps) - wc_pos[0])
            for _ in range(n):
                wc_steps[wc_pos[0]]()
                wc_pos[0] += 1

        with ExitStack() as s1:
            s1.enter_context(nc.named_scope("p1_attn"))
            XT, XT_b = _sb(s1, nc, "XT", [128, 8, T], BF16)
            wall, wall_b = _sb(s1, nc, "wall", [128, 8, 1544], BF16)
            lgG8, lgG8_b = _sb(s1, nc, "lgG8", [8, T], F32)
            lgGn, lgGn_b = _sb(s1, nc, "lgGn", [128, 32, 8], F32)

            p.dma("ge", wall[:], w_qkvT.rearrange("(dc p) o -> p dc o", p=128), w=[wall_b])
            p.flush("ge")
            wcast_emit(2)

            with ExitStack() as sA:
                xs = [_sb(sA, nc, "p1xs%d" % i, [128, D], F32) for i in range(2)]
                xln = [_sb(sA, nc, "p1xl%d" % i, [128, D], BF16) for i in range(2)]
                sq, sq_b = _sb(sA, nc, "p1sq", [128, D], F32)
                st, st_b = _sb(sA, nc, "p1st", [128, 8], F32)
                g8, g8_b = _sb(sA, nc, "p1g8", [8, T], F32)
                gs1, gs1_b = _sb(sA, nc, "p1gs1", [8, 512], F32)
                gs2, gs2_b = _sb(sA, nc, "p1gs2", [8, 512], F32)
                psb, psb_b = _ps(sA, nc, "p1psb", [128, 128], BF16)
                gp, gp_b = _ps(sA, nc, "p1gp", [8, 512], F32)
                tpp, tpp_b = _ps(sA, nc, "p1tpp", [128, 8], F32)

                for blk in range(NBF):
                    pb = blk % 2
                    xsh, xsb = xs[pb]
                    xlh, xlb = xln[pb]
                    p.dma("qs", xsh[:], x_full[blk * 128:(blk + 1) * 128, :], w=[xsb])
                    p.flush("qs")
                    emit_ln(p, xsh[:], [xsb], xlh[:], xlb, sq, sq_b, st, st_b)
                    for dc in range(8):
                        p.op("te", lambda t, dc=dc, xlh=xlh: t.transpose(
                            psb[:], xlh[:, dc * 128:(dc + 1) * 128], identb[:]),
                            r=[xlb, identb_b], w=[psb_b])
                        p.op("se", lambda s, dc=dc, blk=blk: s.copy(
                            XT[:, dc, blk * 128:(blk + 1) * 128], psb[:]),
                            r=[psb_b], w=[XT_b])
                    if blk % 4 == 3:
                        wcast_emit(1)

                for s8 in range(8):
                    for dc in range(8):
                        p.op("te", lambda t, dc=dc, s8=s8: t.matmul(
                            gp[:], wall[:, dc, 1536:1544], XT[:, dc, s8 * 512:(s8 + 1) * 512],
                            start=(dc == 0), stop=(dc == 7)),
                            r=[wall_b, XT_b], w=[gp_b])
                    p.op("se", lambda s, s8=s8: s.copy(g8[:, s8 * 512:(s8 + 1) * 512], gp[:]),
                         r=[gp_b], w=[g8_b])
                for s8 in range(8):
                    sl = slice(s8 * 512, (s8 + 1) * 512)
                    p.op("se", lambda s, sl=sl: s.activation(
                        gs1[:], g8[:, sl], AF.Abs), r=[g8_b], w=[gs1_b])
                    p.op("se", lambda s: s.activation(gs1[:], gs1[:], AF.Exp, scale=-1.0),
                         r=[gs1_b], w=[gs1_b])
                    p.op("se", lambda s: s.activation(gs1[:], gs1[:], AF.Ln, bias=1.0),
                         r=[gs1_b], w=[gs1_b])
                    p.op("ve", lambda v, sl=sl: v.tensor_single_scalar(
                        gs2[:], g8[:, sl], 0.0, ALU.min), r=[g8_b], w=[gs2_b])
                    p.op("ve", lambda v, sl=sl: v.tensor_tensor(
                        g8[:, sl], gs2[:], gs1[:], ALU.subtract),
                        r=[gs2_b, gs1_b], w=[g8_b])
                p.op("ve", lambda v: v.memset(lgG8[:], 1.0), w=[lgG8_b])
                p.op("ve", lambda v: v.tensor_tensor_scan(
                    lgG8[:], lgG8[:], g8[:], 0.0, ALU.mult, ALU.add),
                    r=[g8_b], w=[lgG8_b])
                if "lgg" in dumps:
                    p.dma("qs", dumps["lgg"], lgG8[:], r=[lgG8_b])
                    p.flush("qs")
                for j in range(32):
                    p.op("te", lambda t, j=j: t.transpose(
                        tpp[:], lgG8[:, j * 128:(j + 1) * 128], identf[0:8, 0:8]),
                        r=[lgG8_b, identf_b], w=[tpp_b])
                    p.op("se", lambda s, j=j: s.activation(
                        lgGn[:, j, :], tpp[:], AF.Copy, scale=-1.0),
                        r=[tpp_b], w=[lgGn_b])
                wcast_emit(2)
            p.barrier()

            with ExitStack() as sB:
                KTt = es_kt = sB.enter_context(nc.sbuf_tensor("KTt", [128, T], BF16))
                QTt = sB.enter_context(nc.sbuf_tensor("QTt", [128, T], BF16))
                KT = [(KTt[0:64, :], Buf("KT0")), (KTt[64:128, :], Buf("KT1"))]
                QT = [(QTt[0:64, :], Buf("QT0")), (QTt[64:128, :], Buf("QT1"))]
                Vsb = [_sb(sB, nc, "Vsb%d" % i, [128, 32, 65], BF16) for i in range(2)]
                VTs, VTs_b = _sb(sB, nc, "VTs", [64, T], BF16)
                y2, y2_b = _sb(sB, nc, "y2", [128, 32, 128], BF16)
                As2 = [_sb(sB, nc, "As%d" % i, [128, 2, 512], F32) for i in range(2)]
                Ls2 = [_sb(sB, nc, "Ls%d" % i, [128, 2, 512], F32) for i in range(2)]
                Ps2 = [_sb(sB, nc, "Ps%d" % i, [128, 2, 512], BF16) for i in range(2)]
                Rsb2 = [_sb(sB, nc, "Rsb%d" % i, [128, 512], F32) for i in range(2)]
                rr, rr_b = _sb(sB, nc, "rr", [1, 512], F32)
                rr02 = [_sb(sB, nc, "rr0%d" % i, [1, 512], F32) for i in range(2)]
                msk4, msk4_b = _sb(sB, nc, "msk4", [128, 4, 512], BF16)
                mtmp, mtmp_b = _sb(sB, nc, "mtmp", [128, 512], F32)
                Oe, Oe_b = _sb(sB, nc, "Oe", [65, 512], F32)
                rcp, rcp_b = _sb(sB, nc, "rcp", [128, 1], F32)
                S2 = [_ps(sB, nc, "S_ps%d" % i, [128, 2, 512], F32) for i in range(2)]
                O_ps, O_ps_b = _ps(sB, nc, "O_ps", [65, 512], F32)
                mp_ps, mp_ps_b = _ps(sB, nc, "mp_ps", [128, 512], F32)
                ftr_ps, ftr_ps_b = _ps(sB, nc, "ftr_ps", [128, 65], F32)
                vtr_ps, vtr_ps_b = _ps(sB, nc, "vtr_ps", [128, 64], BF16)

                for k in range(4):
                    p.op("ge", lambda g, k=k: g.iota(
                        mtmp[:], pattern=[[1, 512]], base=-128 * k, channel_multiplier=-1,
                        allow_small_or_imprecise_dtypes=True), w=[mtmp_b])
                    p.op("ve", lambda v, k=k: v.tensor_single_scalar(
                        msk4[:, k, :], mtmp[:], 0.0, ALU.is_lt), r=[mtmp_b], w=[msk4_b])
                p.op("ve", lambda v: v.tensor_scalar_mul(msk4[:], msk4[:], NEGBIG),
                     r=[msk4_b], w=[msk4_b])
                for i in range(2):
                    p.op("ve", lambda v, i=i: v.memset(Vsb[i][0][:, :, 64:65], 1.0),
                         w=[Vsb[i][1]])

                def setup_pieces(h):
                    hb = h % 2
                    kth, ktb = KT[hb]
                    qth, qtb = QT[hb]
                    vh, vb2 = Vsb[hb]
                    pieces = []
                    for s8 in range(8):
                        def go(s8=s8, kth=kth, ktb=ktb, qth=qth, qtb=qtb, vh=vh, vb2=vb2, h=h):
                            for (dst, dstb, off) in ((kth, ktb, 512), (qth, qtb, 0),
                                                     (VTs, VTs_b, 1024)):
                                for dc in range(8):
                                    p.op("te", lambda t, dc=dc, s8=s8, off=off, h=h: t.matmul(
                                        mp_ps[0:64, :],
                                        wall[:, dc, off + h * 64: off + (h + 1) * 64],
                                        XT[:, dc, s8 * 512:(s8 + 1) * 512],
                                        start=(dc == 0), stop=(dc == 7)),
                                        r=[wall_b, XT_b], w=[mp_ps_b])
                                p.op("ve", lambda v, s8=s8, dst=dst: v.tensor_copy(
                                    dst[:, s8 * 512:(s8 + 1) * 512], mp_ps[0:64, :]),
                                    r=[mp_ps_b], w=[dstb])
                            for j in range(4):
                                jj = s8 * 4 + j
                                p.op("te", lambda t, jj=jj: t.transpose(
                                    vtr_ps[:], VTs[:, jj * 128:(jj + 1) * 128],
                                    identb[0:64, 0:64]),
                                    r=[VTs_b, identb_b], w=[vtr_ps_b])
                                p.op("se", lambda s, jj=jj, vh=vh: s.copy(
                                    vh[:, jj, 0:64], vtr_ps[:]),
                                    r=[vtr_ps_b], w=[vb2])
                        pieces.append(go)
                    return pieces

                def attend(h, next_pieces):
                    hb = h % 2
                    kth, ktb = KT[hb]
                    qth, qtb = QT[hb]
                    vh, vb2 = Vsb[hb]
                    pend_fin = [None]

                    def emit_qk(s8, g0, gn):
                        gp = (g0 // 2) % 2
                        Sp, Sp_b = S2[gp]
                        for k in range(gn):
                            c = g0 + k
                            p.op("te", lambda t, k=k, c=c, s8=s8, kth=kth, qth=qth, Sp=Sp: t.matmul(
                                Sp[:, k, :], kth[:, c * 128:(c + 1) * 128],
                                qth[:, s8 * 512:(s8 + 1) * 512], start=True, stop=True),
                                r=[ktb, qtb], w=[Sp_b])

                    def emit_absln(s8, g0, gn):
                        gp = (g0 // 2) % 2
                        Sp, Sp_b = S2[gp]
                        Ah, Ah_b = As2[gp]
                        Lh, Lh_b = Ls2[gp]
                        p.op("se", lambda s, gn=gn, Ah=Ah, Sp=Sp: s.activation(
                            Ah[:, 0:gn, :], Sp[:, 0:gn, :], AF.Abs),
                            r=[Sp_b], w=[Ah_b])
                        p.op("se", lambda s, gn=gn, Ah=Ah, Lh=Lh: s.activation(
                            Lh[:, 0:gn, :], Ah[:, 0:gn, :], AF.Ln, bias=EPS_ATT),
                            r=[Ah_b], w=[Lh_b])

                    def emit_elem_pv(s8, g0, gn, nch):
                        gp = (g0 // 2) % 2
                        Sp, Sp_b = S2[gp]
                        Ah, Ah_b = As2[gp]
                        Lh, Lh_b = Ls2[gp]
                        Ph, Ph_b = Ps2[gp]
                        Rsb, Rsb_b = Rsb2[s8 % 2]
                        for k in range(gn):
                            c = g0 + k
                            p.op("ve", lambda v, k=k, Ah=Ah, Lh=Lh, Rsb=Rsb: v.tensor_tensor(
                                Ah[:, k, :], Lh[:, k, :], Rsb[:], ALU.add),
                                r=[Lh_b, Rsb_b], w=[Ah_b])
                            if c >= 4 * s8:
                                p.op("ve", lambda v, k=k, c=c, s8=s8, Ah=Ah: v.tensor_tensor(
                                    Ah[:, k, :], Ah[:, k, :], msk4[:, c - 4 * s8, :],
                                    ALU.add),
                                    r=[msk4_b], w=[Ah_b])
                        for k in range(gn):
                            c = g0 + k
                            p.op("se", lambda s, k=k, c=c, h=h, Ah=Ah, Ph=Ph: s.activation(
                                Ph[:, k, :], Ah[:, k, :], AF.Exp, scale=2.0,
                                bias=lgGn[:, c, h:h + 1]),
                                r=[Ah_b, lgGn_b], w=[Ph_b])
                        for k in range(gn):
                            c = g0 + k
                            p.op("te", lambda t, k=k, c=c, vh=vh, Ph=Ph: t.matmul(
                                O_ps[:], vh[:, c, :], Ph[:, k, :],
                                start=(c == 0), stop=(c == nch - 1)),
                                r=[vb2, Ph_b], w=[O_ps_b])

                    def make_fin(s8, h):
                        def fin():
                            p.op("se", lambda s: s.copy(Oe[:], O_ps[:]), r=[O_ps_b], w=[Oe_b])
                            for j in range(4):
                                p.op("te", lambda t, j=j: t.transpose(
                                    ftr_ps[:], Oe[:, j * 128:(j + 1) * 128],
                                    identf[0:65, 0:65]),
                                    r=[Oe_b, identf_b], w=[ftr_ps_b])
                                p.op("ve", lambda v: v.reciprocal(rcp[:], ftr_ps[:, 64:65]),
                                     r=[ftr_ps_b], w=[rcp_b])
                                p.op("ve", lambda v, j=j, s8=s8, h=h: v.tensor_scalar_mul(
                                    y2[:, s8 * 4 + j, (h % 2) * 64:(h % 2) * 64 + 64],
                                    ftr_ps[:, 0:64], rcp[:]),
                                    r=[ftr_ps_b, rcp_b], w=[y2_b])
                        return fin

                    rh0, rb0 = rr02[0]
                    p.dma("qs", rh0[:], lgG8[h:h + 1, 0:512], r=[lgG8_b], w=[rb0])
                    for s8 in range(8):
                        rpar = s8 % 2
                        rr0h, rr0b = rr02[rpar]
                        Rsb, Rsb_b = Rsb2[rpar]
                        nch = 4 * (s8 + 1)
                        emit_qk(s8, 0, 2)
                        if pend_fin[0] is not None:
                            pend_fin[0]()
                            pend_fin[0] = None
                        p.flush("qs")
                        if s8 < 7:
                            nh_, nb_ = rr02[(s8 + 1) % 2]
                            p.dma("qs", nh_[:], lgG8[h:h + 1, (s8 + 1) * 512:(s8 + 2) * 512],
                                  r=[lgG8_b], w=[nb_])
                        p.op("ve", lambda v, rr0h=rr0h: v.tensor_scalar(
                            rr[:], rr0h[:], 0.5, LOGSCALE / 2.0, ALU.mult, ALU.add),
                            r=[rr0b], w=[rr_b])
                        p.op("te", lambda t: t.matmul(mp_ps[:], onesf[:], rr[:],
                                                      start=True, stop=True),
                             r=[onesf_b, rr_b], w=[mp_ps_b])
                        p.op("ve", lambda v, Rsb=Rsb: v.tensor_copy(Rsb[:], mp_ps[:]),
                             r=[mp_ps_b], w=[Rsb_b])
                        groups = list(range(0, nch, 2))
                        emit_absln(s8, 0, 2)
                        for gi, g0 in enumerate(groups):
                            gn = min(2, nch - g0)
                            if gi + 1 < len(groups):
                                g1 = groups[gi + 1]
                                gn1 = min(2, nch - g1)
                                emit_qk(s8, g1, gn1)
                                emit_absln(s8, g1, gn1)
                            elif s8 < 4 and next_pieces:
                                next_pieces.pop(0)()
                            emit_elem_pv(s8, g0, gn, nch)
                        if s8 >= 4 and next_pieces:
                            next_pieces.pop(0)()
                        pend_fin[0] = make_fin(s8, h)
                    pend_fin[0]()
                    while next_pieces:
                        next_pieces.pop(0)()
                    if h % 2 == 1:
                        h2 = h // 2
                        for q4 in range(4):
                            p.dma("qs", y_pub.ap().rearrange("(j p) c -> p j c", p=128)
                                  [:, q4 * 8:(q4 + 1) * 8, h2 * 128:(h2 + 1) * 128],
                                  y2[:, q4 * 8:(q4 + 1) * 8, :], r=[y2_b], w=[yb["y_pub"]])
                        p.flush("qs")

                for piece in setup_pieces(0):
                    piece()
                for h in range(NHL):
                    wcast_emit(17)
                    np_ = setup_pieces(h + 1) if h + 1 < NHL else []
                    attend(h, np_)
                wcast_emit(len(wc_steps))
                p.cc("AllGather", ALU.bypass, y_pub.ap().opt(), y_ag.ap().opt(),
                     r=[yb["y_pub"]], w=[yb["y_ag"]])
                if "yag" in dumps:
                    p.dma("qs", dumps["yag"], y_ag.ap(), r=[yb["y_ag"]])
                    p.flush("qs")
            p.barrier()

        with ExitStack() as sR:
            res, res_b = _sb(sR, nc, "res", [128, NB, D], F32)

            with ExitStack() as sRB:
                sRB.enter_context(nc.named_scope("res_build"))
                xs0, xs0_b = _sb(sRB, nc, "rbx", [128, D], F32)
                ya, ya_b = _sb(sRB, nc, "rbya", [128, 512], BF16)
                yB2, yB2_b = _sb(sRB, nc, "rbyb", [128, 512], BF16)
                for j in range(NB):
                    p.dma("qs", xs0[:], x_own[j * 128:(j + 1) * 128, :], w=[xs0_b])
                    p.flush("qs")
                    p.op("ve", lambda v, j=j: v.tensor_copy(res[:, j, :], xs0[:]),
                         r=[xs0_b], w=[res_b])
                    for slot in range(2):
                        p.dma("qs", ya[:], y_ag.ap()[slot * T + j * 128: slot * T + (j + 1) * 128, :],
                              r=[yb["y_ag"]], w=[ya_b])
                        p.dma("qs", yB2[:], y_ag.ap()[slot * T + TOK + j * 128: slot * T + TOK + (j + 1) * 128, :],
                              r=[yb["y_ag"]], w=[yB2_b])
                        p.flush("qs")
                        sl = slice(slot * 512, (slot + 1) * 512)
                        p.op("ve", lambda v, j=j, sl=sl: v.scalar_tensor_tensor(
                            res[:, j, sl], ya[:], mAs[:], res[:, j, sl], ALU.mult, ALU.add),
                            r=[ya_b, mAs_b], w=[res_b])
                        p.op("ve", lambda v, j=j, sl=sl: v.scalar_tensor_tensor(
                            res[:, j, sl], yB2[:], mBs[:], res[:, j, sl], ALU.mult, ALU.add),
                            r=[yB2_b, mBs_b], w=[res_b])
            if "res1" in dumps:
                p.dma("qs", dumps["res1"].rearrange("(j p) d -> p j d", p=128), res[:],
                      r=[res_b])
                p.flush("qs")
            p.barrier()

            def emit_ffn(tag, dump_sco=False):
                with ExitStack() as sF:
                    sF.enter_context(nc.named_scope(tag + "_ffn"))
                    sctm, sctm_b = _sb(sF, nc, tag + "sctm", [128, NB, 8], F32)
                    t8a, t8a_b = _sb(sF, nc, tag + "t8a", [128, 8], F32)
                    t8b, t8b_b = _sb(sF, nc, tag + "t8b", [128, 8], F32)
                    t8c, t8c_b = _sb(sF, nc, tag + "t8c", [128, 8], F32)
                    t1c, t1c_b = _sb(sF, nc, tag + "t1c", [128, 1], F32)
                    mkb, mkb_b = _sb(sF, nc, tag + "mkb", [128, NB, 8], BF16)
                    m1t, m1t_b = _sb(sF, nc, tag + "m1t", [128, NB, 8], F32)
                    m2t, m2t_b = _sb(sF, nc, tag + "m2t", [128, NB, 8], F32)
                    sr1, sr1_b = _sb(sF, nc, tag + "sr1", [128, NB], F32)
                    sr2, sr2_b = _sb(sF, nc, tag + "sr2", [128, NB], F32)
                    cntA, cntA_b = _sb(sF, nc, tag + "cntA", [1, 128], F32)
                    r8 = {}
                    for nm in ("cnt", "al", "incl", "starts", "run", "sbrow", "tmp8"):
                        r8[nm] = _sb(sF, nc, tag + nm, [1, 8], F32)
                    blk2e, blk2e_b = _sb(sF, nc, tag + "blk2e", [1, 40], F32)
                    widx = [_sb(sF, nc, tag + "widx%d" % i, [128, 40], I32) for i in range(2)]
                    slotf, slotf_b = _sb(sF, nc, tag + "slotf", [128, 8], F32)
                    sfa, sfa_b = _sb(sF, nc, tag + "sfa", [128, 8], F32)
                    sfb, sfb_b = _sb(sF, nc, tag + "sfb", [128, 8], F32)

                    inclT, inclT_b = _sb(sF, nc, tag + "inclT", [8, 1], F32)
                    cmpb, cmpb_b = _sb(sF, nc, tag + "cmpb", [8, 40], BF16)
                    with ExitStack() as sFP:
                        XTf, XTf_b = _sb(sFP, nc, tag + "XTf", [128, 8, TOK], BF16)
                        xlf, xlf_b = _sb(sFP, nc, tag + "xlf", [128, D], BF16)
                        sqf, sqf_b = _sb(sFP, nc, tag + "sqf", [128, D], F32)
                        stf, stf_b = _sb(sFP, nc, tag + "stf", [128, 8], F32)
                        lgt, lgt_b = _sb(sFP, nc, tag + "lgt", [8, 512], F32)
                        ltm, ltm_b = _sb(sFP, nc, tag + "ltm", [128, NB, 8], F32)
                        psbF, psbF_b = _ps(sFP, nc, tag + "psbF", [128, 128], BF16)
                        lgp, lgp_b = _ps(sFP, nc, tag + "lgp", [8, 512], F32)
                        tpf, tpf_b = _ps(sFP, nc, tag + "tpf", [128, 8], F32)

                        for j in range(NB):
                            emit_ln(p, res[:, j, :], [res_b], xlf[:], xlf_b,
                                    sqf, sqf_b, stf, stf_b)
                            p.dma("qs", xl_d.ap()[j * 128:(j + 1) * 128, :], xlf[:],
                                  r=[xlf_b], w=[yb["xl_d"]])
                            p.flush("qs")
                            for dc in range(8):
                                p.op("te", lambda t, dc=dc: t.transpose(
                                    psbF[:], xlf[:, dc * 128:(dc + 1) * 128], identb[:]),
                                    r=[xlf_b, identb_b], w=[psbF_b])
                                p.op("se", lambda s, dc=dc, j=j: s.copy(
                                    XTf[:, dc, j * 128:(j + 1) * 128], psbF[:]),
                                    r=[psbF_b], w=[XTf_b])
                        for s4 in range(4):
                            for dc in range(8):
                                p.op("te", lambda t, dc=dc, s4=s4: t.matmul(
                                    lgp[:], gwts[:, dc, :], XTf[:, dc, s4 * 512:(s4 + 1) * 512],
                                    start=(dc == 0), stop=(dc == 7)),
                                    r=[gwts_b, XTf_b], w=[lgp_b])
                            p.op("se", lambda s: s.copy(lgt[:], lgp[:]), r=[lgp_b], w=[lgt_b])
                            for jj in range(4):
                                j = s4 * 4 + jj
                                p.op("te", lambda t, jj=jj: t.transpose(
                                    tpf[:], lgt[:, jj * 128:(jj + 1) * 128], identf[0:8, 0:8]),
                                    r=[lgt_b, identf_b], w=[tpf_b])
                                p.op("ve", lambda v, j=j: v.tensor_copy(ltm[:, j, :], tpf[:]),
                                     r=[tpf_b], w=[ltm_b])
                        for j in range(NB):
                            jj = j
                            p.op("ve", lambda v: v.reduce_max(t1c[:], ltm[:, jj, :], axis=AX.X),
                                 r=[ltm_b], w=[t1c_b])
                            p.op("ve", lambda v: v.tensor_scalar_mul(t1c[:], t1c[:], -1.0),
                                 r=[t1c_b], w=[t1c_b])
                            p.op("se", lambda s: s.activation(t8a[:], ltm[:, jj, :], AF.Exp,
                                                              bias=t1c[:]),
                                 r=[ltm_b, t1c_b], w=[t8a_b])
                            p.op("ve", lambda v: v.reduce_sum(t1c[:], t8a[:], axis=AX.X),
                                 r=[t8a_b], w=[t1c_b])
                            p.op("ve", lambda v: v.reciprocal(t1c[:], t1c[:]),
                                 r=[t1c_b], w=[t1c_b])
                            p.op("ve", lambda v: v.tensor_scalar_mul(sctm[:, jj, :], t8a[:], t1c[:]),
                                 r=[t8a_b, t1c_b], w=[sctm_b])
                            p.op("ve", lambda v: v.tensor_tensor(t8a[:], sctm[:, jj, :], ebc[:],
                                                                 ALU.add),
                                 r=[sctm_b, ebc_b], w=[t8a_b])
                            p.op("ve", lambda v: v.reduce_max(t1c[:], t8a[:], axis=AX.X),
                                 r=[t8a_b], w=[t1c_b])
                            p.op("ve", lambda v: v.tensor_scalar(t8b[:], t8a[:], t1c[:], None,
                                                                 ALU.is_equal),
                                 r=[t8a_b, t1c_b], w=[t8b_b])
                            p.op("ve", lambda v: v.tensor_tensor(t8b[:], t8b[:], iotaB8[:],
                                                                 ALU.mult),
                                 r=[iotaB8_b], w=[t8b_b])
                            p.op("ve", lambda v: v.tensor_single_scalar(t8b[:], t8b[:], BIG,
                                                                        ALU.add),
                                 r=[t8b_b], w=[t8b_b])
                            p.op("ve", lambda v: v.tensor_reduce(t1c[:], t8b[:], axis=AX.X,
                                                                 op=ALU.min),
                                 r=[t8b_b], w=[t1c_b])
                            p.op("ve", lambda v: v.tensor_scalar(t8b[:], iota8[:], t1c[:], None,
                                                                 ALU.is_equal),
                                 r=[iota8_b, t1c_b], w=[t8b_b])
                            p.op("ve", lambda v: v.tensor_copy(m1t[:, jj, :], t8b[:]),
                                 r=[t8b_b], w=[m1t_b])
                            p.op("ve", lambda v: v.scalar_tensor_tensor(
                                t8c[:], t8b[:], -BIG, t8a[:], ALU.mult, ALU.add),
                                r=[t8b_b, t8a_b], w=[t8c_b])
                            p.op("ve", lambda v: v.reduce_max(t1c[:], t8c[:], axis=AX.X),
                                 r=[t8c_b], w=[t1c_b])
                            p.op("ve", lambda v: v.tensor_scalar(t8c[:], t8c[:], t1c[:], None,
                                                                 ALU.is_equal),
                                 r=[t1c_b], w=[t8c_b])
                            p.op("ve", lambda v: v.tensor_tensor(t8c[:], t8c[:], iotaB8[:],
                                                                 ALU.mult),
                                 r=[iotaB8_b], w=[t8c_b])
                            p.op("ve", lambda v: v.tensor_single_scalar(t8c[:], t8c[:], BIG,
                                                                        ALU.add),
                                 r=[t8c_b], w=[t8c_b])
                            p.op("ve", lambda v: v.tensor_reduce(t1c[:], t8c[:], axis=AX.X,
                                                                 op=ALU.min),
                                 r=[t8c_b], w=[t1c_b])
                            p.op("ve", lambda v: v.tensor_scalar(t8c[:], iota8[:], t1c[:], None,
                                                                 ALU.is_equal),
                                 r=[iota8_b, t1c_b], w=[t8c_b])
                            p.op("ve", lambda v: v.tensor_copy(m2t[:, jj, :], t8c[:]),
                                 r=[t8c_b], w=[m2t_b])
                            p.op("ve", lambda v: v.tensor_tensor(t8b[:], t8b[:], t8c[:], ALU.add),
                                 r=[t8c_b], w=[t8b_b])
                            p.op("ve", lambda v: v.tensor_copy(mkb[:, jj, :], t8b[:]),
                                 r=[t8b_b], w=[mkb_b])
                            p.op("ve", lambda v: v.tensor_tensor(sctm[:, jj, :], sctm[:, jj, :],
                                                                 t8b[:], ALU.mult),
                                 r=[t8b_b], w=[sctm_b])
                        if dump_sco and "sco" in dumps:
                            p.dma("qs", dumps["sco"], sctm[:], r=[sctm_b])
                            p.flush("qs")
                    p.barrier()

                    with ExitStack() as sRT:
                        idxs4 = [_sb(sRT, nc, tag + "idxs%d" % i, [128, 1], I32)
                                 for i in range(4)]
                        scs4 = [_sb(sRT, nc, tag + "scs%d" % i, [128, D], BF16)
                                for i in range(4)]
                        xbl, xbl_b = _sb(sRT, nc, tag + "xbl", [128, D], BF16)
                        cntp, cntp_b = _ps(sRT, nc, tag + "cntp", [1, 8], F32)
                        prefp, prefp_b = _ps(sRT, nc, tag + "prefp", [128, 8], F32)
                        sbbcp, sbbcp_b = _ps(sRT, nc, tag + "sbbcp", [128, 8], F32)
                        tixp, tixp_b = _ps(sRT, nc, tag + "tixp", [8, 1], F32)
                        b2ep, b2ep_b = _ps(sRT, nc, tag + "b2ep", [1, 40], F32)
                        e40p, e40p_b = _ps(sRT, nc, tag + "e40p", [128, 40], F32)

                        p.op("ve", lambda v: v.memset(r8["run"][0][:], 0.0), w=[r8["run"][1]])
                        for j in range(NB):
                            p.op("te", lambda t, j=j: t.matmul(
                                cntp[:], onescb[:], mkb[:, j, :], start=True, stop=True),
                                r=[onescb_b, mkb_b], w=[cntp_b])
                            p.op("ve", lambda v, j=j: v.tensor_copy(
                                cntA[0:1, j * 8:(j + 1) * 8], cntp[:]), r=[cntp_b], w=[cntA_b])
                            p.op("ve", lambda v, j=j: v.tensor_tensor(
                                r8["run"][0][:], r8["run"][0][:], cntA[0:1, j * 8:(j + 1) * 8],
                                ALU.add), r=[cntA_b], w=[r8["run"][1]])
                        p.op("ve", lambda v: v.tensor_copy(r8["cnt"][0][:], r8["run"][0][:]),
                             r=[r8["run"][1]], w=[r8["cnt"][1]])
                        p.op("ve", lambda v: v.memset(r8["al"][0][:], 0.0), w=[r8["al"][1]])
                        for m in range(16):
                            p.op("ve", lambda v, m=m: v.tensor_single_scalar(
                                r8["tmp8"][0][:], r8["cnt"][0][:], float(m * 128), ALU.is_gt),
                                r=[r8["cnt"][1]], w=[r8["tmp8"][1]])
                            p.op("ve", lambda v: v.scalar_tensor_tensor(
                                r8["al"][0][:], r8["tmp8"][0][:], 128.0, r8["al"][0][:],
                                ALU.mult, ALU.add), r=[r8["tmp8"][1]], w=[r8["al"][1]])
                        p.op("ve", lambda v: v.tensor_tensor_scan(
                            r8["incl"][0][:], os18[:], r8["al"][0][:], 0.0, ALU.mult, ALU.add),
                            r=[os18_b, r8["al"][1]], w=[r8["incl"][1]])
                        p.op("ve", lambda v: v.tensor_tensor(
                            r8["starts"][0][:], r8["incl"][0][:], r8["al"][0][:], ALU.subtract),
                            r=[r8["incl"][1], r8["al"][1]], w=[r8["starts"][1]])
                        p.op("te", lambda t: t.transpose(tixp[:], r8["incl"][0][:],
                                                         identf[0:1, 0:1]),
                             r=[r8["incl"][1], identf_b], w=[tixp_b])
                        p.op("ve", lambda v: v.tensor_copy(inclT[:], tixp[:]),
                             r=[tixp_b], w=[inclT_b])
                        p.op("ve", lambda v: v.tensor_scalar(
                            cmpb[:], iota40[:], inclT[:], None, ALU.is_ge),
                            r=[iota40_b, inclT_b], w=[cmpb_b])
                        p.op("te", lambda t: t.matmul(b2ep[:], onescb[0:8, :], cmpb[:],
                                                      start=True, stop=True),
                             r=[onescb_b, cmpb_b], w=[b2ep_b])
                        p.op("ve", lambda v: v.tensor_copy(blk2e[:], b2ep[:]),
                             r=[b2ep_b], w=[blk2e_b])
                        p.op("te", lambda t: t.matmul(e40p[:], onesf[:], blk2e[:],
                                                      start=True, stop=True),
                             r=[onesf_b, blk2e_b], w=[e40p_b])
                        for hh in range(2):
                            wih, wib = widx[hh]
                            p.op("ve", lambda v, wih=wih, hh=hh: v.tensor_scalar(
                                wih[:], e40p[:], 256.0, float(hh * 128), ALU.mult, ALU.add),
                                r=[e40p_b], w=[wib])
                            p.op("ve", lambda v, wih=wih: v.tensor_scalar(
                                wih[:], wih[:], iotac[:], None, ALU.add),
                                r=[iotac_b], w=[wib])

                        p.op("ve", lambda v: v.memset(r8["run"][0][:], 0.0), w=[r8["run"][1]])
                        for j in range(NB):
                            p.op("ve", lambda v, j=j: v.tensor_tensor(
                                r8["sbrow"][0][:], r8["starts"][0][:], r8["run"][0][:], ALU.add),
                                r=[r8["starts"][1], r8["run"][1]], w=[r8["sbrow"][1]])
                            p.op("ve", lambda v, j=j: v.tensor_tensor(
                                r8["run"][0][:], r8["run"][0][:], cntA[0:1, j * 8:(j + 1) * 8],
                                ALU.add), r=[cntA_b], w=[r8["run"][1]])
                            p.op("te", lambda t, j=j: t.matmul(
                                prefp[:], trix[:], mkb[:, j, :], start=True, stop=True),
                                r=[trix_b, mkb_b], w=[prefp_b])
                            p.op("te", lambda t: t.matmul(
                                sbbcp[:], onesf[:], r8["sbrow"][0][:], start=True, stop=True),
                                r=[onesf_b, r8["sbrow"][1]], w=[sbbcp_b])
                            p.op("ve", lambda v, j=j: v.tensor_tensor(
                                sfa[:], prefp[:], m1t[:, j, :], ALU.mult),
                                r=[prefp_b, m1t_b], w=[sfa_b])
                            p.op("ve", lambda v, j=j: v.tensor_tensor(
                                sfa[:], prefp[:], mkb[:, j, :], ALU.mult),
                                r=[prefp_b, mkb_b], w=[sfa_b])
                            p.op("ve", lambda v, j=j: v.tensor_tensor(
                                sfb[:], sbbcp[:], mkb[:, j, :], ALU.mult),
                                r=[sbbcp_b, mkb_b], w=[sfb_b])
                            p.op("ve", lambda v: v.tensor_tensor(
                                slotf[:], sfa[:], sfb[:], ALU.add),
                                r=[sfa_b, sfb_b], w=[slotf_b])
                            p.op("ve", lambda v, j=j: v.tensor_scalar(
                                sfa[:], mkb[:, j, :], -BIG, BIG, ALU.mult, ALU.add),
                                r=[mkb_b], w=[sfa_b])
                            p.op("ve", lambda v: v.tensor_tensor(
                                slotf[:], slotf[:], sfa[:], ALU.add),
                                r=[sfa_b], w=[slotf_b])
                            p.op("ve", lambda v, j=j: v.tensor_tensor(
                                sfa[:], slotf[:], m1t[:, j, :], ALU.mult),
                                r=[slotf_b, m1t_b], w=[sfa_b])
                            p.op("ve", lambda v, j=j: v.reduce_sum(
                                sr1[:, j:j + 1], sfa[:], axis=AX.X), r=[sfa_b], w=[sr1_b])
                            p.op("ve", lambda v, j=j: v.tensor_tensor(
                                sfa[:], slotf[:], m2t[:, j, :], ALU.mult),
                                r=[slotf_b, m2t_b], w=[sfa_b])
                            p.op("ve", lambda v, j=j: v.reduce_sum(
                                sr2[:, j:j + 1], sfa[:], axis=AX.X), r=[sfa_b], w=[sr2_b])
                            p.dma("qs", xbl[:], xl_d.ap()[j * 128:(j + 1) * 128, :],
                                  r=[yb["xl_d"]], w=[xbl_b])
                            p.flush("qs")
                            for e in range(NE):
                                sch, scb = scs4[e % 4]
                                ixh, ixb = idxs4[e % 4]
                                p.op("ve", lambda v, j=j, e=e, sch=sch: v.tensor_scalar_mul(
                                    sch[:], xbl[:], sctm[:, j, e:e + 1]),
                                    r=[xbl_b, sctm_b], w=[scb])
                                p.op("ve", lambda v, e=e, ixh=ixh: v.tensor_copy(
                                    ixh[:], slotf[:, e:e + 1]), r=[slotf_b], w=[ixb])
                                p.dma("ge", xsl_d.ap(), sch[:], r=[scb, ixb],
                                      w=[yb["xsl_d"]],
                                      indirect=(ixh, 0))
                                if e % 4 == 3:
                                    p.flush("ge")
                        p.flush("ge")
                    p.barrier()

                    with ExitStack() as sF2:
                        w13u = [_sb(sF2, nc, tag + "w13u%d" % i, [128, 2, 11, D], BF16)
                                for i in range(2)]
                        w2u, w2u_b = _sb(sF2, nc, tag + "w2u", [128, 11, D], BF16)
                        idxw = [_sb(sF2, nc, tag + "idxw%d" % i, [128, 1], I32) for i in range(2)]
                        xg = [_sb(sF2, nc, tag + "xg%d" % i, [128, D], BF16) for i in range(2)]
                        xbt, xbt_b = _sb(sF2, nc, tag + "xbt", [128, 8, 128], BF16)
                        hbl, hbl_b = _sb(sF2, nc, tag + "hbl", [128, 11, 128], BF16)
                        ofm, ofm_b = _sb(sF2, nc, tag + "ofm", [128, 8, 128], F32)
                        otm, otm_b = _sb(sF2, nc, tag + "otm", [128, D], BF16)
                        s1b, s1b_b = _sb(sF2, nc, tag + "s1b", [128, 128], F32)
                        g1b, g1b_b = xg[0]
                        g2b, g2b_b = xg[1]
                        ir1, ir1_b = _sb(sF2, nc, tag + "ir1", [128, 1], I32)
                        ir2, ir2_b = _sb(sF2, nc, tag + "ir2", [128, 1], I32)
                        u1p = [_ps(sF2, nc, tag + "u1p%d" % i, [128, 128], F32) for i in range(2)]
                        u3p = [_ps(sF2, nc, tag + "u3p%d" % i, [128, 128], F32) for i in range(2)]
                        o_p = [_ps(sF2, nc, tag + "o_p%d" % i, [128, 128], F32) for i in range(2)]
                        trxp, trxp_b = _ps(sF2, nc, tag + "trxp", [128, 128], BF16)
                        trop, trop_b = _ps(sF2, nc, tag + "trop", [128, 128], F32)

                        w13view = w13b.ap().rearrange("e h p m l w -> (e h p) (m l w)")
                        w2view = w2b.ap().rearrange("e h p l w -> (e h p) (l w)")

                        idxw2 = [_sb(sF2, nc, tag + "idxw2_%d" % i, [128, 1], I32)
                                 for i in range(2)]

                        def ld_w13(blk, half):
                            up = (blk * 2 + half) % 2
                            wh, whb = w13u[up]
                            iw, iwb = idxw[up]
                            p.op("ve", lambda v, blk=blk, half=half, iw=iw: v.tensor_copy(
                                iw[:], widx[half][0][:, blk:blk + 1]),
                                r=[widx[half][1]], w=[iwb])
                            p.dma("ge", wh[:].rearrange("p a l w -> p (a l w)"), w13view,
                                  r=[iwb], w=[whb], indirect_in=(iw, 0))

                        def ld_w2(blk, half):
                            up = (blk * 2 + half) % 2
                            iw, iwb = idxw2[up]
                            p.op("ve", lambda v, blk=blk, half=half, iw=iw: v.tensor_copy(
                                iw[:], widx[half][0][:, blk:blk + 1]),
                                r=[widx[half][1]], w=[iwb])
                            p.dma("ge", w2u[:].rearrange("p l w -> p (l w)"), w2view,
                                  r=[iwb], w=[w2u_b], indirect_in=(iw, 0))

                        def ld_x(blk):
                            bp = blk % 2
                            xh, xb = xg[bp]
                            p.dma("qs", xh[:], xsl_d.ap()[blk * 128:(blk + 1) * 128, :],
                                  r=[yb["xsl_d"]], w=[xb])
                            p.flush("qs")

                        ld_x(0)
                        ld_w13(0, 0)
                        units = [(b_, h_) for b_ in range(40) for h_ in (0, 1)]
                        for ui, (blk, half) in enumerate(units):
                            bp = blk % 2
                            xh, xb = xg[bp]
                            if True:
                                up = (blk * 2 + half) % 2
                                wh, whb = w13u[up]
                                if ui + 1 < len(units):
                                    ld_w13(*units[ui + 1])
                                ld_w2(blk, half)
                                if half == 0 and blk < 39:
                                    ld_x(blk + 1)
                                p.flush("ge")
                                if half == 0:
                                    for dc in range(8):
                                        p.op("te", lambda t, dc=dc, xh=xh: t.transpose(
                                            trxp[:], xh[:, dc * 128:(dc + 1) * 128], identb[:]),
                                            r=[xb, identb_b], w=[trxp_b])
                                        p.op("se", lambda s, dc=dc: s.copy(xbt[:, dc, :], trxp[:]),
                                             r=[trxp_b], w=[xbt_b])
                                for hl in range(11):
                                    pu = hl % 2
                                    up1, up1_b = u1p[pu]
                                    up3, up3_b = u3p[pu]
                                    for dc in range(8):
                                        p.op("te", lambda t, dc=dc, hl=hl, wh=wh, up1=up1: t.matmul(
                                            up1[:], wh[:, 0, hl, dc * 128:(dc + 1) * 128],
                                            xbt[:, dc, :], start=(dc == 0), stop=(dc == 7)),
                                            r=[whb, xbt_b], w=[up1_b])
                                    for dc in range(8):
                                        p.op("te", lambda t, dc=dc, hl=hl, wh=wh, up3=up3: t.matmul(
                                            up3[:], wh[:, 1, hl, dc * 128:(dc + 1) * 128],
                                            xbt[:, dc, :], start=(dc == 0), stop=(dc == 7)),
                                            r=[whb, xbt_b], w=[up3_b])
                                    p.op("se", lambda s, up1=up1: s.activation(
                                        s1b[:], up1[:], AF.Silu), r=[up1_b], w=[s1b_b])
                                    p.op("ve", lambda v, hl=hl, up3=up3: v.tensor_tensor(
                                        hbl[:, hl, :], up3[:], s1b[:], ALU.mult),
                                        r=[up3_b, s1b_b], w=[hbl_b])
                                for dc in range(8):
                                    po = dc % 2
                                    oph, opb = o_p[po]
                                    for hl in range(11):
                                        p.op("te", lambda t, hl=hl, dc=dc, oph=oph: t.matmul(
                                            oph[:], w2u[:, hl, dc * 128:(dc + 1) * 128],
                                            hbl[:, hl, :], start=(hl == 0), stop=(hl == 10)),
                                            r=[w2u_b, hbl_b], w=[opb])
                                    if half == 0:
                                        p.op("ve", lambda v, dc=dc, oph=oph: v.tensor_copy(
                                            ofm[:, dc, :], oph[:]), r=[opb], w=[ofm_b])
                                    else:
                                        p.op("ve", lambda v, dc=dc, oph=oph: v.tensor_tensor(
                                            ofm[:, dc, :], ofm[:, dc, :], oph[:], ALU.add),
                                            r=[opb], w=[ofm_b])
                            if half == 1:
                                for dc in range(8):
                                    p.op("te", lambda t, dc=dc: t.transpose(
                                        trop[:], ofm[:, dc, :], identf[:]),
                                        r=[ofm_b, identf_b], w=[trop_b])
                                    p.op("ve", lambda v, dc=dc: v.tensor_copy(
                                        otm[:, dc * 128:(dc + 1) * 128], trop[:]),
                                        r=[trop_b], w=[otm_b])
                                p.dma("qs", osl_d.ap()[blk * 128:(blk + 1) * 128, :], otm[:],
                                      r=[otm_b], w=[yb["osl_d"]])
                                p.flush("qs")
                        for j in range(NB):
                            p.op("ve", lambda v, j=j: v.tensor_copy(ir1[:], sr1[:, j:j + 1]),
                                 r=[sr1_b], w=[ir1_b])
                            p.op("ve", lambda v, j=j: v.tensor_copy(ir2[:], sr2[:, j:j + 1]),
                                 r=[sr2_b], w=[ir2_b])
                            p.dma("ge", g1b[:], osl_d.ap(), r=[yb["osl_d"], ir1_b], w=[g1b_b],
                                  indirect_in=(ir1, 0))
                            p.dma("ge", g2b[:], osl_d.ap(), r=[yb["osl_d"], ir2_b], w=[g2b_b],
                                  indirect_in=(ir2, 0))
                            p.flush("ge")
                            p.op("ve", lambda v, j=j: v.tensor_tensor(
                                res[:, j, :], res[:, j, :], g1b[:], ALU.add),
                                r=[g1b_b], w=[res_b])
                            p.op("ve", lambda v, j=j: v.tensor_tensor(
                                res[:, j, :], res[:, j, :], g2b[:], ALU.add),
                                r=[g2b_b], w=[res_b])
                p.barrier()

            emit_ffn("f1", dump_sco=True)
            if "res2" in dumps:
                p.dma("qs", dumps["res2"].rearrange("(j p) d -> p j d", p=128), res[:],
                      r=[res_b])
                p.flush("qs")
                p.barrier()

            with ExitStack() as sH:
                sH.enter_context(nc.named_scope("hgrn"))
                X3T, X3T_b = _sb(sH, nc, "X3T", [128, 8, TOK], BF16)
                ost, ost_b = _sb(sH, nc, "ost", [128, 8, TOK], BF16)
                bnd, bnd_b = _sb(sH, nc, "bnd", [128, 8, 3], BF16)
                hend, hend_b = _sb(sH, nc, "hend", [128, 8], F32)
                carry, carry_b = _sb(sH, nc, "carry", [128, 8], F32)
                cg0, cg0_b = _sb(sH, nc, "cg0", [128, 8], F32)
                bg0, bg0_b = _sb(sH, nc, "bg0", [128, 8, 3], BF16)

                with ExitStack() as sH0:
                    xlh, xlh_b = _sb(sH0, nc, "h_xl", [128, D], BF16)
                    sqh, sqh_b = _sb(sH0, nc, "h_sq", [128, D], F32)
                    sth, sth_b = _sb(sH0, nc, "h_st", [128, 8], F32)
                    psbH, psbH_b = _ps(sH0, nc, "h_psb", [128, 128], BF16)
                    for j in range(NB):
                        emit_ln(p, res[:, j, :], [res_b], xlh[:], xlh_b, sqh, sqh_b, sth, sth_b)
                        for dc in range(8):
                            p.op("te", lambda t, dc=dc: t.transpose(
                                psbH[:], xlh[:, dc * 128:(dc + 1) * 128], identb[:]),
                                r=[xlh_b, identb_b], w=[psbH_b])
                            p.op("se", lambda s, dc=dc, j=j: s.copy(
                                X3T[:, dc, j * 128:(j + 1) * 128], psbH[:]),
                                r=[psbH_b], w=[X3T_b])
                    p.op("ve", lambda v: v.tensor_copy(bnd[:], X3T[:, :, TOK - 3:TOK]),
                         r=[X3T_b], w=[bnd_b])
                    p.dma("qs", b_pub.ap(), bnd[:].rearrange("p h w -> p (h w)"),
                          r=[bnd_b], w=[yb["b_pub"]])
                    p.flush("qs")
                    p.cc("AllGather", ALU.bypass, b_pub.ap().opt(), b_ag.ap().opt(),
                         r=[yb["b_pub"]], w=[yb["b_ag"]])
                    p.dma("qs", bg0[:].rearrange("p h w -> p (h w)"), b_ag.ap()[0:128, :],
                          r=[yb["b_ag"]], w=[bg0_b])
                    p.flush("qs")
                    p.op("ve", lambda v: v.tensor_scalar_mul(bnd[:], bg0[:], mBs[:]),
                         r=[bg0_b, mBs_b], w=[bnd_b])
                p.barrier()

                def hgrn_pass(pass2):
                    with ExitStack() as sP:
                        wic = [_sb(sP, nc, "hwi%d%d" % (pass2, i), [128, 8, 128], BF16)
                               for i in range(2)]
                        wfc = [_sb(sP, nc, "hwf%d%d" % (pass2, i), [128, 8, 128], BF16)
                               for i in range(2)]
                        ipd, ipd_b = _sb(sP, nc, "hip%d" % pass2, [128, TOK + 3], BF16)
                        fpd, fpd_b = _sb(sP, nc, "hfp%d" % pass2, [128, TOK + 3], BF16)
                        fgr, fgr_b = _sb(sP, nc, "hfg%d" % pass2, [128, TOK], F32)
                        iir, iir_b = _sb(sP, nc, "hii%d" % pass2, [128, TOK], F32)
                        scs, scs_b = _sb(sP, nc, "hsc%d" % pass2, [128, TOK], F32)
                        c1, c1_b = _sb(sP, nc, "hc1%d" % pass2, [128, 512], F32)
                        c2, c2_b = _sb(sP, nc, "hc2%d" % pass2, [128, 512], F32)
                        sgx, sgx_b = _sb(sP, nc, "hsg%d" % pass2, [128, 512], F32)
                        slx, slx_b = _sb(sP, nc, "hsl%d" % pass2, [128, 512], F32)
                        ipp = [_ps(sP, nc, "hipp%d%d" % (pass2, i), [128, 512], F32)
                               for i in range(2)]
                        fpp = [_ps(sP, nc, "hfpp%d%d" % (pass2, i), [128, 512], F32)
                               for i in range(2)]
                        bpp, bpp_b = _ps(sP, nc, "hbpp%d" % pass2, [128, 3], F32)

                        for oc in range(8):
                            par = oc % 2
                            wih, wib = wic[par]
                            wfh, wfb = wfc[par]
                            p.dma("ge", wih[:], wi[oc], w=[wib])
                            p.dma("ge", wfh[:], wf[oc], w=[wfb])
                            p.flush("ge")
                            for (wh, wb, pad, padb) in ((wih, wib, ipd, ipd_b),
                                                        (wfh, wfb, fpd, fpd_b)):
                                for dc in range(8):
                                    p.op("te", lambda t, dc=dc, wh=wh: t.matmul(
                                        bpp[:], wh[:, dc, :], bnd[:, dc, :],
                                        start=(dc == 0), stop=(dc == 7)),
                                        r=[wb, bnd_b], w=[bpp_b])
                                p.op("se", lambda s, pad=pad: s.copy(pad[:, 0:3], bpp[:]),
                                     r=[bpp_b], w=[padb])
                                pp = ipp if pad is ipd else fpp
                                for s4 in range(4):
                                    ph, pb = pp[s4 % 2]
                                    for dc in range(8):
                                        p.op("te", lambda t, dc=dc, s4=s4, wh=wh, ph=ph: t.matmul(
                                            ph[:], wh[:, dc, :],
                                            X3T[:, dc, s4 * 512:(s4 + 1) * 512],
                                            start=(dc == 0), stop=(dc == 7)),
                                            r=[wb, X3T_b], w=[pb])
                                    p.op("se", lambda s, s4=s4, ph=ph, pad=pad: s.copy(
                                        pad[:, 3 + s4 * 512: 3 + (s4 + 1) * 512], ph[:]),
                                        r=[pb], w=[padb])
                            for s4 in range(4):
                                sl = slice(s4 * 512, (s4 + 1) * 512)
                                p.op("ve", lambda v, s4=s4, oc=oc: v.tensor_scalar_mul(
                                    c1[:], ipd[:, s4 * 512:s4 * 512 + 512], cwi[:, oc, 0:1]),
                                    r=[ipd_b, cwi_b], w=[c1_b])
                                for k in range(1, 4):
                                    p.op("ve", lambda v, s4=s4, k=k, oc=oc: v.scalar_tensor_tensor(
                                        c1[:], ipd[:, s4 * 512 + k:s4 * 512 + k + 512],
                                        cwi[:, oc, k:k + 1], c1[:], ALU.mult, ALU.add),
                                        r=[ipd_b, cwi_b], w=[c1_b])
                                p.op("ve", lambda v, s4=s4, oc=oc: v.tensor_scalar_mul(
                                    c2[:], fpd[:, s4 * 512:s4 * 512 + 512], cwf[:, oc, 0:1]),
                                    r=[fpd_b, cwf_b], w=[c2_b])
                                for k in range(1, 4):
                                    p.op("ve", lambda v, s4=s4, k=k, oc=oc: v.scalar_tensor_tensor(
                                        c2[:], fpd[:, s4 * 512 + k:s4 * 512 + k + 512],
                                        cwf[:, oc, k:k + 1], c2[:], ALU.mult, ALU.add),
                                        r=[fpd_b, cwf_b], w=[c2_b])
                                p.op("se", lambda s, sl=sl: s.activation(
                                    fgr[:, sl], c2[:], AF.Sigmoid), r=[c2_b], w=[fgr_b])
                                p.op("se", lambda s: s.activation(
                                    sgx[:], c1[:], AF.Sigmoid), r=[c1_b], w=[sgx_b])
                                p.op("ve", lambda v: v.tensor_tensor(
                                    slx[:], c1[:], sgx[:], ALU.mult), r=[c1_b, sgx_b], w=[slx_b])
                                p.op("ve", lambda v, sl=sl: v.tensor_scalar(
                                    sgx[:], fgr[:, sl], -1.0, 1.0, ALU.mult, ALU.add),
                                    r=[fgr_b], w=[sgx_b])
                                p.op("ve", lambda v, sl=sl: v.tensor_tensor(
                                    iir[:, sl], slx[:], sgx[:], ALU.mult),
                                    r=[slx_b, sgx_b], w=[iir_b])
                            if not pass2:
                                p.op("ve", lambda v: v.tensor_tensor_scan(
                                    scs[:], fgr[:], iir[:], 0.0, ALU.mult, ALU.add),
                                    r=[fgr_b, iir_b], w=[scs_b])
                                p.op("ve", lambda v, oc=oc: v.tensor_copy(
                                    hend[:, oc:oc + 1], scs[:, TOK - 1:TOK]),
                                    r=[scs_b], w=[hend_b])
                            else:
                                p.op("ve", lambda v, oc=oc: v.tensor_tensor_scan(
                                    ost[:, oc, :], fgr[:], iir[:], carry[:, oc:oc + 1],
                                    ALU.mult, ALU.add),
                                    r=[fgr_b, iir_b, carry_b], w=[ost_b])

                hgrn_pass(False)
                p.barrier()
                p.dma("qs", c_pub.ap(), hend[:], r=[hend_b], w=[yb["c_pub"]])
                p.flush("qs")
                p.cc("AllGather", ALU.bypass, c_pub.ap().opt(), c_ag.ap().opt(),
                     r=[yb["c_pub"]], w=[yb["c_ag"]])
                p.dma("qs", cg0[:], c_ag.ap()[0:128, :], r=[yb["c_ag"]], w=[cg0_b])
                p.flush("qs")
                p.op("ve", lambda v: v.tensor_scalar_mul(carry[:], cg0[:], mBs[:]),
                     r=[cg0_b, mBs_b], w=[carry_b])
                hgrn_pass(True)
                p.barrier()

                with ExitStack() as sN:
                    o2s, o2s_b = _sb(sN, nc, "n_o2", [128, TOK], BF16)
                    rrow, rrow_b = _sb(sN, nc, "n_rr", [1, TOK], F32)
                    gss, gss_b = _sb(sN, nc, "n_gs", [128, 512], F32)
                    gsl, gsl_b = _sb(sN, nc, "n_gl", [128, 512], F32)
                    ons, ons_b = _sb(sN, nc, "n_on", [128, 512], F32)
                    wgc = [_sb(sN, nc, "n_wg%d" % i, [128, 8, 128], BF16) for i in range(2)]
                    with ExitStack() as sN1:
                        rmp, rmp_b = _ps(sN1, nc, "n_rmp", [1, 4, 512], F32)
                        for oc in range(8):
                            p.op("ve", lambda v, oc=oc: v.tensor_tensor(
                                o2s[:], ost[:, oc, :], ost[:, oc, :], ALU.mult),
                                r=[ost_b], w=[o2s_b])
                            for s4 in range(4):
                                p.op("te", lambda t, s4=s4, oc=oc: t.matmul(
                                    rmp[:, s4, :], onescb[:],
                                    o2s[:, s4 * 512:(s4 + 1) * 512],
                                    start=(oc == 0), stop=(oc == 7)),
                                    r=[onescb_b, o2s_b], w=[rmp_b])
                        p.op("se", lambda s: s.copy(rrow[:], rmp[0:1, :, :].rearrange(
                            "x a b -> x (a b)")), r=[rmp_b], w=[rrow_b])
                    p.barrier()
                    p.op("se", lambda s: s.activation(rrow[:], rrow[:], AF.Ln,
                                                      scale=1.0 / D, bias=RMS_EPS),
                         r=[rrow_b], w=[rrow_b])
                    p.op("se", lambda s: s.activation(rrow[:], rrow[:], AF.Exp, scale=-0.5),
                         r=[rrow_b], w=[rrow_b])
                    with ExitStack() as sN2:
                        rbp, rbp_b = _ps(sN2, nc, "n_rbp", [128, 4, 512], F32)
                        gpp = [_ps(sN2, nc, "n_gp%d" % i, [128, 512], F32) for i in range(2)]
                        for s4 in range(4):
                            p.op("te", lambda t, s4=s4: t.matmul(
                                rbp[:, s4, :], onesf[:], rrow[0:1, s4 * 512:(s4 + 1) * 512],
                                start=True, stop=True),
                                r=[onesf_b, rrow_b], w=[rbp_b])
                        for oc in range(8):
                            par = oc % 2
                            wgh, wgb = wgc[par]
                            p.dma("ge", wgh[:], wg[oc], w=[wgb])
                            p.flush("ge")
                            for s4 in range(4):
                                gph, gpb = gpp[s4 % 2]
                                for dc in range(8):
                                    p.op("te", lambda t, dc=dc, s4=s4, wgh=wgh, gph=gph: t.matmul(
                                        gph[:], wgh[:, dc, :],
                                        X3T[:, dc, s4 * 512:(s4 + 1) * 512],
                                        start=(dc == 0), stop=(dc == 7)),
                                        r=[wgb, X3T_b], w=[gpb])
                                p.op("se", lambda s, gph=gph: s.activation(
                                    gss[:], gph[:], AF.Sigmoid), r=[gpb], w=[gss_b])
                                p.op("ve", lambda v, gph=gph: v.tensor_tensor(
                                    gsl[:], gph[:], gss[:], ALU.mult), r=[gpb, gss_b], w=[gsl_b])
                                sl = slice(s4 * 512, (s4 + 1) * 512)
                                p.op("ve", lambda v, oc=oc, s4=s4, sl=sl: v.scalar_tensor_tensor(
                                    ons[:], ost[:, oc, sl], gnws[:, oc:oc + 1], rbp[:, s4, :],
                                    ALU.mult, ALU.mult),
                                    r=[ost_b, gnws_b, rbp_b], w=[ons_b])
                                p.op("ve", lambda v, oc=oc, sl=sl: v.tensor_tensor(
                                    ost[:, oc, sl], ons[:], gsl[:], ALU.mult),
                                    r=[ons_b, gsl_b], w=[ost_b])
                    p.barrier()
                    with ExitStack() as sN3:
                        wop = [_ps(sN3, nc, "n_wo%d" % i, [128, 512], F32) for i in range(2)]
                        trh, trh_b = _ps(sN3, nc, "n_trh", [128, 4, 128], BF16)
                        owh, owh_b = _sb(sN3, nc, "n_ow", [128, 512], BF16)
                        woc = [_sb(sN3, nc, "n_wc%d" % i, [128, 8, 128], BF16) for i in range(2)]
                        for ooc in range(8):
                            par = ooc % 2
                            wch, wcb = woc[par]
                            p.dma("ge", wch[:], wo[ooc], w=[wcb])
                            p.flush("ge")
                            for tt in range(4):
                                oph, opb = wop[tt % 2]
                                for dc in range(8):
                                    p.op("te", lambda t, dc=dc, tt=tt, wch=wch, oph=oph: t.matmul(
                                        oph[:], wch[:, dc, :],
                                        ost[:, dc, tt * 512:(tt + 1) * 512],
                                        start=(dc == 0), stop=(dc == 7)),
                                        r=[wcb, ost_b], w=[opb])
                                p.op("ve", lambda v, oph=oph: v.tensor_copy(owh[:], oph[:]),
                                     r=[opb], w=[owh_b])
                                for k in range(4):
                                    p.op("te", lambda t, k=k: t.transpose(
                                        trh[:, k, :], owh[:, k * 128:(k + 1) * 128], identb[:]),
                                        r=[owh_b, identb_b], w=[trh_b])
                                p.op("ve", lambda v, tt=tt, ooc=ooc: v.tensor_tensor(
                                    res[:, tt * 4:(tt + 1) * 4, ooc * 128:(ooc + 1) * 128],
                                    res[:, tt * 4:(tt + 1) * 4, ooc * 128:(ooc + 1) * 128],
                                    trh[:], ALU.add),
                                    r=[trh_b], w=[res_b])
            p.barrier()
            if "res3" in dumps:
                p.dma("qs", dumps["res3"].rearrange("(j p) d -> p j d", p=128), res[:],
                      r=[res_b])
                p.flush("qs")
                p.barrier()

            emit_ffn("f2")

            for j in range(NB):
                p.dma("qs", y_out[j * 128:(j + 1) * 128, :], res[:, j, :], r=[res_b])
            p.flush("qs")
        p.barrier()
    return nc


def kernel(**inputs):
    x = np.ascontiguousarray(np.asarray(inputs["L0_attn_a_x"], dtype=np.float32))
    W = np.asarray(inputs["L0_attn_a_W"], dtype=np.float32)
    Wi = np.ascontiguousarray(np.asarray(inputs["L0_attn_b_Wi"], dtype=np.float32))
    Wf = np.ascontiguousarray(np.asarray(inputs["L0_attn_b_Wf"], dtype=np.float32))
    Wg = np.ascontiguousarray(np.asarray(inputs["L0_attn_b_Wg"], dtype=np.float32))
    Wo = np.ascontiguousarray(np.asarray(inputs["L0_attn_b_Wo"], dtype=np.float32))
    ci = np.asarray(inputs["L0_attn_b_conv_i"], dtype=np.float32)
    cf = np.asarray(inputs["L0_attn_b_conv_f"], dtype=np.float32)
    gn = np.asarray(inputs["L0_attn_b_g_norm_weight"], dtype=np.float32)
    gw = np.asarray(inputs["L0_ffn_gate_w"], dtype=np.float32)
    eb = np.asarray(inputs["L0_ffn_expert_bias"], dtype=np.float32)
    w1 = np.ascontiguousarray(np.asarray(inputs["L0_ffn_w1"], dtype=np.float32))
    w2 = np.ascontiguousarray(np.asarray(inputs["L0_ffn_w2"], dtype=np.float32))
    w3 = np.ascontiguousarray(np.asarray(inputs["L0_ffn_w3"], dtype=np.float32))

    w1r = np.ascontiguousarray(w1.reshape(8, 8, 128, 22, 128).transpose(0, 3, 2, 1, 4)
                               .reshape(8, 22, 128, 1024))
    w3r = np.ascontiguousarray(w3.reshape(8, 8, 128, 22, 128).transpose(0, 3, 2, 1, 4)
                               .reshape(8, 22, 128, 1024))
    w2r = np.ascontiguousarray(w2.reshape(8, 2, 11, 128, 1024).transpose(0, 1, 3, 2, 4))
    def colmajor(m):
        return np.ascontiguousarray(m.reshape(8, 128, 8, 128).transpose(2, 1, 0, 3))
    wir, wfr, wgr_, wor = colmajor(Wi), colmajor(Wf), colmajor(Wg), colmajor(Wo)
    ci_l = np.ascontiguousarray(ci.reshape(8, 128, 4).transpose(1, 0, 2))
    cf_l = np.ascontiguousarray(cf.reshape(8, 128, 4).transpose(1, 0, 2))
    gn_l = np.ascontiguousarray(gn.reshape(8, 128).T)
    gwt_l = np.ascontiguousarray(gw.T.reshape(8, 128, 8).transpose(1, 0, 2))
    eb_l = np.ascontiguousarray(eb.reshape(1, 8))

    nc = build_nc()
    in_maps = []
    for c in range(8):
        b, hf = c // 2, c % 2
        wq = W[hf * 512:(hf + 1) * 512]
        wk = W[1024 + hf * 512:1024 + (hf + 1) * 512]
        wv = W[2048 + hf * 512:2048 + (hf + 1) * 512]
        wgr = W[3072 + hf * 8:3072 + (hf + 1) * 8]
        wqkvT = np.ascontiguousarray(np.concatenate([wq, wk, wv, wgr], axis=0).T)
        im = {
            "x_own": np.ascontiguousarray(x[b, hf * TOK:(hf + 1) * TOK]),
            "x_full": np.ascontiguousarray(x[b]),
            "w_qkvT": wqkvT,
            "wi": wir, "wf": wfr, "wg": wgr_, "wo": wor,
            "conv_i": ci_l, "conv_f": cf_l, "gnw": gn_l, "gwt": gwt_l, "ebias": eb_l,
            "w1": w1r, "w2": w2r, "w3": w3r,
            "mA": np.full((128, 1), 1.0 - hf, np.float32),
            "mB": np.full((128, 1), float(hf), np.float32),
        }
        in_maps.append(im)
    res = run_bass_kernel_spmd(nc, in_maps, core_ids=list(range(8)))
    out = np.zeros((4, T, D), np.float32)
    for c in range(8):
        b, hf = c // 2, c % 2
        out[b, hf * TOK:(hf + 1) * TOK] = res.results[c]["y_out"]
    return out


# version 11
# speedup vs baseline: 1.4259x; 1.0607x over previous
import numpy as np
from contextlib import ExitStack

import concourse.bass as bass
import concourse.mybir as mybir
from concourse.bass_utils import run_bass_kernel_spmd

import bass_rust as _br
F32 = mybir.dt.float32
BF16 = mybir.dt.bfloat16
I32 = mybir.dt.int32
IOA = _br.IndirectOffsetOnAxis
AF = mybir.ActivationFunctionType
ALU = mybir.AluOpType
AX = mybir.AxisListType

D = 1024
T = 4096
TOK = 2048
NB = 16
NBF = 32
NHL = 8
HS = 64
HID = 2816
HC = 22
NE = 8
LOGSCALE = float(np.log(1.0 / 8.0))
EPS_ATT = 1e-7
LN_EPS = 1e-5
RMS_EPS = 1e-5
NEGBIG = -1.0e9
BIG = 1000.0
GROUPS2 = [[0, 1], [2, 3], [4, 5], [6, 7]]

_PENDQ = {"qs": ("PEND_qs", 0), "ge": ("PEND_ge", 0)}


class Buf:
    __slots__ = ("w", "r", "name")

    def __init__(self, name=""):
        self.w = None
        self.r = []
        self.name = name


class Prog:
    def __init__(self, nc, es):
        self.nc = nc
        self.E = {"te": nc.tensor, "ve": nc.vector, "se": nc.scalar,
                  "ge": nc.gpsimd, "qs": nc.sync}
        self.sem = {}
        for name in ("te", "ve", "se", "ge", "qsd", "ged"):
            self.sem[name] = es.enter_context(nc.semaphore("S_" + name))
        self.cnt = {"te": 0, "ve": 0, "se": 0, "ge": 0}
        self.dcnt = {"qs": 0, "ge": 0}
        self.pend = {"qs": [], "ge": []}
        self.waited = {k: {} for k in self.E}

    def _wait(self, ekey, toks):
        w = self.waited[ekey]
        need = {}
        for t in toks:
            if t is None:
                continue
            s, v = t
            if s.startswith("PEND"):
                if s == "PEND_" + ekey:
                    continue
                raise RuntimeError("buffer consumed before dma flush (%s)" % s)
            if s == "te" and ekey == "te":
                continue
            if v > w.get(s, 0) and v > need.get(s, 0):
                need[s] = v
        for s, v in sorted(need.items()):
            self.E[ekey].wait_ge(self.sem[s], v)
            w[s] = v

    def _collect(self, r, w):
        toks = []
        for b in r:
            toks.append(b.w)
        for b in w:
            toks.append(b.w)
            toks.extend(b.r)
        return toks

    def op(self, ekey, fn, r=(), w=()):
        self._wait(ekey, self._collect(r, w))
        inst = fn(self.E[ekey])
        self.cnt[ekey] += 1
        inst.then_inc(self.sem[ekey], 1)
        t = (ekey, self.cnt[ekey])
        for b in r:
            b.r.append(t)
        for b in w:
            b.w = t
            b.r = []
        return t

    def dma(self, q, out_ap, in_ap, r=(), w=(), indirect=None, indirect_in=None, **kw):
        self._wait(q, self._collect(r, w))
        if indirect is not None:
            idx, ax = indirect
            inst = self.E[q].indirect_dma_start(out_ap, IOA(idx[:], ax), in_ap, None,
                                                oob_is_err=False)
        elif indirect_in is not None:
            idx, ax = indirect_in
            inst = self.E[q].indirect_dma_start(out_ap, None, in_ap, IOA(idx[:], ax),
                                                oob_is_err=False)
        else:
            inst = self.E[q].dma_start(out=out_ap, in_=in_ap, **kw)
        dsem = "qsd" if q == "qs" else "ged"
        inst.then_inc(self.sem[dsem], 16)
        self.dcnt[q] += 16
        pt = _PENDQ[q]
        for b in w:
            b.w = pt
            b.r = []
        for b in r:
            b.r.append(pt)
        self.pend[q].append((r, w))

    def flush(self, q):
        if not self.pend[q]:
            return
        dsem = "qsd" if q == "qs" else "ged"
        v = self.dcnt[q]
        self.E[q].wait_ge(self.sem[dsem], v)
        self.waited[q][dsem] = v
        t = (dsem, v)
        pt = _PENDQ[q]
        for r, w in self.pend[q]:
            for b in w:
                if b.w == pt:
                    b.w = t
            for b in r:
                b.r = [t if x == pt else x for x in b.r]
        self.pend[q] = []

    def cc(self, kind, aluop, in_ap, out_ap, r=(), w=()):
        return self.op("ge", lambda g: g.collective_compute(
            kind, aluop, replica_groups=GROUPS2,
            ins=[in_ap], outs=[out_ap]), r=r, w=w)

    def barrier(self):
        self.flush("qs")
        self.flush("ge")
        toks = [(e, self.cnt[e]) for e in self.cnt]
        toks += [("qsd", self.dcnt["qs"]), ("ged", self.dcnt["ge"])]
        for ekey in self.E:
            self._wait(ekey, toks)


def _sb(es, nc, name, shape, dtype):
    h = es.enter_context(nc.sbuf_tensor(name, shape, dtype))
    return h, Buf(name)


def _ps(es, nc, name, shape, dtype):
    esz = mybir.dt.size(dtype)
    n = 1
    for d in shape[1:]:
        n *= d
    per = n * esz
    pad = (-per) % 2048
    if pad == 0:
        h = es.enter_context(nc.psum_tensor(name, shape, dtype))
        return h, Buf(name)
    flat = n + pad // esz
    h = es.enter_context(nc.psum_tensor(name, [shape[0], flat], dtype))
    ap = h[:, 0:n]
    if len(shape) == 3:
        ap = ap.rearrange("p (a b) -> p a b", a=shape[1])
    return ap, Buf(name)


def emit_ln(p, src_ap, src_bufs, out_ap, out_buf, sq, sqb, st, stb):
    p.op("ve", lambda v: v.reduce_sum(st[:, 0:1], src_ap, axis=AX.X), r=src_bufs, w=[stb])
    p.op("se", lambda s: s.activation(sq[:], src_ap, AF.Square), r=src_bufs, w=[sqb])
    p.op("ve", lambda v: v.reduce_sum(st[:, 1:2], sq[:], axis=AX.X), r=[sqb], w=[stb])
    p.op("ve", lambda v: v.tensor_scalar_mul(st[:, 2:3], st[:, 0:1], 1.0 / D), r=[stb], w=[stb])
    p.op("ve", lambda v: v.tensor_tensor(st[:, 3:4], st[:, 2:3], st[:, 2:3], ALU.mult), r=[stb], w=[stb])
    p.op("ve", lambda v: v.tensor_scalar_mul(st[:, 4:5], st[:, 1:2], 1.0 / D), r=[stb], w=[stb])
    p.op("ve", lambda v: v.tensor_tensor(st[:, 5:6], st[:, 4:5], st[:, 3:4], ALU.subtract), r=[stb], w=[stb])
    p.op("se", lambda s: s.activation(st[:, 6:7], st[:, 5:6], AF.Ln, bias=LN_EPS), r=[stb], w=[stb])
    p.op("se", lambda s: s.activation(st[:, 7:8], st[:, 6:7], AF.Exp, scale=-0.5), r=[stb], w=[stb])
    p.op("ve", lambda v: v.tensor_scalar(out_ap, src_ap, st[:, 2:3], st[:, 7:8],
                                         ALU.subtract, ALU.mult),
         r=list(src_bufs) + [stb], w=[out_buf])


def build_nc(dump=()):
    nc = bass.Bass("TRN2", target_bir_lowering=False)

    x_own = nc.dram_tensor("x_own", [TOK, D], F32, kind="ExternalInput").ap()
    x_full = nc.dram_tensor("x_full", [T, D], F32, kind="ExternalInput").ap()
    w_qkvT = nc.dram_tensor("w_qkvT", [D, 1544], F32, kind="ExternalInput").ap()
    wi = nc.dram_tensor("wi", [8, 128, 8, 128], F32, kind="ExternalInput").ap()
    wf = nc.dram_tensor("wf", [8, 128, 8, 128], F32, kind="ExternalInput").ap()
    wg = nc.dram_tensor("wg", [8, 128, 8, 128], F32, kind="ExternalInput").ap()
    wo = nc.dram_tensor("wo", [8, 128, 8, 128], F32, kind="ExternalInput").ap()
    conv_i = nc.dram_tensor("conv_i", [128, 8, 4], F32, kind="ExternalInput").ap()
    conv_f = nc.dram_tensor("conv_f", [128, 8, 4], F32, kind="ExternalInput").ap()
    gnw = nc.dram_tensor("gnw", [128, 8], F32, kind="ExternalInput").ap()
    gwt = nc.dram_tensor("gwt", [128, 8, 8], F32, kind="ExternalInput").ap()
    ebias = nc.dram_tensor("ebias", [1, 8], F32, kind="ExternalInput").ap()
    w1 = nc.dram_tensor("w1", [NE, HC, 128, D], F32, kind="ExternalInput").ap()
    w2 = nc.dram_tensor("w2", [NE, 2, 128, 11, D], F32, kind="ExternalInput").ap()
    w3 = nc.dram_tensor("w3", [NE, HC, 128, D], F32, kind="ExternalInput").ap()
    mA = nc.dram_tensor("mA", [128, 1], F32, kind="ExternalInput").ap()
    mB = nc.dram_tensor("mB", [128, 1], F32, kind="ExternalInput").ap()
    y_out = nc.dram_tensor("y_out", [TOK, D], F32, kind="ExternalOutput").ap()
    dumps = {}
    for nm in dump:
        shp = {"res1": [TOK, D], "res2": [TOK, D], "res3": [TOK, D],
               "sco": [128, NB, 8], "lgg": [8, T], "yag": [8192, 512]}[nm]
        dt = BF16 if nm == "yag" else F32
        dumps[nm] = nc.dram_tensor("d_" + nm, shp, dt, kind="ExternalOutput").ap()

    y_pub = nc.dram_tensor("y_pub", [T, 512], BF16)
    y_ag = nc.dram_tensor("y_ag", [2 * T, 512], BF16)
    b_pub = nc.dram_tensor("b_pub", [128, 24], BF16)
    b_ag = nc.dram_tensor("b_ag", [256, 24], BF16)
    c_pub = nc.dram_tensor("c_pub", [128, 8], F32)
    c_ag = nc.dram_tensor("c_ag", [256, 8], F32)
    w13b = nc.dram_tensor("w13b", [NE, 2, 128, 2, 11, D], BF16)
    w2b = nc.dram_tensor("w2b", [NE, 2, 128, 11, D], BF16)
    xl_d = nc.dram_tensor("xl_d", [TOK, D], BF16)
    xsl_d = nc.dram_tensor("xsl_d", [5120, D], BF16)
    osl_d = nc.dram_tensor("osl_d", [5120, D], BF16)
    yb = {n: Buf(n) for n in ("y_pub", "y_ag", "b_pub", "b_ag", "c_pub", "c_ag",
                              "w13b", "w2b", "xl_d", "xsl_d", "osl_d")}

    es = ExitStack()
    with es:
        p = Prog(nc, es)

        identf, identf_b = _sb(es, nc, "identf", [128, 128], F32)
        identb, identb_b = _sb(es, nc, "identb", [128, 128], BF16)
        onesf, onesf_b = _sb(es, nc, "onesf", [1, 128], F32)
        onescb, onescb_b = _sb(es, nc, "onescb", [128, 1], BF16)
        iota8, iota8_b = _sb(es, nc, "iota8", [128, 8], F32)
        iotaB8, iotaB8_b = _sb(es, nc, "iotaB8", [128, 8], F32)
        ebc, ebc_b = _sb(es, nc, "ebc", [128, 8], F32)
        ebrow, ebrow_b = _sb(es, nc, "ebrow", [1, 8], F32)
        cwi, cwi_b = _sb(es, nc, "cwi", [128, 8, 4], F32)
        cwf, cwf_b = _sb(es, nc, "cwf", [128, 8, 4], F32)
        gnws, gnws_b = _sb(es, nc, "gnws", [128, 8], F32)
        gwts, gwts_b = _sb(es, nc, "gwts", [128, 8, 8], BF16)
        mAs, mAs_b = _sb(es, nc, "mAs", [128, 1], F32)
        mBs, mBs_b = _sb(es, nc, "mBs", [128, 1], F32)
        iosc, iosc_b = _sb(es, nc, "iosc", [128, 128], F32)
        trix, trix_b = _sb(es, nc, "trix", [128, 128], BF16)
        iotac, iotac_b = _sb(es, nc, "iotac", [128, 1], F32)
        iota40, iota40_b = _sb(es, nc, "iota40", [8, 40], F32)
        iotok, iotok_b = _sb(es, nc, "iotok", [128, 16], F32)
        os18, os18_b = _sb(es, nc, "os18", [1, 8], F32)
        wst = [_sb(es, nc, "wst%d" % i, [128, D], BF16) for i in range(4)]
        cst, cst_b = _sb(es, nc, "cst", [128, 4], F32)
        for i, val in enumerate((LN_EPS, EPS_ATT, 1.0, RMS_EPS)):
            p.op("ve", lambda v, i=i, val=val: v.memset(cst[:, i:i + 1], val), w=[cst_b])
            nc.const_aps.register(F32, val, cst[:, i:i + 1])

        p.op("ge", lambda g: g.iota(iosc[:], pattern=[[1, 128]], base=0,
                                    channel_multiplier=-1,
                                    allow_small_or_imprecise_dtypes=True), w=[iosc_b])
        p.op("ve", lambda v: v.tensor_single_scalar(identf[:], iosc[:], 0.0, ALU.is_equal),
             r=[iosc_b], w=[identf_b])
        p.op("ve", lambda v: v.tensor_copy(identb[:], identf[:]), r=[identf_b], w=[identb_b])
        p.op("ve", lambda v: v.memset(onesf[:], 1.0), w=[onesf_b])
        p.op("ve", lambda v: v.memset(onescb[:], 1.0), w=[onescb_b])
        p.op("ge", lambda g: g.iota(iota8[:], pattern=[[1, 8]], base=0,
                                    channel_multiplier=0,
                                    allow_small_or_imprecise_dtypes=True), w=[iota8_b])
        p.op("ve", lambda v: v.tensor_single_scalar(trix[:], iosc[:], 0.0, ALU.is_gt),
             r=[iosc_b], w=[trix_b])
        p.op("ge", lambda g: g.iota(iotac[:], pattern=[[1, 1]], base=0,
                                    channel_multiplier=1,
                                    allow_small_or_imprecise_dtypes=True), w=[iotac_b])
        p.op("ge", lambda g: g.iota(iota40[:], pattern=[[128, 40]], base=0,
                                    channel_multiplier=0,
                                    allow_small_or_imprecise_dtypes=True), w=[iota40_b])
        p.op("ge", lambda g: g.iota(iotok[:], pattern=[[128, 16]], base=0,
                                    channel_multiplier=1,
                                    allow_small_or_imprecise_dtypes=True), w=[iotok_b])
        p.op("ve", lambda v: v.memset(os18[:], 1.0), w=[os18_b])
        p.op("ve", lambda v: v.tensor_single_scalar(iotaB8[:], iota8[:], -BIG, ALU.add),
             r=[iota8_b], w=[iotaB8_b])
        p.dma("qs", cwi[:], conv_i, w=[cwi_b])
        p.dma("qs", cwf[:], conv_f, w=[cwf_b])
        p.dma("qs", gnws[:], gnw, w=[gnws_b])
        p.dma("qs", mAs[:], mA, w=[mAs_b])
        p.dma("qs", mBs[:], mB, w=[mBs_b])
        p.dma("qs", ebrow[:], ebias, w=[ebrow_b])
        p.dma("ge", gwts[:], gwt, w=[gwts_b])
        p.flush("qs")
        p.flush("ge")

        with nc.psum_tensor("ebp", [128, 512], F32) as ebp:
            ebp_b = Buf("ebp")
            p.op("te", lambda t: t.matmul(ebp[:, 0:8], onesf[:], ebrow[:], start=True, stop=True),
                 r=[onesf_b, ebrow_b], w=[ebp_b])
            p.op("ve", lambda v: v.tensor_copy(ebc[:], ebp[:, 0:8]), r=[ebp_b], w=[ebc_b])
        p.barrier()

        wc_steps = []

        def _wc_pair(srcs):
            def go():
                for i, (sap, dap) in enumerate(srcs):
                    sbh, sbb = wst[i]
                    p.dma("ge", sbh[:], sap, w=[sbb])
                p.flush("ge")
                for i, (sap, dap) in enumerate(srcs):
                    sbh, sbb = wst[i]
                    p.dma("ge", dap, sbh[:], r=[sbb], w=[yb["w13b"], yb["w2b"]])
                p.flush("ge")
            return go

        _grp = []
        for e in range(NE):
            for hc in range(HC):
                half, hl = divmod(hc, 11)
                _grp.append((w1[e, hc], w13b.ap()[e, half, :, 0, hl, :]))
                _grp.append((w3[e, hc], w13b.ap()[e, half, :, 1, hl, :]))
                _grp.append((w2[e, half, :, hl, :], w2b.ap()[e, half, :, hl, :]))
                if len(_grp) >= 4:
                    wc_steps.append(_wc_pair(_grp[:4]))
                    _grp = _grp[4:]
        if _grp:
            wc_steps.append(_wc_pair(_grp))
        wc_pos = [0]

        def wcast_emit(k):
            n = min(k, len(wc_steps) - wc_pos[0])
            for _ in range(n):
                wc_steps[wc_pos[0]]()
                wc_pos[0] += 1

        with ExitStack() as s1:
            s1.enter_context(nc.named_scope("p1_attn"))
            XT, XT_b = _sb(s1, nc, "XT", [128, 8, T], BF16)
            wall, wall_b = _sb(s1, nc, "wall", [128, 8, 1544], BF16)
            lgG8, lgG8_b = _sb(s1, nc, "lgG8", [8, T], F32)
            lgGn, lgGn_b = _sb(s1, nc, "lgGn", [128, 32, 8], F32)

            p.dma("ge", wall[:], w_qkvT.rearrange("(dc p) o -> p dc o", p=128), w=[wall_b])
            p.flush("ge")
            wcast_emit(2)

            with ExitStack() as sA:
                xs = [_sb(sA, nc, "p1xs%d" % i, [128, D], F32) for i in range(2)]
                xln = [_sb(sA, nc, "p1xl%d" % i, [128, D], BF16) for i in range(2)]
                sq, sq_b = _sb(sA, nc, "p1sq", [128, D], F32)
                st, st_b = _sb(sA, nc, "p1st", [128, 8], F32)
                g8, g8_b = _sb(sA, nc, "p1g8", [8, T], F32)
                gs1, gs1_b = _sb(sA, nc, "p1gs1", [8, 512], F32)
                gs2, gs2_b = _sb(sA, nc, "p1gs2", [8, 512], F32)
                psb, psb_b = _ps(sA, nc, "p1psb", [128, 128], BF16)
                gp, gp_b = _ps(sA, nc, "p1gp", [8, 512], F32)
                tpp, tpp_b = _ps(sA, nc, "p1tpp", [128, 8], F32)

                for blk in range(NBF):
                    pb = blk % 2
                    xsh, xsb = xs[pb]
                    xlh, xlb = xln[pb]
                    p.dma("qs", xsh[:], x_full[blk * 128:(blk + 1) * 128, :], w=[xsb])
                    p.flush("qs")
                    emit_ln(p, xsh[:], [xsb], xlh[:], xlb, sq, sq_b, st, st_b)
                    for dc in range(8):
                        p.op("te", lambda t, dc=dc, xlh=xlh: t.transpose(
                            psb[:], xlh[:, dc * 128:(dc + 1) * 128], identb[:]),
                            r=[xlb, identb_b], w=[psb_b])
                        p.op("se", lambda s, dc=dc, blk=blk: s.copy(
                            XT[:, dc, blk * 128:(blk + 1) * 128], psb[:]),
                            r=[psb_b], w=[XT_b])
                    if blk % 4 == 3:
                        wcast_emit(1)

                for s8 in range(8):
                    for dc in range(8):
                        p.op("te", lambda t, dc=dc, s8=s8: t.matmul(
                            gp[:], wall[:, dc, 1536:1544], XT[:, dc, s8 * 512:(s8 + 1) * 512],
                            start=(dc == 0), stop=(dc == 7)),
                            r=[wall_b, XT_b], w=[gp_b])
                    p.op("se", lambda s, s8=s8: s.copy(g8[:, s8 * 512:(s8 + 1) * 512], gp[:]),
                         r=[gp_b], w=[g8_b])
                for s8 in range(8):
                    sl = slice(s8 * 512, (s8 + 1) * 512)
                    p.op("se", lambda s, sl=sl: s.activation(
                        gs1[:], g8[:, sl], AF.Abs), r=[g8_b], w=[gs1_b])
                    p.op("se", lambda s: s.activation(gs1[:], gs1[:], AF.Exp, scale=-1.0),
                         r=[gs1_b], w=[gs1_b])
                    p.op("se", lambda s: s.activation(gs1[:], gs1[:], AF.Ln, bias=1.0),
                         r=[gs1_b], w=[gs1_b])
                    p.op("ve", lambda v, sl=sl: v.tensor_single_scalar(
                        gs2[:], g8[:, sl], 0.0, ALU.min), r=[g8_b], w=[gs2_b])
                    p.op("ve", lambda v, sl=sl: v.tensor_tensor(
                        g8[:, sl], gs2[:], gs1[:], ALU.subtract),
                        r=[gs2_b, gs1_b], w=[g8_b])
                p.op("ve", lambda v: v.memset(lgG8[:], 1.0), w=[lgG8_b])
                p.op("ve", lambda v: v.tensor_tensor_scan(
                    lgG8[:], lgG8[:], g8[:], 0.0, ALU.mult, ALU.add),
                    r=[g8_b], w=[lgG8_b])
                if "lgg" in dumps:
                    p.dma("qs", dumps["lgg"], lgG8[:], r=[lgG8_b])
                    p.flush("qs")
                for j in range(32):
                    p.op("te", lambda t, j=j: t.transpose(
                        tpp[:], lgG8[:, j * 128:(j + 1) * 128], identf[0:8, 0:8]),
                        r=[lgG8_b, identf_b], w=[tpp_b])
                    p.op("se", lambda s, j=j: s.activation(
                        lgGn[:, j, :], tpp[:], AF.Copy, scale=-1.0),
                        r=[tpp_b], w=[lgGn_b])
                wcast_emit(2)
            p.barrier()

            with ExitStack() as sB:
                KTt = es_kt = sB.enter_context(nc.sbuf_tensor("KTt", [128, T], BF16))
                QTt = sB.enter_context(nc.sbuf_tensor("QTt", [128, T], BF16))
                KT = [(KTt[0:64, :], Buf("KT0")), (KTt[64:128, :], Buf("KT1"))]
                QT = [(QTt[0:64, :], Buf("QT0")), (QTt[64:128, :], Buf("QT1"))]
                Vsb = [_sb(sB, nc, "Vsb%d" % i, [128, 32, 65], BF16) for i in range(2)]
                VTs, VTs_b = _sb(sB, nc, "VTs", [64, T], BF16)
                y2, y2_b = _sb(sB, nc, "y2", [128, 32, 128], BF16)
                As2 = [_sb(sB, nc, "As%d" % i, [128, 2, 512], F32) for i in range(2)]
                Ls2 = [_sb(sB, nc, "Ls%d" % i, [128, 2, 512], F32) for i in range(2)]
                Ps2 = [_sb(sB, nc, "Ps%d" % i, [128, 2, 512], BF16) for i in range(2)]
                Rsb2 = [_sb(sB, nc, "Rsb%d" % i, [128, 512], F32) for i in range(2)]
                rr, rr_b = _sb(sB, nc, "rr", [1, 512], F32)
                rr02 = [_sb(sB, nc, "rr0%d" % i, [1, 512], F32) for i in range(2)]
                msk4, msk4_b = _sb(sB, nc, "msk4", [128, 4, 512], BF16)
                mtmp, mtmp_b = _sb(sB, nc, "mtmp", [128, 512], F32)
                Oe, Oe_b = _sb(sB, nc, "Oe", [65, 512], F32)
                rcp, rcp_b = _sb(sB, nc, "rcp", [128, 1], F32)
                S2 = [_ps(sB, nc, "S_ps%d" % i, [128, 2, 512], F32) for i in range(2)]
                O_ps, O_ps_b = _ps(sB, nc, "O_ps", [65, 512], F32)
                mp_ps, mp_ps_b = _ps(sB, nc, "mp_ps", [128, 512], F32)
                ftr_ps, ftr_ps_b = _ps(sB, nc, "ftr_ps", [128, 65], F32)
                vtr_ps, vtr_ps_b = _ps(sB, nc, "vtr_ps", [128, 64], BF16)

                for k in range(4):
                    p.op("ge", lambda g, k=k: g.iota(
                        mtmp[:], pattern=[[1, 512]], base=-128 * k, channel_multiplier=-1,
                        allow_small_or_imprecise_dtypes=True), w=[mtmp_b])
                    p.op("ve", lambda v, k=k: v.tensor_single_scalar(
                        msk4[:, k, :], mtmp[:], 0.0, ALU.is_lt), r=[mtmp_b], w=[msk4_b])
                p.op("ve", lambda v: v.tensor_scalar_mul(msk4[:], msk4[:], NEGBIG),
                     r=[msk4_b], w=[msk4_b])
                for i in range(2):
                    p.op("ve", lambda v, i=i: v.memset(Vsb[i][0][:, :, 64:65], 1.0),
                         w=[Vsb[i][1]])

                def setup_pieces(h):
                    hb = h % 2
                    kth, ktb = KT[hb]
                    qth, qtb = QT[hb]
                    vh, vb2 = Vsb[hb]
                    pieces = []
                    for s8 in range(8):
                        def go(s8=s8, kth=kth, ktb=ktb, qth=qth, qtb=qtb, vh=vh, vb2=vb2, h=h):
                            for (dst, dstb, off) in ((kth, ktb, 512), (qth, qtb, 0),
                                                     (VTs, VTs_b, 1024)):
                                for dc in range(8):
                                    p.op("te", lambda t, dc=dc, s8=s8, off=off, h=h: t.matmul(
                                        mp_ps[0:64, :],
                                        wall[:, dc, off + h * 64: off + (h + 1) * 64],
                                        XT[:, dc, s8 * 512:(s8 + 1) * 512],
                                        start=(dc == 0), stop=(dc == 7)),
                                        r=[wall_b, XT_b], w=[mp_ps_b])
                                p.op("ve", lambda v, s8=s8, dst=dst: v.tensor_copy(
                                    dst[:, s8 * 512:(s8 + 1) * 512], mp_ps[0:64, :]),
                                    r=[mp_ps_b], w=[dstb])
                            for j in range(4):
                                jj = s8 * 4 + j
                                p.op("te", lambda t, jj=jj: t.transpose(
                                    vtr_ps[:], VTs[:, jj * 128:(jj + 1) * 128],
                                    identb[0:64, 0:64]),
                                    r=[VTs_b, identb_b], w=[vtr_ps_b])
                                p.op("se", lambda s, jj=jj, vh=vh: s.copy(
                                    vh[:, jj, 0:64], vtr_ps[:]),
                                    r=[vtr_ps_b], w=[vb2])
                        pieces.append(go)
                    return pieces

                def attend(h, next_pieces):
                    hb = h % 2
                    kth, ktb = KT[hb]
                    qth, qtb = QT[hb]
                    vh, vb2 = Vsb[hb]
                    pend_fin = [None]

                    def emit_qk(s8, g0, gn):
                        gp = (g0 // 2) % 2
                        Sp, Sp_b = S2[gp]
                        for k in range(gn):
                            c = g0 + k
                            p.op("te", lambda t, k=k, c=c, s8=s8, kth=kth, qth=qth, Sp=Sp: t.matmul(
                                Sp[:, k, :], kth[:, c * 128:(c + 1) * 128],
                                qth[:, s8 * 512:(s8 + 1) * 512], start=True, stop=True),
                                r=[ktb, qtb], w=[Sp_b])

                    def emit_absln(s8, g0, gn):
                        gp = (g0 // 2) % 2
                        Sp, Sp_b = S2[gp]
                        Ah, Ah_b = As2[gp]
                        Lh, Lh_b = Ls2[gp]
                        p.op("se", lambda s, gn=gn, Ah=Ah, Sp=Sp: s.activation(
                            Ah[:, 0:gn, :], Sp[:, 0:gn, :], AF.Abs),
                            r=[Sp_b], w=[Ah_b])
                        p.op("se", lambda s, gn=gn, Ah=Ah, Lh=Lh: s.activation(
                            Lh[:, 0:gn, :], Ah[:, 0:gn, :], AF.Ln, bias=EPS_ATT),
                            r=[Ah_b], w=[Lh_b])

                    def emit_elem_pv(s8, g0, gn, nch):
                        gp = (g0 // 2) % 2
                        Sp, Sp_b = S2[gp]
                        Ah, Ah_b = As2[gp]
                        Lh, Lh_b = Ls2[gp]
                        Ph, Ph_b = Ps2[gp]
                        Rsb, Rsb_b = Rsb2[s8 % 2]
                        for k in range(gn):
                            c = g0 + k
                            p.op("ve", lambda v, k=k, Ah=Ah, Lh=Lh, Rsb=Rsb: v.tensor_tensor(
                                Ah[:, k, :], Lh[:, k, :], Rsb[:], ALU.add),
                                r=[Lh_b, Rsb_b], w=[Ah_b])
                            if c >= 4 * s8:
                                p.op("ve", lambda v, k=k, c=c, s8=s8, Ah=Ah: v.tensor_tensor(
                                    Ah[:, k, :], Ah[:, k, :], msk4[:, c - 4 * s8, :],
                                    ALU.add),
                                    r=[msk4_b], w=[Ah_b])
                        for k in range(gn):
                            c = g0 + k
                            p.op("se", lambda s, k=k, c=c, h=h, Ah=Ah, Ph=Ph: s.activation(
                                Ph[:, k, :], Ah[:, k, :], AF.Exp, scale=2.0,
                                bias=lgGn[:, c, h:h + 1]),
                                r=[Ah_b, lgGn_b], w=[Ph_b])
                        for k in range(gn):
                            c = g0 + k
                            p.op("te", lambda t, k=k, c=c, vh=vh, Ph=Ph: t.matmul(
                                O_ps[:], vh[:, c, :], Ph[:, k, :],
                                start=(c == 0), stop=(c == nch - 1)),
                                r=[vb2, Ph_b], w=[O_ps_b])

                    def make_fin(s8, h):
                        def fin():
                            p.op("se", lambda s: s.copy(Oe[:], O_ps[:]), r=[O_ps_b], w=[Oe_b])
                            for j in range(4):
                                p.op("te", lambda t, j=j: t.transpose(
                                    ftr_ps[:], Oe[:, j * 128:(j + 1) * 128],
                                    identf[0:65, 0:65]),
                                    r=[Oe_b, identf_b], w=[ftr_ps_b])
                                p.op("ve", lambda v: v.reciprocal(rcp[:], ftr_ps[:, 64:65]),
                                     r=[ftr_ps_b], w=[rcp_b])
                                p.op("ve", lambda v, j=j, s8=s8, h=h: v.tensor_scalar_mul(
                                    y2[:, s8 * 4 + j, (h % 2) * 64:(h % 2) * 64 + 64],
                                    ftr_ps[:, 0:64], rcp[:]),
                                    r=[ftr_ps_b, rcp_b], w=[y2_b])
                        return fin

                    rh0, rb0 = rr02[0]
                    p.dma("qs", rh0[:], lgG8[h:h + 1, 0:512], r=[lgG8_b], w=[rb0])
                    for s8 in range(8):
                        rpar = s8 % 2
                        rr0h, rr0b = rr02[rpar]
                        Rsb, Rsb_b = Rsb2[rpar]
                        nch = 4 * (s8 + 1)
                        emit_qk(s8, 0, 2)
                        if pend_fin[0] is not None:
                            pend_fin[0]()
                            pend_fin[0] = None
                        p.flush("qs")
                        if s8 < 7:
                            nh_, nb_ = rr02[(s8 + 1) % 2]
                            p.dma("qs", nh_[:], lgG8[h:h + 1, (s8 + 1) * 512:(s8 + 2) * 512],
                                  r=[lgG8_b], w=[nb_])
                        p.op("ve", lambda v, rr0h=rr0h: v.tensor_scalar(
                            rr[:], rr0h[:], 0.5, LOGSCALE / 2.0, ALU.mult, ALU.add),
                            r=[rr0b], w=[rr_b])
                        p.op("te", lambda t: t.matmul(mp_ps[:], onesf[:], rr[:],
                                                      start=True, stop=True),
                             r=[onesf_b, rr_b], w=[mp_ps_b])
                        p.op("ve", lambda v, Rsb=Rsb: v.tensor_copy(Rsb[:], mp_ps[:]),
                             r=[mp_ps_b], w=[Rsb_b])
                        groups = list(range(0, nch, 2))
                        ng = len(groups)

                        def _gn(gi):
                            return min(2, nch - groups[gi])

                        if ng > 1:
                            emit_qk(s8, groups[1], _gn(1))
                        emit_absln(s8, 0, 2)
                        for gi, g0 in enumerate(groups):
                            gn = min(2, nch - g0)
                            if gi + 2 < ng:
                                emit_qk(s8, groups[gi + 2], _gn(gi + 2))
                            if gi + 1 < ng:
                                emit_absln(s8, groups[gi + 1], _gn(gi + 1))
                            elif s8 < 4 and next_pieces:
                                next_pieces.pop(0)()
                            emit_elem_pv(s8, g0, gn, nch)
                        if s8 >= 4 and next_pieces:
                            next_pieces.pop(0)()
                        pend_fin[0] = make_fin(s8, h)
                    pend_fin[0]()
                    while next_pieces:
                        next_pieces.pop(0)()
                    if h % 2 == 1:
                        h2 = h // 2
                        for q4 in range(4):
                            p.dma("qs", y_pub.ap().rearrange("(j p) c -> p j c", p=128)
                                  [:, q4 * 8:(q4 + 1) * 8, h2 * 128:(h2 + 1) * 128],
                                  y2[:, q4 * 8:(q4 + 1) * 8, :], r=[y2_b], w=[yb["y_pub"]])
                        p.flush("qs")

                for piece in setup_pieces(0):
                    piece()
                for h in range(NHL):
                    wcast_emit(17)
                    np_ = setup_pieces(h + 1) if h + 1 < NHL else []
                    attend(h, np_)
                wcast_emit(len(wc_steps))
                p.cc("AllGather", ALU.bypass, y_pub.ap().opt(), y_ag.ap().opt(),
                     r=[yb["y_pub"]], w=[yb["y_ag"]])
                if "yag" in dumps:
                    p.dma("qs", dumps["yag"], y_ag.ap(), r=[yb["y_ag"]])
                    p.flush("qs")
            p.barrier()

        with ExitStack() as sR:
            res, res_b = _sb(sR, nc, "res", [128, NB, D], F32)

            with ExitStack() as sRB:
                sRB.enter_context(nc.named_scope("res_build"))
                xs0, xs0_b = _sb(sRB, nc, "rbx", [128, D], F32)
                ya, ya_b = _sb(sRB, nc, "rbya", [128, 512], BF16)
                yB2, yB2_b = _sb(sRB, nc, "rbyb", [128, 512], BF16)
                for j in range(NB):
                    p.dma("qs", xs0[:], x_own[j * 128:(j + 1) * 128, :], w=[xs0_b])
                    p.flush("qs")
                    p.op("ve", lambda v, j=j: v.tensor_copy(res[:, j, :], xs0[:]),
                         r=[xs0_b], w=[res_b])
                    for slot in range(2):
                        p.dma("qs", ya[:], y_ag.ap()[slot * T + j * 128: slot * T + (j + 1) * 128, :],
                              r=[yb["y_ag"]], w=[ya_b])
                        p.dma("qs", yB2[:], y_ag.ap()[slot * T + TOK + j * 128: slot * T + TOK + (j + 1) * 128, :],
                              r=[yb["y_ag"]], w=[yB2_b])
                        p.flush("qs")
                        sl = slice(slot * 512, (slot + 1) * 512)
                        p.op("ve", lambda v, j=j, sl=sl: v.scalar_tensor_tensor(
                            res[:, j, sl], ya[:], mAs[:], res[:, j, sl], ALU.mult, ALU.add),
                            r=[ya_b, mAs_b], w=[res_b])
                        p.op("ve", lambda v, j=j, sl=sl: v.scalar_tensor_tensor(
                            res[:, j, sl], yB2[:], mBs[:], res[:, j, sl], ALU.mult, ALU.add),
                            r=[yB2_b, mBs_b], w=[res_b])
            if "res1" in dumps:
                p.dma("qs", dumps["res1"].rearrange("(j p) d -> p j d", p=128), res[:],
                      r=[res_b])
                p.flush("qs")
            p.barrier()

            def emit_ffn(tag, dump_sco=False):
                with ExitStack() as sF:
                    sF.enter_context(nc.named_scope(tag + "_ffn"))
                    sctm, sctm_b = _sb(sF, nc, tag + "sctm", [128, NB, 8], F32)
                    t8a, t8a_b = _sb(sF, nc, tag + "t8a", [128, 8], F32)
                    t8b, t8b_b = _sb(sF, nc, tag + "t8b", [128, 8], F32)
                    t8c, t8c_b = _sb(sF, nc, tag + "t8c", [128, 8], F32)
                    t1c, t1c_b = _sb(sF, nc, tag + "t1c", [128, 1], F32)
                    mkb, mkb_b = _sb(sF, nc, tag + "mkb", [128, NB, 8], BF16)
                    m1t, m1t_b = _sb(sF, nc, tag + "m1t", [128, NB, 8], F32)
                    m2t, m2t_b = _sb(sF, nc, tag + "m2t", [128, NB, 8], F32)
                    sr1, sr1_b = _sb(sF, nc, tag + "sr1", [128, NB], F32)
                    sr2, sr2_b = _sb(sF, nc, tag + "sr2", [128, NB], F32)
                    cntA, cntA_b = _sb(sF, nc, tag + "cntA", [1, 128], F32)
                    r8 = {}
                    for nm in ("cnt", "al", "incl", "starts", "run", "sbrow", "tmp8"):
                        r8[nm] = _sb(sF, nc, tag + nm, [1, 8], F32)
                    blk2e, blk2e_b = _sb(sF, nc, tag + "blk2e", [1, 40], F32)
                    widx = [_sb(sF, nc, tag + "widx%d" % i, [128, 40], I32) for i in range(2)]
                    slotf, slotf_b = _sb(sF, nc, tag + "slotf", [128, 8], F32)
                    sfa, sfa_b = _sb(sF, nc, tag + "sfa", [128, 8], F32)
                    sfb, sfb_b = _sb(sF, nc, tag + "sfb", [128, 8], F32)

                    inclT, inclT_b = _sb(sF, nc, tag + "inclT", [8, 1], F32)
                    cmpb, cmpb_b = _sb(sF, nc, tag + "cmpb", [8, 40], BF16)
                    with ExitStack() as sFP:
                        XTf, XTf_b = _sb(sFP, nc, tag + "XTf", [128, 8, TOK], BF16)
                        xlf, xlf_b = _sb(sFP, nc, tag + "xlf", [128, D], BF16)
                        sqf, sqf_b = _sb(sFP, nc, tag + "sqf", [128, D], F32)
                        stf, stf_b = _sb(sFP, nc, tag + "stf", [128, 8], F32)
                        lgt, lgt_b = _sb(sFP, nc, tag + "lgt", [8, 512], F32)
                        ltm, ltm_b = _sb(sFP, nc, tag + "ltm", [128, NB, 8], F32)
                        psbF, psbF_b = _ps(sFP, nc, tag + "psbF", [128, 128], BF16)
                        lgp, lgp_b = _ps(sFP, nc, tag + "lgp", [8, 512], F32)
                        tpf, tpf_b = _ps(sFP, nc, tag + "tpf", [128, 8], F32)

                        for j in range(NB):
                            emit_ln(p, res[:, j, :], [res_b], xlf[:], xlf_b,
                                    sqf, sqf_b, stf, stf_b)
                            p.dma("qs", xl_d.ap()[j * 128:(j + 1) * 128, :], xlf[:],
                                  r=[xlf_b], w=[yb["xl_d"]])
                            p.flush("qs")
                            for dc in range(8):
                                p.op("te", lambda t, dc=dc: t.transpose(
                                    psbF[:], xlf[:, dc * 128:(dc + 1) * 128], identb[:]),
                                    r=[xlf_b, identb_b], w=[psbF_b])
                                p.op("se", lambda s, dc=dc, j=j: s.copy(
                                    XTf[:, dc, j * 128:(j + 1) * 128], psbF[:]),
                                    r=[psbF_b], w=[XTf_b])
                        for s4 in range(4):
                            for dc in range(8):
                                p.op("te", lambda t, dc=dc, s4=s4: t.matmul(
                                    lgp[:], gwts[:, dc, :], XTf[:, dc, s4 * 512:(s4 + 1) * 512],
                                    start=(dc == 0), stop=(dc == 7)),
                                    r=[gwts_b, XTf_b], w=[lgp_b])
                            p.op("se", lambda s: s.copy(lgt[:], lgp[:]), r=[lgp_b], w=[lgt_b])
                            for jj in range(4):
                                j = s4 * 4 + jj
                                p.op("te", lambda t, jj=jj: t.transpose(
                                    tpf[:], lgt[:, jj * 128:(jj + 1) * 128], identf[0:8, 0:8]),
                                    r=[lgt_b, identf_b], w=[tpf_b])
                                p.op("ve", lambda v, j=j: v.tensor_copy(ltm[:, j, :], tpf[:]),
                                     r=[tpf_b], w=[ltm_b])
                        for j in range(NB):
                            jj = j
                            p.op("ve", lambda v: v.reduce_max(t1c[:], ltm[:, jj, :], axis=AX.X),
                                 r=[ltm_b], w=[t1c_b])
                            p.op("ve", lambda v: v.tensor_scalar_mul(t1c[:], t1c[:], -1.0),
                                 r=[t1c_b], w=[t1c_b])
                            p.op("se", lambda s: s.activation(t8a[:], ltm[:, jj, :], AF.Exp,
                                                              bias=t1c[:]),
                                 r=[ltm_b, t1c_b], w=[t8a_b])
                            p.op("ve", lambda v: v.reduce_sum(t1c[:], t8a[:], axis=AX.X),
                                 r=[t8a_b], w=[t1c_b])
                            p.op("ve", lambda v: v.reciprocal(t1c[:], t1c[:]),
                                 r=[t1c_b], w=[t1c_b])
                            p.op("ve", lambda v: v.tensor_scalar_mul(sctm[:, jj, :], t8a[:], t1c[:]),
                                 r=[t8a_b, t1c_b], w=[sctm_b])
                            p.op("ve", lambda v: v.tensor_tensor(t8a[:], sctm[:, jj, :], ebc[:],
                                                                 ALU.add),
                                 r=[sctm_b, ebc_b], w=[t8a_b])
                            p.op("ve", lambda v: v.reduce_max(t1c[:], t8a[:], axis=AX.X),
                                 r=[t8a_b], w=[t1c_b])
                            p.op("ve", lambda v: v.tensor_scalar(t8b[:], t8a[:], t1c[:], None,
                                                                 ALU.is_equal),
                                 r=[t8a_b, t1c_b], w=[t8b_b])
                            p.op("ve", lambda v: v.tensor_tensor(t8b[:], t8b[:], iotaB8[:],
                                                                 ALU.mult),
                                 r=[iotaB8_b], w=[t8b_b])
                            p.op("ve", lambda v: v.tensor_single_scalar(t8b[:], t8b[:], BIG,
                                                                        ALU.add),
                                 r=[t8b_b], w=[t8b_b])
                            p.op("ve", lambda v: v.tensor_reduce(t1c[:], t8b[:], axis=AX.X,
                                                                 op=ALU.min),
                                 r=[t8b_b], w=[t1c_b])
                            p.op("ve", lambda v: v.tensor_scalar(t8b[:], iota8[:], t1c[:], None,
                                                                 ALU.is_equal),
                                 r=[iota8_b, t1c_b], w=[t8b_b])
                            p.op("ve", lambda v: v.tensor_copy(m1t[:, jj, :], t8b[:]),
                                 r=[t8b_b], w=[m1t_b])
                            p.op("ve", lambda v: v.scalar_tensor_tensor(
                                t8c[:], t8b[:], -BIG, t8a[:], ALU.mult, ALU.add),
                                r=[t8b_b, t8a_b], w=[t8c_b])
                            p.op("ve", lambda v: v.reduce_max(t1c[:], t8c[:], axis=AX.X),
                                 r=[t8c_b], w=[t1c_b])
                            p.op("ve", lambda v: v.tensor_scalar(t8c[:], t8c[:], t1c[:], None,
                                                                 ALU.is_equal),
                                 r=[t1c_b], w=[t8c_b])
                            p.op("ve", lambda v: v.tensor_tensor(t8c[:], t8c[:], iotaB8[:],
                                                                 ALU.mult),
                                 r=[iotaB8_b], w=[t8c_b])
                            p.op("ve", lambda v: v.tensor_single_scalar(t8c[:], t8c[:], BIG,
                                                                        ALU.add),
                                 r=[t8c_b], w=[t8c_b])
                            p.op("ve", lambda v: v.tensor_reduce(t1c[:], t8c[:], axis=AX.X,
                                                                 op=ALU.min),
                                 r=[t8c_b], w=[t1c_b])
                            p.op("ve", lambda v: v.tensor_scalar(t8c[:], iota8[:], t1c[:], None,
                                                                 ALU.is_equal),
                                 r=[iota8_b, t1c_b], w=[t8c_b])
                            p.op("ve", lambda v: v.tensor_copy(m2t[:, jj, :], t8c[:]),
                                 r=[t8c_b], w=[m2t_b])
                            p.op("ve", lambda v: v.tensor_tensor(t8b[:], t8b[:], t8c[:], ALU.add),
                                 r=[t8c_b], w=[t8b_b])
                            p.op("ve", lambda v: v.tensor_copy(mkb[:, jj, :], t8b[:]),
                                 r=[t8b_b], w=[mkb_b])
                            p.op("ve", lambda v: v.tensor_tensor(sctm[:, jj, :], sctm[:, jj, :],
                                                                 t8b[:], ALU.mult),
                                 r=[t8b_b], w=[sctm_b])
                        if dump_sco and "sco" in dumps:
                            p.dma("qs", dumps["sco"], sctm[:], r=[sctm_b])
                            p.flush("qs")
                    p.barrier()

                    with ExitStack() as sRT:
                        idxs4 = [_sb(sRT, nc, tag + "idxs%d" % i, [128, 1], I32)
                                 for i in range(4)]
                        scs4 = [_sb(sRT, nc, tag + "scs%d" % i, [128, D], BF16)
                                for i in range(4)]
                        xbl, xbl_b = _sb(sRT, nc, tag + "xbl", [128, D], BF16)
                        cntp, cntp_b = _ps(sRT, nc, tag + "cntp", [1, 8], F32)
                        prefp, prefp_b = _ps(sRT, nc, tag + "prefp", [128, 8], F32)
                        sbbcp, sbbcp_b = _ps(sRT, nc, tag + "sbbcp", [128, 8], F32)
                        tixp, tixp_b = _ps(sRT, nc, tag + "tixp", [8, 1], F32)
                        b2ep, b2ep_b = _ps(sRT, nc, tag + "b2ep", [1, 40], F32)
                        e40p, e40p_b = _ps(sRT, nc, tag + "e40p", [128, 40], F32)

                        p.op("ve", lambda v: v.memset(r8["run"][0][:], 0.0), w=[r8["run"][1]])
                        for j in range(NB):
                            p.op("te", lambda t, j=j: t.matmul(
                                cntp[:], onescb[:], mkb[:, j, :], start=True, stop=True),
                                r=[onescb_b, mkb_b], w=[cntp_b])
                            p.op("ve", lambda v, j=j: v.tensor_copy(
                                cntA[0:1, j * 8:(j + 1) * 8], cntp[:]), r=[cntp_b], w=[cntA_b])
                            p.op("ve", lambda v, j=j: v.tensor_tensor(
                                r8["run"][0][:], r8["run"][0][:], cntA[0:1, j * 8:(j + 1) * 8],
                                ALU.add), r=[cntA_b], w=[r8["run"][1]])
                        p.op("ve", lambda v: v.tensor_copy(r8["cnt"][0][:], r8["run"][0][:]),
                             r=[r8["run"][1]], w=[r8["cnt"][1]])
                        p.op("ve", lambda v: v.memset(r8["al"][0][:], 0.0), w=[r8["al"][1]])
                        for m in range(16):
                            p.op("ve", lambda v, m=m: v.tensor_single_scalar(
                                r8["tmp8"][0][:], r8["cnt"][0][:], float(m * 128), ALU.is_gt),
                                r=[r8["cnt"][1]], w=[r8["tmp8"][1]])
                            p.op("ve", lambda v: v.scalar_tensor_tensor(
                                r8["al"][0][:], r8["tmp8"][0][:], 128.0, r8["al"][0][:],
                                ALU.mult, ALU.add), r=[r8["tmp8"][1]], w=[r8["al"][1]])
                        p.op("ve", lambda v: v.tensor_tensor_scan(
                            r8["incl"][0][:], os18[:], r8["al"][0][:], 0.0, ALU.mult, ALU.add),
                            r=[os18_b, r8["al"][1]], w=[r8["incl"][1]])
                        p.op("ve", lambda v: v.tensor_tensor(
                            r8["starts"][0][:], r8["incl"][0][:], r8["al"][0][:], ALU.subtract),
                            r=[r8["incl"][1], r8["al"][1]], w=[r8["starts"][1]])
                        p.op("te", lambda t: t.transpose(tixp[:], r8["incl"][0][:],
                                                         identf[0:1, 0:1]),
                             r=[r8["incl"][1], identf_b], w=[tixp_b])
                        p.op("ve", lambda v: v.tensor_copy(inclT[:], tixp[:]),
                             r=[tixp_b], w=[inclT_b])
                        p.op("ve", lambda v: v.tensor_scalar(
                            cmpb[:], iota40[:], inclT[:], None, ALU.is_ge),
                            r=[iota40_b, inclT_b], w=[cmpb_b])
                        p.op("te", lambda t: t.matmul(b2ep[:], onescb[0:8, :], cmpb[:],
                                                      start=True, stop=True),
                             r=[onescb_b, cmpb_b], w=[b2ep_b])
                        p.op("ve", lambda v: v.tensor_copy(blk2e[:], b2ep[:]),
                             r=[b2ep_b], w=[blk2e_b])
                        p.op("te", lambda t: t.matmul(e40p[:], onesf[:], blk2e[:],
                                                      start=True, stop=True),
                             r=[onesf_b, blk2e_b], w=[e40p_b])
                        for hh in range(2):
                            wih, wib = widx[hh]
                            p.op("ve", lambda v, wih=wih, hh=hh: v.tensor_scalar(
                                wih[:], e40p[:], 256.0, float(hh * 128), ALU.mult, ALU.add),
                                r=[e40p_b], w=[wib])
                            p.op("ve", lambda v, wih=wih: v.tensor_scalar(
                                wih[:], wih[:], iotac[:], None, ALU.add),
                                r=[iotac_b], w=[wib])

                        p.op("ve", lambda v: v.memset(r8["run"][0][:], 0.0), w=[r8["run"][1]])
                        for j in range(NB):
                            p.op("ve", lambda v, j=j: v.tensor_tensor(
                                r8["sbrow"][0][:], r8["starts"][0][:], r8["run"][0][:], ALU.add),
                                r=[r8["starts"][1], r8["run"][1]], w=[r8["sbrow"][1]])
                            p.op("ve", lambda v, j=j: v.tensor_tensor(
                                r8["run"][0][:], r8["run"][0][:], cntA[0:1, j * 8:(j + 1) * 8],
                                ALU.add), r=[cntA_b], w=[r8["run"][1]])
                            p.op("te", lambda t, j=j: t.matmul(
                                prefp[:], trix[:], mkb[:, j, :], start=True, stop=True),
                                r=[trix_b, mkb_b], w=[prefp_b])
                            p.op("te", lambda t: t.matmul(
                                sbbcp[:], onesf[:], r8["sbrow"][0][:], start=True, stop=True),
                                r=[onesf_b, r8["sbrow"][1]], w=[sbbcp_b])
                            p.op("ve", lambda v, j=j: v.tensor_tensor(
                                sfa[:], prefp[:], m1t[:, j, :], ALU.mult),
                                r=[prefp_b, m1t_b], w=[sfa_b])
                            p.op("ve", lambda v, j=j: v.tensor_tensor(
                                sfa[:], prefp[:], mkb[:, j, :], ALU.mult),
                                r=[prefp_b, mkb_b], w=[sfa_b])
                            p.op("ve", lambda v, j=j: v.tensor_tensor(
                                sfb[:], sbbcp[:], mkb[:, j, :], ALU.mult),
                                r=[sbbcp_b, mkb_b], w=[sfb_b])
                            p.op("ve", lambda v: v.tensor_tensor(
                                slotf[:], sfa[:], sfb[:], ALU.add),
                                r=[sfa_b, sfb_b], w=[slotf_b])
                            p.op("ve", lambda v, j=j: v.tensor_scalar(
                                sfa[:], mkb[:, j, :], -BIG, BIG, ALU.mult, ALU.add),
                                r=[mkb_b], w=[sfa_b])
                            p.op("ve", lambda v: v.tensor_tensor(
                                slotf[:], slotf[:], sfa[:], ALU.add),
                                r=[sfa_b], w=[slotf_b])
                            p.op("ve", lambda v, j=j: v.tensor_tensor(
                                sfa[:], slotf[:], m1t[:, j, :], ALU.mult),
                                r=[slotf_b, m1t_b], w=[sfa_b])
                            p.op("ve", lambda v, j=j: v.reduce_sum(
                                sr1[:, j:j + 1], sfa[:], axis=AX.X), r=[sfa_b], w=[sr1_b])
                            p.op("ve", lambda v, j=j: v.tensor_tensor(
                                sfa[:], slotf[:], m2t[:, j, :], ALU.mult),
                                r=[slotf_b, m2t_b], w=[sfa_b])
                            p.op("ve", lambda v, j=j: v.reduce_sum(
                                sr2[:, j:j + 1], sfa[:], axis=AX.X), r=[sfa_b], w=[sr2_b])
                            p.dma("qs", xbl[:], xl_d.ap()[j * 128:(j + 1) * 128, :],
                                  r=[yb["xl_d"]], w=[xbl_b])
                            p.flush("qs")
                            for e in range(NE):
                                sch, scb = scs4[e % 4]
                                ixh, ixb = idxs4[e % 4]
                                p.op("ve", lambda v, j=j, e=e, sch=sch: v.tensor_scalar_mul(
                                    sch[:], xbl[:], sctm[:, j, e:e + 1]),
                                    r=[xbl_b, sctm_b], w=[scb])
                                p.op("ve", lambda v, e=e, ixh=ixh: v.tensor_copy(
                                    ixh[:], slotf[:, e:e + 1]), r=[slotf_b], w=[ixb])
                                p.dma("ge", xsl_d.ap(), sch[:], r=[scb, ixb],
                                      w=[yb["xsl_d"]],
                                      indirect=(ixh, 0))
                                if e % 4 == 3:
                                    p.flush("ge")
                        p.flush("ge")
                    p.barrier()

                    with ExitStack() as sF2:
                        w13u = [_sb(sF2, nc, tag + "w13u%d" % i, [128, 2, 11, D], BF16)
                                for i in range(2)]
                        w2u, w2u_b = _sb(sF2, nc, tag + "w2u", [128, 11, D], BF16)
                        idxw = [_sb(sF2, nc, tag + "idxw%d" % i, [128, 1], I32) for i in range(2)]
                        xg = [_sb(sF2, nc, tag + "xg%d" % i, [128, D], BF16) for i in range(2)]
                        xbt, xbt_b = _sb(sF2, nc, tag + "xbt", [128, 8, 128], BF16)
                        hbl, hbl_b = _sb(sF2, nc, tag + "hbl", [128, 11, 128], BF16)
                        ofm, ofm_b = _sb(sF2, nc, tag + "ofm", [128, 8, 128], F32)
                        otm, otm_b = _sb(sF2, nc, tag + "otm", [128, D], BF16)
                        s1b, s1b_b = _sb(sF2, nc, tag + "s1b", [128, 128], F32)
                        g1b, g1b_b = xg[0]
                        g2b, g2b_b = xg[1]
                        ir1, ir1_b = _sb(sF2, nc, tag + "ir1", [128, 1], I32)
                        ir2, ir2_b = _sb(sF2, nc, tag + "ir2", [128, 1], I32)
                        u1p = [_ps(sF2, nc, tag + "u1p%d" % i, [128, 128], F32) for i in range(2)]
                        u3p = [_ps(sF2, nc, tag + "u3p%d" % i, [128, 128], F32) for i in range(2)]
                        o_p = [_ps(sF2, nc, tag + "o_p%d" % i, [128, 128], F32) for i in range(2)]
                        trxp, trxp_b = _ps(sF2, nc, tag + "trxp", [128, 128], BF16)
                        trop, trop_b = _ps(sF2, nc, tag + "trop", [128, 128], F32)

                        w13view = w13b.ap().rearrange("e h p m l w -> (e h p) (m l w)")
                        w2view = w2b.ap().rearrange("e h p l w -> (e h p) (l w)")

                        idxw2 = [_sb(sF2, nc, tag + "idxw2_%d" % i, [128, 1], I32)
                                 for i in range(2)]

                        def ld_w13(blk, half):
                            up = (blk * 2 + half) % 2
                            wh, whb = w13u[up]
                            iw, iwb = idxw[up]
                            p.op("ve", lambda v, blk=blk, half=half, iw=iw: v.tensor_copy(
                                iw[:], widx[half][0][:, blk:blk + 1]),
                                r=[widx[half][1]], w=[iwb])
                            p.dma("ge", wh[:].rearrange("p a l w -> p (a l w)"), w13view,
                                  r=[iwb], w=[whb], indirect_in=(iw, 0))

                        def ld_w2(blk, half):
                            up = (blk * 2 + half) % 2
                            iw, iwb = idxw2[up]
                            p.op("ve", lambda v, blk=blk, half=half, iw=iw: v.tensor_copy(
                                iw[:], widx[half][0][:, blk:blk + 1]),
                                r=[widx[half][1]], w=[iwb])
                            p.dma("ge", w2u[:].rearrange("p l w -> p (l w)"), w2view,
                                  r=[iwb], w=[w2u_b], indirect_in=(iw, 0))

                        def ld_x(blk):
                            bp = blk % 2
                            xh, xb = xg[bp]
                            p.dma("qs", xh[:], xsl_d.ap()[blk * 128:(blk + 1) * 128, :],
                                  r=[yb["xsl_d"]], w=[xb])
                            p.flush("qs")

                        ld_x(0)
                        ld_w13(0, 0)
                        units = [(b_, h_) for b_ in range(40) for h_ in (0, 1)]
                        for ui, (blk, half) in enumerate(units):
                            bp = blk % 2
                            xh, xb = xg[bp]
                            if True:
                                up = (blk * 2 + half) % 2
                                wh, whb = w13u[up]
                                if ui + 1 < len(units):
                                    ld_w13(*units[ui + 1])
                                ld_w2(blk, half)
                                if half == 0 and blk < 39:
                                    ld_x(blk + 1)
                                p.flush("ge")
                                if half == 0:
                                    for dc in range(8):
                                        p.op("te", lambda t, dc=dc, xh=xh: t.transpose(
                                            trxp[:], xh[:, dc * 128:(dc + 1) * 128], identb[:]),
                                            r=[xb, identb_b], w=[trxp_b])
                                        p.op("se", lambda s, dc=dc: s.copy(xbt[:, dc, :], trxp[:]),
                                             r=[trxp_b], w=[xbt_b])
                                for hl in range(11):
                                    pu = hl % 2
                                    up1, up1_b = u1p[pu]
                                    up3, up3_b = u3p[pu]
                                    for dc in range(8):
                                        p.op("te", lambda t, dc=dc, hl=hl, wh=wh, up1=up1: t.matmul(
                                            up1[:], wh[:, 0, hl, dc * 128:(dc + 1) * 128],
                                            xbt[:, dc, :], start=(dc == 0), stop=(dc == 7)),
                                            r=[whb, xbt_b], w=[up1_b])
                                    for dc in range(8):
                                        p.op("te", lambda t, dc=dc, hl=hl, wh=wh, up3=up3: t.matmul(
                                            up3[:], wh[:, 1, hl, dc * 128:(dc + 1) * 128],
                                            xbt[:, dc, :], start=(dc == 0), stop=(dc == 7)),
                                            r=[whb, xbt_b], w=[up3_b])
                                    p.op("se", lambda s, up1=up1: s.activation(
                                        s1b[:], up1[:], AF.Silu), r=[up1_b], w=[s1b_b])
                                    p.op("ve", lambda v, hl=hl, up3=up3: v.tensor_tensor(
                                        hbl[:, hl, :], up3[:], s1b[:], ALU.mult),
                                        r=[up3_b, s1b_b], w=[hbl_b])
                                for dc in range(8):
                                    po = dc % 2
                                    oph, opb = o_p[po]
                                    for hl in range(11):
                                        p.op("te", lambda t, hl=hl, dc=dc, oph=oph: t.matmul(
                                            oph[:], w2u[:, hl, dc * 128:(dc + 1) * 128],
                                            hbl[:, hl, :], start=(hl == 0), stop=(hl == 10)),
                                            r=[w2u_b, hbl_b], w=[opb])
                                    if half == 0:
                                        p.op("ve", lambda v, dc=dc, oph=oph: v.tensor_copy(
                                            ofm[:, dc, :], oph[:]), r=[opb], w=[ofm_b])
                                    else:
                                        p.op("ve", lambda v, dc=dc, oph=oph: v.tensor_tensor(
                                            ofm[:, dc, :], ofm[:, dc, :], oph[:], ALU.add),
                                            r=[opb], w=[ofm_b])
                            if half == 1:
                                for dc in range(8):
                                    p.op("te", lambda t, dc=dc: t.transpose(
                                        trop[:], ofm[:, dc, :], identf[:]),
                                        r=[ofm_b, identf_b], w=[trop_b])
                                    p.op("ve", lambda v, dc=dc: v.tensor_copy(
                                        otm[:, dc * 128:(dc + 1) * 128], trop[:]),
                                        r=[trop_b], w=[otm_b])
                                p.dma("qs", osl_d.ap()[blk * 128:(blk + 1) * 128, :], otm[:],
                                      r=[otm_b], w=[yb["osl_d"]])
                                p.flush("qs")
                        for j in range(NB):
                            p.op("ve", lambda v, j=j: v.tensor_copy(ir1[:], sr1[:, j:j + 1]),
                                 r=[sr1_b], w=[ir1_b])
                            p.op("ve", lambda v, j=j: v.tensor_copy(ir2[:], sr2[:, j:j + 1]),
                                 r=[sr2_b], w=[ir2_b])
                            p.dma("ge", g1b[:], osl_d.ap(), r=[yb["osl_d"], ir1_b], w=[g1b_b],
                                  indirect_in=(ir1, 0))
                            p.dma("ge", g2b[:], osl_d.ap(), r=[yb["osl_d"], ir2_b], w=[g2b_b],
                                  indirect_in=(ir2, 0))
                            p.flush("ge")
                            p.op("ve", lambda v, j=j: v.tensor_tensor(
                                res[:, j, :], res[:, j, :], g1b[:], ALU.add),
                                r=[g1b_b], w=[res_b])
                            p.op("ve", lambda v, j=j: v.tensor_tensor(
                                res[:, j, :], res[:, j, :], g2b[:], ALU.add),
                                r=[g2b_b], w=[res_b])
                p.barrier()

            emit_ffn("f1", dump_sco=True)
            if "res2" in dumps:
                p.dma("qs", dumps["res2"].rearrange("(j p) d -> p j d", p=128), res[:],
                      r=[res_b])
                p.flush("qs")
                p.barrier()

            with ExitStack() as sH:
                sH.enter_context(nc.named_scope("hgrn"))
                X3T, X3T_b = _sb(sH, nc, "X3T", [128, 8, TOK], BF16)
                ost, ost_b = _sb(sH, nc, "ost", [128, 8, TOK], BF16)
                bnd, bnd_b = _sb(sH, nc, "bnd", [128, 8, 3], BF16)
                hend, hend_b = _sb(sH, nc, "hend", [128, 8], F32)
                carry, carry_b = _sb(sH, nc, "carry", [128, 8], F32)
                cg0, cg0_b = _sb(sH, nc, "cg0", [128, 8], F32)
                bg0, bg0_b = _sb(sH, nc, "bg0", [128, 8, 3], BF16)

                with ExitStack() as sH0:
                    xlh, xlh_b = _sb(sH0, nc, "h_xl", [128, D], BF16)
                    sqh, sqh_b = _sb(sH0, nc, "h_sq", [128, D], F32)
                    sth, sth_b = _sb(sH0, nc, "h_st", [128, 8], F32)
                    psbH, psbH_b = _ps(sH0, nc, "h_psb", [128, 128], BF16)
                    for j in range(NB):
                        emit_ln(p, res[:, j, :], [res_b], xlh[:], xlh_b, sqh, sqh_b, sth, sth_b)
                        for dc in range(8):
                            p.op("te", lambda t, dc=dc: t.transpose(
                                psbH[:], xlh[:, dc * 128:(dc + 1) * 128], identb[:]),
                                r=[xlh_b, identb_b], w=[psbH_b])
                            p.op("se", lambda s, dc=dc, j=j: s.copy(
                                X3T[:, dc, j * 128:(j + 1) * 128], psbH[:]),
                                r=[psbH_b], w=[X3T_b])
                    p.op("ve", lambda v: v.tensor_copy(bnd[:], X3T[:, :, TOK - 3:TOK]),
                         r=[X3T_b], w=[bnd_b])
                    p.dma("qs", b_pub.ap(), bnd[:].rearrange("p h w -> p (h w)"),
                          r=[bnd_b], w=[yb["b_pub"]])
                    p.flush("qs")
                    p.cc("AllGather", ALU.bypass, b_pub.ap().opt(), b_ag.ap().opt(),
                         r=[yb["b_pub"]], w=[yb["b_ag"]])
                    p.dma("qs", bg0[:].rearrange("p h w -> p (h w)"), b_ag.ap()[0:128, :],
                          r=[yb["b_ag"]], w=[bg0_b])
                    p.flush("qs")
                    p.op("ve", lambda v: v.tensor_scalar_mul(bnd[:], bg0[:], mBs[:]),
                         r=[bg0_b, mBs_b], w=[bnd_b])
                p.barrier()

                def hgrn_pass(pass2):
                    with ExitStack() as sP:
                        wic = [_sb(sP, nc, "hwi%d%d" % (pass2, i), [128, 8, 128], BF16)
                               for i in range(2)]
                        wfc = [_sb(sP, nc, "hwf%d%d" % (pass2, i), [128, 8, 128], BF16)
                               for i in range(2)]
                        ipd, ipd_b = _sb(sP, nc, "hip%d" % pass2, [128, TOK + 3], BF16)
                        fpd, fpd_b = _sb(sP, nc, "hfp%d" % pass2, [128, TOK + 3], BF16)
                        fgr, fgr_b = _sb(sP, nc, "hfg%d" % pass2, [128, TOK], F32)
                        iir, iir_b = _sb(sP, nc, "hii%d" % pass2, [128, TOK], F32)
                        scs, scs_b = _sb(sP, nc, "hsc%d" % pass2, [128, TOK], F32)
                        c1, c1_b = _sb(sP, nc, "hc1%d" % pass2, [128, 512], F32)
                        c2, c2_b = _sb(sP, nc, "hc2%d" % pass2, [128, 512], F32)
                        sgx, sgx_b = _sb(sP, nc, "hsg%d" % pass2, [128, 512], F32)
                        slx, slx_b = _sb(sP, nc, "hsl%d" % pass2, [128, 512], F32)
                        ipp = [_ps(sP, nc, "hipp%d%d" % (pass2, i), [128, 512], F32)
                               for i in range(2)]
                        fpp = [_ps(sP, nc, "hfpp%d%d" % (pass2, i), [128, 512], F32)
                               for i in range(2)]
                        bpp, bpp_b = _ps(sP, nc, "hbpp%d" % pass2, [128, 3], F32)

                        for oc in range(8):
                            par = oc % 2
                            wih, wib = wic[par]
                            wfh, wfb = wfc[par]
                            p.dma("ge", wih[:], wi[oc], w=[wib])
                            p.dma("ge", wfh[:], wf[oc], w=[wfb])
                            p.flush("ge")
                            for (wh, wb, pad, padb) in ((wih, wib, ipd, ipd_b),
                                                        (wfh, wfb, fpd, fpd_b)):
                                for dc in range(8):
                                    p.op("te", lambda t, dc=dc, wh=wh: t.matmul(
                                        bpp[:], wh[:, dc, :], bnd[:, dc, :],
                                        start=(dc == 0), stop=(dc == 7)),
                                        r=[wb, bnd_b], w=[bpp_b])
                                p.op("se", lambda s, pad=pad: s.copy(pad[:, 0:3], bpp[:]),
                                     r=[bpp_b], w=[padb])
                                pp = ipp if pad is ipd else fpp
                                for s4 in range(4):
                                    ph, pb = pp[s4 % 2]
                                    for dc in range(8):
                                        p.op("te", lambda t, dc=dc, s4=s4, wh=wh, ph=ph: t.matmul(
                                            ph[:], wh[:, dc, :],
                                            X3T[:, dc, s4 * 512:(s4 + 1) * 512],
                                            start=(dc == 0), stop=(dc == 7)),
                                            r=[wb, X3T_b], w=[pb])
                                    p.op("se", lambda s, s4=s4, ph=ph, pad=pad: s.copy(
                                        pad[:, 3 + s4 * 512: 3 + (s4 + 1) * 512], ph[:]),
                                        r=[pb], w=[padb])
                            for s4 in range(4):
                                sl = slice(s4 * 512, (s4 + 1) * 512)
                                p.op("ve", lambda v, s4=s4, oc=oc: v.tensor_scalar_mul(
                                    c1[:], ipd[:, s4 * 512:s4 * 512 + 512], cwi[:, oc, 0:1]),
                                    r=[ipd_b, cwi_b], w=[c1_b])
                                for k in range(1, 4):
                                    p.op("ve", lambda v, s4=s4, k=k, oc=oc: v.scalar_tensor_tensor(
                                        c1[:], ipd[:, s4 * 512 + k:s4 * 512 + k + 512],
                                        cwi[:, oc, k:k + 1], c1[:], ALU.mult, ALU.add),
                                        r=[ipd_b, cwi_b], w=[c1_b])
                                p.op("ve", lambda v, s4=s4, oc=oc: v.tensor_scalar_mul(
                                    c2[:], fpd[:, s4 * 512:s4 * 512 + 512], cwf[:, oc, 0:1]),
                                    r=[fpd_b, cwf_b], w=[c2_b])
                                for k in range(1, 4):
                                    p.op("ve", lambda v, s4=s4, k=k, oc=oc: v.scalar_tensor_tensor(
                                        c2[:], fpd[:, s4 * 512 + k:s4 * 512 + k + 512],
                                        cwf[:, oc, k:k + 1], c2[:], ALU.mult, ALU.add),
                                        r=[fpd_b, cwf_b], w=[c2_b])
                                p.op("se", lambda s, sl=sl: s.activation(
                                    fgr[:, sl], c2[:], AF.Sigmoid), r=[c2_b], w=[fgr_b])
                                p.op("se", lambda s: s.activation(
                                    sgx[:], c1[:], AF.Sigmoid), r=[c1_b], w=[sgx_b])
                                p.op("ve", lambda v: v.tensor_tensor(
                                    slx[:], c1[:], sgx[:], ALU.mult), r=[c1_b, sgx_b], w=[slx_b])
                                p.op("ve", lambda v, sl=sl: v.tensor_scalar(
                                    sgx[:], fgr[:, sl], -1.0, 1.0, ALU.mult, ALU.add),
                                    r=[fgr_b], w=[sgx_b])
                                p.op("ve", lambda v, sl=sl: v.tensor_tensor(
                                    iir[:, sl], slx[:], sgx[:], ALU.mult),
                                    r=[slx_b, sgx_b], w=[iir_b])
                            if not pass2:
                                p.op("ve", lambda v: v.tensor_tensor_scan(
                                    scs[:], fgr[:], iir[:], 0.0, ALU.mult, ALU.add),
                                    r=[fgr_b, iir_b], w=[scs_b])
                                p.op("ve", lambda v, oc=oc: v.tensor_copy(
                                    hend[:, oc:oc + 1], scs[:, TOK - 1:TOK]),
                                    r=[scs_b], w=[hend_b])
                            else:
                                p.op("ve", lambda v, oc=oc: v.tensor_tensor_scan(
                                    ost[:, oc, :], fgr[:], iir[:], carry[:, oc:oc + 1],
                                    ALU.mult, ALU.add),
                                    r=[fgr_b, iir_b, carry_b], w=[ost_b])

                hgrn_pass(False)
                p.barrier()
                p.dma("qs", c_pub.ap(), hend[:], r=[hend_b], w=[yb["c_pub"]])
                p.flush("qs")
                p.cc("AllGather", ALU.bypass, c_pub.ap().opt(), c_ag.ap().opt(),
                     r=[yb["c_pub"]], w=[yb["c_ag"]])
                p.dma("qs", cg0[:], c_ag.ap()[0:128, :], r=[yb["c_ag"]], w=[cg0_b])
                p.flush("qs")
                p.op("ve", lambda v: v.tensor_scalar_mul(carry[:], cg0[:], mBs[:]),
                     r=[cg0_b, mBs_b], w=[carry_b])
                hgrn_pass(True)
                p.barrier()

                with ExitStack() as sN:
                    o2s, o2s_b = _sb(sN, nc, "n_o2", [128, TOK], BF16)
                    rrow, rrow_b = _sb(sN, nc, "n_rr", [1, TOK], F32)
                    gss, gss_b = _sb(sN, nc, "n_gs", [128, 512], F32)
                    gsl, gsl_b = _sb(sN, nc, "n_gl", [128, 512], F32)
                    ons, ons_b = _sb(sN, nc, "n_on", [128, 512], F32)
                    wgc = [_sb(sN, nc, "n_wg%d" % i, [128, 8, 128], BF16) for i in range(2)]
                    with ExitStack() as sN1:
                        rmp, rmp_b = _ps(sN1, nc, "n_rmp", [1, 4, 512], F32)
                        for oc in range(8):
                            p.op("ve", lambda v, oc=oc: v.tensor_tensor(
                                o2s[:], ost[:, oc, :], ost[:, oc, :], ALU.mult),
                                r=[ost_b], w=[o2s_b])
                            for s4 in range(4):
                                p.op("te", lambda t, s4=s4, oc=oc: t.matmul(
                                    rmp[:, s4, :], onescb[:],
                                    o2s[:, s4 * 512:(s4 + 1) * 512],
                                    start=(oc == 0), stop=(oc == 7)),
                                    r=[onescb_b, o2s_b], w=[rmp_b])
                        p.op("se", lambda s: s.copy(rrow[:], rmp[0:1, :, :].rearrange(
                            "x a b -> x (a b)")), r=[rmp_b], w=[rrow_b])
                    p.barrier()
                    p.op("se", lambda s: s.activation(rrow[:], rrow[:], AF.Ln,
                                                      scale=1.0 / D, bias=RMS_EPS),
                         r=[rrow_b], w=[rrow_b])
                    p.op("se", lambda s: s.activation(rrow[:], rrow[:], AF.Exp, scale=-0.5),
                         r=[rrow_b], w=[rrow_b])
                    with ExitStack() as sN2:
                        rbp, rbp_b = _ps(sN2, nc, "n_rbp", [128, 4, 512], F32)
                        gpp = [_ps(sN2, nc, "n_gp%d" % i, [128, 512], F32) for i in range(2)]
                        for s4 in range(4):
                            p.op("te", lambda t, s4=s4: t.matmul(
                                rbp[:, s4, :], onesf[:], rrow[0:1, s4 * 512:(s4 + 1) * 512],
                                start=True, stop=True),
                                r=[onesf_b, rrow_b], w=[rbp_b])
                        for oc in range(8):
                            par = oc % 2
                            wgh, wgb = wgc[par]
                            p.dma("ge", wgh[:], wg[oc], w=[wgb])
                            p.flush("ge")
                            for s4 in range(4):
                                gph, gpb = gpp[s4 % 2]
                                for dc in range(8):
                                    p.op("te", lambda t, dc=dc, s4=s4, wgh=wgh, gph=gph: t.matmul(
                                        gph[:], wgh[:, dc, :],
                                        X3T[:, dc, s4 * 512:(s4 + 1) * 512],
                                        start=(dc == 0), stop=(dc == 7)),
                                        r=[wgb, X3T_b], w=[gpb])
                                p.op("se", lambda s, gph=gph: s.activation(
                                    gss[:], gph[:], AF.Sigmoid), r=[gpb], w=[gss_b])
                                p.op("ve", lambda v, gph=gph: v.tensor_tensor(
                                    gsl[:], gph[:], gss[:], ALU.mult), r=[gpb, gss_b], w=[gsl_b])
                                sl = slice(s4 * 512, (s4 + 1) * 512)
                                p.op("ve", lambda v, oc=oc, s4=s4, sl=sl: v.scalar_tensor_tensor(
                                    ons[:], ost[:, oc, sl], gnws[:, oc:oc + 1], rbp[:, s4, :],
                                    ALU.mult, ALU.mult),
                                    r=[ost_b, gnws_b, rbp_b], w=[ons_b])
                                p.op("ve", lambda v, oc=oc, sl=sl: v.tensor_tensor(
                                    ost[:, oc, sl], ons[:], gsl[:], ALU.mult),
                                    r=[ons_b, gsl_b], w=[ost_b])
                    p.barrier()
                    with ExitStack() as sN3:
                        wop = [_ps(sN3, nc, "n_wo%d" % i, [128, 512], F32) for i in range(2)]
                        trh, trh_b = _ps(sN3, nc, "n_trh", [128, 4, 128], BF16)
                        owh, owh_b = _sb(sN3, nc, "n_ow", [128, 512], BF16)
                        woc = [_sb(sN3, nc, "n_wc%d" % i, [128, 8, 128], BF16) for i in range(2)]
                        for ooc in range(8):
                            par = ooc % 2
                            wch, wcb = woc[par]
                            p.dma("ge", wch[:], wo[ooc], w=[wcb])
                            p.flush("ge")
                            for tt in range(4):
                                oph, opb = wop[tt % 2]
                                for dc in range(8):
                                    p.op("te", lambda t, dc=dc, tt=tt, wch=wch, oph=oph: t.matmul(
                                        oph[:], wch[:, dc, :],
                                        ost[:, dc, tt * 512:(tt + 1) * 512],
                                        start=(dc == 0), stop=(dc == 7)),
                                        r=[wcb, ost_b], w=[opb])
                                p.op("ve", lambda v, oph=oph: v.tensor_copy(owh[:], oph[:]),
                                     r=[opb], w=[owh_b])
                                for k in range(4):
                                    p.op("te", lambda t, k=k: t.transpose(
                                        trh[:, k, :], owh[:, k * 128:(k + 1) * 128], identb[:]),
                                        r=[owh_b, identb_b], w=[trh_b])
                                p.op("ve", lambda v, tt=tt, ooc=ooc: v.tensor_tensor(
                                    res[:, tt * 4:(tt + 1) * 4, ooc * 128:(ooc + 1) * 128],
                                    res[:, tt * 4:(tt + 1) * 4, ooc * 128:(ooc + 1) * 128],
                                    trh[:], ALU.add),
                                    r=[trh_b], w=[res_b])
            p.barrier()
            if "res3" in dumps:
                p.dma("qs", dumps["res3"].rearrange("(j p) d -> p j d", p=128), res[:],
                      r=[res_b])
                p.flush("qs")
                p.barrier()

            emit_ffn("f2")

            for j in range(NB):
                p.dma("qs", y_out[j * 128:(j + 1) * 128, :], res[:, j, :], r=[res_b])
            p.flush("qs")
        p.barrier()
    return nc


def kernel(**inputs):
    x = np.ascontiguousarray(np.asarray(inputs["L0_attn_a_x"], dtype=np.float32))
    W = np.asarray(inputs["L0_attn_a_W"], dtype=np.float32)
    Wi = np.ascontiguousarray(np.asarray(inputs["L0_attn_b_Wi"], dtype=np.float32))
    Wf = np.ascontiguousarray(np.asarray(inputs["L0_attn_b_Wf"], dtype=np.float32))
    Wg = np.ascontiguousarray(np.asarray(inputs["L0_attn_b_Wg"], dtype=np.float32))
    Wo = np.ascontiguousarray(np.asarray(inputs["L0_attn_b_Wo"], dtype=np.float32))
    ci = np.asarray(inputs["L0_attn_b_conv_i"], dtype=np.float32)
    cf = np.asarray(inputs["L0_attn_b_conv_f"], dtype=np.float32)
    gn = np.asarray(inputs["L0_attn_b_g_norm_weight"], dtype=np.float32)
    gw = np.asarray(inputs["L0_ffn_gate_w"], dtype=np.float32)
    eb = np.asarray(inputs["L0_ffn_expert_bias"], dtype=np.float32)
    w1 = np.ascontiguousarray(np.asarray(inputs["L0_ffn_w1"], dtype=np.float32))
    w2 = np.ascontiguousarray(np.asarray(inputs["L0_ffn_w2"], dtype=np.float32))
    w3 = np.ascontiguousarray(np.asarray(inputs["L0_ffn_w3"], dtype=np.float32))

    w1r = np.ascontiguousarray(w1.reshape(8, 8, 128, 22, 128).transpose(0, 3, 2, 1, 4)
                               .reshape(8, 22, 128, 1024))
    w3r = np.ascontiguousarray(w3.reshape(8, 8, 128, 22, 128).transpose(0, 3, 2, 1, 4)
                               .reshape(8, 22, 128, 1024))
    w2r = np.ascontiguousarray(w2.reshape(8, 2, 11, 128, 1024).transpose(0, 1, 3, 2, 4))
    def colmajor(m):
        return np.ascontiguousarray(m.reshape(8, 128, 8, 128).transpose(2, 1, 0, 3))
    wir, wfr, wgr_, wor = colmajor(Wi), colmajor(Wf), colmajor(Wg), colmajor(Wo)
    ci_l = np.ascontiguousarray(ci.reshape(8, 128, 4).transpose(1, 0, 2))
    cf_l = np.ascontiguousarray(cf.reshape(8, 128, 4).transpose(1, 0, 2))
    gn_l = np.ascontiguousarray(gn.reshape(8, 128).T)
    gwt_l = np.ascontiguousarray(gw.T.reshape(8, 128, 8).transpose(1, 0, 2))
    eb_l = np.ascontiguousarray(eb.reshape(1, 8))

    nc = build_nc()
    in_maps = []
    for c in range(8):
        b, hf = c // 2, c % 2
        wq = W[hf * 512:(hf + 1) * 512]
        wk = W[1024 + hf * 512:1024 + (hf + 1) * 512]
        wv = W[2048 + hf * 512:2048 + (hf + 1) * 512]
        wgr = W[3072 + hf * 8:3072 + (hf + 1) * 8]
        wqkvT = np.ascontiguousarray(np.concatenate([wq, wk, wv, wgr], axis=0).T)
        im = {
            "x_own": np.ascontiguousarray(x[b, hf * TOK:(hf + 1) * TOK]),
            "x_full": np.ascontiguousarray(x[b]),
            "w_qkvT": wqkvT,
            "wi": wir, "wf": wfr, "wg": wgr_, "wo": wor,
            "conv_i": ci_l, "conv_f": cf_l, "gnw": gn_l, "gwt": gwt_l, "ebias": eb_l,
            "w1": w1r, "w2": w2r, "w3": w3r,
            "mA": np.full((128, 1), 1.0 - hf, np.float32),
            "mB": np.full((128, 1), float(hf), np.float32),
        }
        in_maps.append(im)
    res = run_bass_kernel_spmd(nc, in_maps, core_ids=list(range(8)))
    out = np.zeros((4, T, D), np.float32)
    for c in range(8):
        b, hf = c // 2, c % 2
        out[b, hf * TOK:(hf + 1) * TOK] = res.results[c]["y_out"]
    return out


# version 12 (final)
# speedup vs baseline: 1.6527x; 1.1590x over previous
import numpy as np
from contextlib import ExitStack

import concourse.bass as bass
import concourse.mybir as mybir
from concourse.bass_utils import run_bass_kernel_spmd

import bass_rust as _br
F32 = mybir.dt.float32
BF16 = mybir.dt.bfloat16
I32 = mybir.dt.int32
IOA = _br.IndirectOffsetOnAxis
AF = mybir.ActivationFunctionType
ALU = mybir.AluOpType
AX = mybir.AxisListType

D = 1024
T = 4096
TOK = 2048
NB = 16
NBF = 32
NHL = 8
HS = 64
HID = 2816
HC = 22
NE = 8
LOGSCALE = float(np.log(1.0 / 8.0))
EPS_ATT = 1e-7
LN_EPS = 1e-5
RMS_EPS = 1e-5
NEGBIG = -1.0e9
BIG = 1000.0
GROUPS2 = [[0, 1], [2, 3], [4, 5], [6, 7]]

_PENDQ = {"qs": ("PEND_qs", 0), "ge": ("PEND_ge", 0)}


class Buf:
    __slots__ = ("w", "r", "name")

    def __init__(self, name=""):
        self.w = None
        self.r = []
        self.name = name


class Prog:
    def __init__(self, nc, es):
        self.nc = nc
        self.E = {"te": nc.tensor, "ve": nc.vector, "se": nc.scalar,
                  "ge": nc.gpsimd, "qs": nc.sync}
        self.sem = {}
        for name in ("te", "ve", "se", "ge", "qsd", "ged"):
            self.sem[name] = es.enter_context(nc.semaphore("S_" + name))
        self.cnt = {"te": 0, "ve": 0, "se": 0, "ge": 0}
        self.dcnt = {"qs": 0, "ge": 0}
        self.pend = {"qs": [], "ge": []}
        self.waited = {k: {} for k in self.E}

    def _wait(self, ekey, toks):
        w = self.waited[ekey]
        need = {}
        for t in toks:
            if t is None:
                continue
            s, v = t
            if s.startswith("PEND"):
                if s == "PEND_" + ekey:
                    continue
                raise RuntimeError("buffer consumed before dma flush (%s)" % s)
            if s == "te" and ekey == "te":
                continue
            if v > w.get(s, 0) and v > need.get(s, 0):
                need[s] = v
        for s, v in sorted(need.items()):
            self.E[ekey].wait_ge(self.sem[s], v)
            w[s] = v

    def _collect(self, r, w):
        toks = []
        for b in r:
            toks.append(b.w)
        for b in w:
            toks.append(b.w)
            toks.extend(b.r)
        return toks

    def op(self, ekey, fn, r=(), w=()):
        self._wait(ekey, self._collect(r, w))
        inst = fn(self.E[ekey])
        self.cnt[ekey] += 1
        inst.then_inc(self.sem[ekey], 1)
        t = (ekey, self.cnt[ekey])
        for b in r:
            b.r.append(t)
        for b in w:
            b.w = t
            b.r = []
        return t

    def dma(self, q, out_ap, in_ap, r=(), w=(), indirect=None, indirect_in=None, **kw):
        self._wait(q, self._collect(r, w))
        if indirect is not None:
            idx, ax = indirect
            inst = self.E[q].indirect_dma_start(out_ap, IOA(idx[:], ax), in_ap, None,
                                                oob_is_err=False)
        elif indirect_in is not None:
            idx, ax = indirect_in
            inst = self.E[q].indirect_dma_start(out_ap, None, in_ap, IOA(idx[:], ax),
                                                oob_is_err=False)
        else:
            inst = self.E[q].dma_start(out=out_ap, in_=in_ap, **kw)
        dsem = "qsd" if q == "qs" else "ged"
        inst.then_inc(self.sem[dsem], 16)
        self.dcnt[q] += 16
        pt = _PENDQ[q]
        for b in w:
            b.w = pt
            b.r = []
        for b in r:
            b.r.append(pt)
        self.pend[q].append((r, w))

    def flush(self, q):
        if not self.pend[q]:
            return
        dsem = "qsd" if q == "qs" else "ged"
        v = self.dcnt[q]
        self.E[q].wait_ge(self.sem[dsem], v)
        self.waited[q][dsem] = v
        t = (dsem, v)
        pt = _PENDQ[q]
        for r, w in self.pend[q]:
            for b in w:
                if b.w == pt:
                    b.w = t
            for b in r:
                b.r = [t if x == pt else x for x in b.r]
        self.pend[q] = []

    def cc(self, kind, aluop, in_ap, out_ap, r=(), w=()):
        return self.op("ge", lambda g: g.collective_compute(
            kind, aluop, replica_groups=GROUPS2,
            ins=[in_ap], outs=[out_ap]), r=r, w=w)

    def barrier(self):
        self.flush("qs")
        self.flush("ge")
        toks = [(e, self.cnt[e]) for e in self.cnt]
        toks += [("qsd", self.dcnt["qs"]), ("ged", self.dcnt["ge"])]
        for ekey in self.E:
            self._wait(ekey, toks)


def _sb(es, nc, name, shape, dtype):
    h = es.enter_context(nc.sbuf_tensor(name, shape, dtype))
    return h, Buf(name)


def _ps(es, nc, name, shape, dtype):
    esz = mybir.dt.size(dtype)
    n = 1
    for d in shape[1:]:
        n *= d
    per = n * esz
    pad = (-per) % 2048
    if pad == 0:
        h = es.enter_context(nc.psum_tensor(name, shape, dtype))
        return h, Buf(name)
    flat = n + pad // esz
    h = es.enter_context(nc.psum_tensor(name, [shape[0], flat], dtype))
    ap = h[:, 0:n]
    if len(shape) == 3:
        ap = ap.rearrange("p (a b) -> p a b", a=shape[1])
    return ap, Buf(name)


def emit_ln(p, src_ap, src_bufs, out_ap, out_buf, sq, sqb, st, stb):
    p.op("ve", lambda v: v.reduce_sum(st[:, 0:1], src_ap, axis=AX.X), r=src_bufs, w=[stb])
    p.op("se", lambda s: s.activation(sq[:], src_ap, AF.Square), r=src_bufs, w=[sqb])
    p.op("ve", lambda v: v.reduce_sum(st[:, 1:2], sq[:], axis=AX.X), r=[sqb], w=[stb])
    p.op("ve", lambda v: v.tensor_scalar_mul(st[:, 2:3], st[:, 0:1], 1.0 / D), r=[stb], w=[stb])
    p.op("ve", lambda v: v.tensor_tensor(st[:, 3:4], st[:, 2:3], st[:, 2:3], ALU.mult), r=[stb], w=[stb])
    p.op("ve", lambda v: v.tensor_scalar_mul(st[:, 4:5], st[:, 1:2], 1.0 / D), r=[stb], w=[stb])
    p.op("ve", lambda v: v.tensor_tensor(st[:, 5:6], st[:, 4:5], st[:, 3:4], ALU.subtract), r=[stb], w=[stb])
    p.op("se", lambda s: s.activation(st[:, 6:7], st[:, 5:6], AF.Ln, bias=LN_EPS), r=[stb], w=[stb])
    p.op("se", lambda s: s.activation(st[:, 7:8], st[:, 6:7], AF.Exp, scale=-0.5), r=[stb], w=[stb])
    p.op("ve", lambda v: v.tensor_scalar(out_ap, src_ap, st[:, 2:3], st[:, 7:8],
                                         ALU.subtract, ALU.mult),
         r=list(src_bufs) + [stb], w=[out_buf])


def build_nc(dump=()):
    nc = bass.Bass("TRN2", target_bir_lowering=False)

    x_own = nc.dram_tensor("x_own", [TOK, D], F32, kind="ExternalInput").ap()
    x_full = nc.dram_tensor("x_full", [T, D], F32, kind="ExternalInput").ap()
    w_qkvT = nc.dram_tensor("w_qkvT", [D, 1544], F32, kind="ExternalInput").ap()
    wi = nc.dram_tensor("wi", [8, 128, 8, 128], F32, kind="ExternalInput").ap()
    wf = nc.dram_tensor("wf", [8, 128, 8, 128], F32, kind="ExternalInput").ap()
    wg = nc.dram_tensor("wg", [8, 128, 8, 128], F32, kind="ExternalInput").ap()
    wo = nc.dram_tensor("wo", [8, 128, 8, 128], F32, kind="ExternalInput").ap()
    conv_i = nc.dram_tensor("conv_i", [128, 8, 4], F32, kind="ExternalInput").ap()
    conv_f = nc.dram_tensor("conv_f", [128, 8, 4], F32, kind="ExternalInput").ap()
    gnw = nc.dram_tensor("gnw", [128, 8], F32, kind="ExternalInput").ap()
    gwt = nc.dram_tensor("gwt", [128, 8, 8], F32, kind="ExternalInput").ap()
    ebias = nc.dram_tensor("ebias", [1, 8], F32, kind="ExternalInput").ap()
    w1 = nc.dram_tensor("w1", [NE, HC, 128, D], F32, kind="ExternalInput").ap()
    w2 = nc.dram_tensor("w2", [NE, 2, 128, 11, D], F32, kind="ExternalInput").ap()
    w3 = nc.dram_tensor("w3", [NE, HC, 128, D], F32, kind="ExternalInput").ap()
    mA = nc.dram_tensor("mA", [128, 1], F32, kind="ExternalInput").ap()
    mB = nc.dram_tensor("mB", [128, 1], F32, kind="ExternalInput").ap()
    y_out = nc.dram_tensor("y_out", [TOK, D], F32, kind="ExternalOutput").ap()
    dumps = {}
    for nm in dump:
        shp = {"res1": [TOK, D], "res2": [TOK, D], "res3": [TOK, D],
               "sco": [128, NB, 8], "lgg": [8, T], "yag": [8192, 512]}[nm]
        dt = BF16 if nm == "yag" else F32
        dumps[nm] = nc.dram_tensor("d_" + nm, shp, dt, kind="ExternalOutput").ap()

    y_pub = nc.dram_tensor("y_pub", [T, 512], BF16)
    y_ag = nc.dram_tensor("y_ag", [2 * T, 512], BF16)
    b_pub = nc.dram_tensor("b_pub", [128, 24], BF16)
    b_ag = nc.dram_tensor("b_ag", [256, 24], BF16)
    c_pub = nc.dram_tensor("c_pub", [128, 8], F32)
    c_ag = nc.dram_tensor("c_ag", [256, 8], F32)
    w13b = nc.dram_tensor("w13b", [NE, 2, 128, 2, 11, D], BF16)
    w2b = nc.dram_tensor("w2b", [NE, 2, 128, 11, D], BF16)
    xl_d = nc.dram_tensor("xl_d", [TOK, D], BF16)
    xsl_d = nc.dram_tensor("xsl_d", [5120, D], BF16)
    osl_d = nc.dram_tensor("osl_d", [5120, D], BF16)
    yb = {n: Buf(n) for n in ("y_pub", "y_ag", "b_pub", "b_ag", "c_pub", "c_ag",
                              "w13b", "w2b", "xl_d", "xsl_d", "osl_d")}

    es = ExitStack()
    with es:
        p = Prog(nc, es)

        identf, identf_b = _sb(es, nc, "identf", [128, 128], F32)
        identb, identb_b = _sb(es, nc, "identb", [128, 128], BF16)
        onesf, onesf_b = _sb(es, nc, "onesf", [1, 128], F32)
        onescb, onescb_b = _sb(es, nc, "onescb", [128, 1], BF16)
        iota8, iota8_b = _sb(es, nc, "iota8", [128, 8], F32)
        iotaB8, iotaB8_b = _sb(es, nc, "iotaB8", [128, 8], F32)
        ebc, ebc_b = _sb(es, nc, "ebc", [128, 8], F32)
        ebrow, ebrow_b = _sb(es, nc, "ebrow", [1, 8], F32)
        cwi, cwi_b = _sb(es, nc, "cwi", [128, 8, 4], F32)
        cwf, cwf_b = _sb(es, nc, "cwf", [128, 8, 4], F32)
        gnws, gnws_b = _sb(es, nc, "gnws", [128, 8], F32)
        gwts, gwts_b = _sb(es, nc, "gwts", [128, 8, 8], BF16)
        mAs, mAs_b = _sb(es, nc, "mAs", [128, 1], F32)
        mBs, mBs_b = _sb(es, nc, "mBs", [128, 1], F32)
        iosc, iosc_b = _sb(es, nc, "iosc", [128, 128], F32)
        trix, trix_b = _sb(es, nc, "trix", [128, 128], BF16)
        iotac, iotac_b = _sb(es, nc, "iotac", [128, 1], F32)
        iota40, iota40_b = _sb(es, nc, "iota40", [8, 40], F32)
        iotok, iotok_b = _sb(es, nc, "iotok", [128, 16], F32)
        os18, os18_b = _sb(es, nc, "os18", [1, 8], F32)
        wst = [_sb(es, nc, "wst%d" % i, [128, D], BF16) for i in range(4)]
        cst, cst_b = _sb(es, nc, "cst", [128, 4], F32)
        for i, val in enumerate((LN_EPS, EPS_ATT, 1.0, RMS_EPS)):
            p.op("ve", lambda v, i=i, val=val: v.memset(cst[:, i:i + 1], val), w=[cst_b])
            nc.const_aps.register(F32, val, cst[:, i:i + 1])

        p.op("ge", lambda g: g.iota(iosc[:], pattern=[[1, 128]], base=0,
                                    channel_multiplier=-1,
                                    allow_small_or_imprecise_dtypes=True), w=[iosc_b])
        p.op("ve", lambda v: v.tensor_single_scalar(identf[:], iosc[:], 0.0, ALU.is_equal),
             r=[iosc_b], w=[identf_b])
        p.op("ve", lambda v: v.tensor_copy(identb[:], identf[:]), r=[identf_b], w=[identb_b])
        p.op("ve", lambda v: v.memset(onesf[:], 1.0), w=[onesf_b])
        p.op("ve", lambda v: v.memset(onescb[:], 1.0), w=[onescb_b])
        p.op("ge", lambda g: g.iota(iota8[:], pattern=[[1, 8]], base=0,
                                    channel_multiplier=0,
                                    allow_small_or_imprecise_dtypes=True), w=[iota8_b])
        p.op("ve", lambda v: v.tensor_single_scalar(trix[:], iosc[:], 0.0, ALU.is_gt),
             r=[iosc_b], w=[trix_b])
        p.op("ge", lambda g: g.iota(iotac[:], pattern=[[1, 1]], base=0,
                                    channel_multiplier=1,
                                    allow_small_or_imprecise_dtypes=True), w=[iotac_b])
        p.op("ge", lambda g: g.iota(iota40[:], pattern=[[128, 40]], base=0,
                                    channel_multiplier=0,
                                    allow_small_or_imprecise_dtypes=True), w=[iota40_b])
        p.op("ge", lambda g: g.iota(iotok[:], pattern=[[128, 16]], base=0,
                                    channel_multiplier=1,
                                    allow_small_or_imprecise_dtypes=True), w=[iotok_b])
        p.op("ve", lambda v: v.memset(os18[:], 1.0), w=[os18_b])
        p.op("ve", lambda v: v.tensor_single_scalar(iotaB8[:], iota8[:], -BIG, ALU.add),
             r=[iota8_b], w=[iotaB8_b])
        p.dma("qs", cwi[:], conv_i, w=[cwi_b])
        p.dma("qs", cwf[:], conv_f, w=[cwf_b])
        p.dma("qs", gnws[:], gnw, w=[gnws_b])
        p.dma("qs", mAs[:], mA, w=[mAs_b])
        p.dma("qs", mBs[:], mB, w=[mBs_b])
        p.dma("qs", ebrow[:], ebias, w=[ebrow_b])
        p.dma("ge", gwts[:], gwt, w=[gwts_b])
        p.flush("qs")
        p.flush("ge")

        with nc.psum_tensor("ebp", [128, 512], F32) as ebp:
            ebp_b = Buf("ebp")
            p.op("te", lambda t: t.matmul(ebp[:, 0:8], onesf[:], ebrow[:], start=True, stop=True),
                 r=[onesf_b, ebrow_b], w=[ebp_b])
            p.op("ve", lambda v: v.tensor_copy(ebc[:], ebp[:, 0:8]), r=[ebp_b], w=[ebc_b])
        p.barrier()

        wc_steps = []

        def _wc_pair(srcs):
            def go():
                for i, (sap, dap) in enumerate(srcs):
                    sbh, sbb = wst[i]
                    p.dma("ge", sbh[:], sap, w=[sbb])
                p.flush("ge")
                for i, (sap, dap) in enumerate(srcs):
                    sbh, sbb = wst[i]
                    p.dma("ge", dap, sbh[:], r=[sbb], w=[yb["w13b"], yb["w2b"]])
                p.flush("ge")
            return go

        _grp = []
        for e in range(NE):
            for hc in range(HC):
                half, hl = divmod(hc, 11)
                _grp.append((w1[e, hc], w13b.ap()[e, half, :, 0, hl, :]))
                _grp.append((w3[e, hc], w13b.ap()[e, half, :, 1, hl, :]))
                _grp.append((w2[e, half, :, hl, :], w2b.ap()[e, half, :, hl, :]))
                if len(_grp) >= 4:
                    wc_steps.append(_wc_pair(_grp[:4]))
                    _grp = _grp[4:]
        if _grp:
            wc_steps.append(_wc_pair(_grp))
        wc_pos = [0]

        def wcast_emit(k):
            n = min(k, len(wc_steps) - wc_pos[0])
            for _ in range(n):
                wc_steps[wc_pos[0]]()
                wc_pos[0] += 1

        with ExitStack() as s1:
            s1.enter_context(nc.named_scope("p1_attn"))
            XT, XT_b = _sb(s1, nc, "XT", [128, 8, T], BF16)
            wall, wall_b = _sb(s1, nc, "wall", [128, 8, 1544], BF16)
            lgG8, lgG8_b = _sb(s1, nc, "lgG8", [8, T], F32)
            lgGn, lgGn_b = _sb(s1, nc, "lgGn", [128, 32, 8], F32)

            p.dma("ge", wall[:], w_qkvT.rearrange("(dc p) o -> p dc o", p=128), w=[wall_b])
            p.flush("ge")
            wcast_emit(2)

            with ExitStack() as sA:
                xs = [_sb(sA, nc, "p1xs%d" % i, [128, D], F32) for i in range(2)]
                xln = [_sb(sA, nc, "p1xl%d" % i, [128, D], BF16) for i in range(2)]
                sq, sq_b = _sb(sA, nc, "p1sq", [128, D], F32)
                st, st_b = _sb(sA, nc, "p1st", [128, 8], F32)
                g8, g8_b = _sb(sA, nc, "p1g8", [8, T], F32)
                gs1, gs1_b = _sb(sA, nc, "p1gs1", [8, 512], F32)
                gs2, gs2_b = _sb(sA, nc, "p1gs2", [8, 512], F32)
                psb, psb_b = _ps(sA, nc, "p1psb", [128, 128], BF16)
                gp, gp_b = _ps(sA, nc, "p1gp", [8, 512], F32)
                tpp, tpp_b = _ps(sA, nc, "p1tpp", [128, 8], F32)

                for blk in range(NBF):
                    pb = blk % 2
                    xsh, xsb = xs[pb]
                    xlh, xlb = xln[pb]
                    p.dma("qs", xsh[:], x_full[blk * 128:(blk + 1) * 128, :], w=[xsb])
                    p.flush("qs")
                    emit_ln(p, xsh[:], [xsb], xlh[:], xlb, sq, sq_b, st, st_b)
                    for dc in range(8):
                        p.op("te", lambda t, dc=dc, xlh=xlh: t.transpose(
                            psb[:], xlh[:, dc * 128:(dc + 1) * 128], identb[:]),
                            r=[xlb, identb_b], w=[psb_b])
                        p.op("se", lambda s, dc=dc, blk=blk: s.copy(
                            XT[:, dc, blk * 128:(blk + 1) * 128], psb[:]),
                            r=[psb_b], w=[XT_b])
                    if blk % 4 == 3:
                        wcast_emit(1)

                for s8 in range(8):
                    for dc in range(8):
                        p.op("te", lambda t, dc=dc, s8=s8: t.matmul(
                            gp[:], wall[:, dc, 1536:1544], XT[:, dc, s8 * 512:(s8 + 1) * 512],
                            start=(dc == 0), stop=(dc == 7)),
                            r=[wall_b, XT_b], w=[gp_b])
                    p.op("se", lambda s, s8=s8: s.copy(g8[:, s8 * 512:(s8 + 1) * 512], gp[:]),
                         r=[gp_b], w=[g8_b])
                for s8 in range(8):
                    sl = slice(s8 * 512, (s8 + 1) * 512)
                    p.op("se", lambda s, sl=sl: s.activation(
                        gs1[:], g8[:, sl], AF.Abs), r=[g8_b], w=[gs1_b])
                    p.op("se", lambda s: s.activation(gs1[:], gs1[:], AF.Exp, scale=-1.0),
                         r=[gs1_b], w=[gs1_b])
                    p.op("se", lambda s: s.activation(gs1[:], gs1[:], AF.Ln, bias=1.0),
                         r=[gs1_b], w=[gs1_b])
                    p.op("ve", lambda v, sl=sl: v.tensor_single_scalar(
                        gs2[:], g8[:, sl], 0.0, ALU.min), r=[g8_b], w=[gs2_b])
                    p.op("ve", lambda v, sl=sl: v.tensor_tensor(
                        g8[:, sl], gs2[:], gs1[:], ALU.subtract),
                        r=[gs2_b, gs1_b], w=[g8_b])
                p.op("ve", lambda v: v.memset(lgG8[:], 1.0), w=[lgG8_b])
                p.op("ve", lambda v: v.tensor_tensor_scan(
                    lgG8[:], lgG8[:], g8[:], 0.0, ALU.mult, ALU.add),
                    r=[g8_b], w=[lgG8_b])
                if "lgg" in dumps:
                    p.dma("qs", dumps["lgg"], lgG8[:], r=[lgG8_b])
                    p.flush("qs")
                for j in range(32):
                    p.op("te", lambda t, j=j: t.transpose(
                        tpp[:], lgG8[:, j * 128:(j + 1) * 128], identf[0:8, 0:8]),
                        r=[lgG8_b, identf_b], w=[tpp_b])
                    p.op("se", lambda s, j=j: s.activation(
                        lgGn[:, j, :], tpp[:], AF.Copy, scale=-1.0),
                        r=[tpp_b], w=[lgGn_b])
                wcast_emit(2)
            p.barrier()

            with ExitStack() as sB:
                KTt = es_kt = sB.enter_context(nc.sbuf_tensor("KTt", [128, T], BF16))
                QTt = sB.enter_context(nc.sbuf_tensor("QTt", [128, T], BF16))
                KT = [(KTt[0:64, :], Buf("KT0")), (KTt[64:128, :], Buf("KT1"))]
                QT = [(QTt[0:64, :], Buf("QT0")), (QTt[64:128, :], Buf("QT1"))]
                Vsb = [_sb(sB, nc, "Vsb%d" % i, [128, 32, 65], BF16) for i in range(2)]
                VTs, VTs_b = _sb(sB, nc, "VTs", [64, T], BF16)
                y2, y2_b = _sb(sB, nc, "y2", [128, 32, 128], BF16)
                As2 = [_sb(sB, nc, "As%d" % i, [128, 2, 512], F32) for i in range(2)]
                Ls2 = [_sb(sB, nc, "Ls%d" % i, [128, 2, 512], F32) for i in range(2)]
                Ps2 = [_sb(sB, nc, "Ps%d" % i, [128, 2, 512], BF16) for i in range(2)]
                Rsb2 = [_sb(sB, nc, "Rsb%d" % i, [128, 512], F32) for i in range(2)]
                rr, rr_b = _sb(sB, nc, "rr", [1, 512], F32)
                rr02 = [_sb(sB, nc, "rr0%d" % i, [1, 512], F32) for i in range(2)]
                msk4, msk4_b = _sb(sB, nc, "msk4", [128, 4, 512], BF16)
                mtmp, mtmp_b = _sb(sB, nc, "mtmp", [128, 512], F32)
                Oe, Oe_b = _sb(sB, nc, "Oe", [65, 512], F32)
                rcp, rcp_b = _sb(sB, nc, "rcp", [128, 1], F32)
                S2 = [_ps(sB, nc, "S_ps%d" % i, [128, 2, 512], F32) for i in range(2)]
                O_ps, O_ps_b = _ps(sB, nc, "O_ps", [65, 512], F32)
                mp_ps, mp_ps_b = _ps(sB, nc, "mp_ps", [128, 512], F32)
                ftr_ps, ftr_ps_b = _ps(sB, nc, "ftr_ps", [128, 65], F32)
                vtr_ps, vtr_ps_b = _ps(sB, nc, "vtr_ps", [128, 64], BF16)

                for k in range(4):
                    p.op("ge", lambda g, k=k: g.iota(
                        mtmp[:], pattern=[[1, 512]], base=-128 * k, channel_multiplier=-1,
                        allow_small_or_imprecise_dtypes=True), w=[mtmp_b])
                    p.op("ve", lambda v, k=k: v.tensor_single_scalar(
                        msk4[:, k, :], mtmp[:], 0.0, ALU.is_lt), r=[mtmp_b], w=[msk4_b])
                p.op("ve", lambda v: v.tensor_scalar_mul(msk4[:], msk4[:], NEGBIG),
                     r=[msk4_b], w=[msk4_b])
                for i in range(2):
                    p.op("ve", lambda v, i=i: v.memset(Vsb[i][0][:, :, 64:65], 1.0),
                         w=[Vsb[i][1]])

                def setup_pieces(h):
                    hb = h % 2
                    kth, ktb = KT[hb]
                    qth, qtb = QT[hb]
                    vh, vb2 = Vsb[hb]
                    pieces = []
                    for s8 in range(8):
                        def go(s8=s8, kth=kth, ktb=ktb, qth=qth, qtb=qtb, vh=vh, vb2=vb2, h=h):
                            for (dst, dstb, off) in ((kth, ktb, 512), (qth, qtb, 0),
                                                     (VTs, VTs_b, 1024)):
                                for dc in range(8):
                                    p.op("te", lambda t, dc=dc, s8=s8, off=off, h=h: t.matmul(
                                        mp_ps[0:64, :],
                                        wall[:, dc, off + h * 64: off + (h + 1) * 64],
                                        XT[:, dc, s8 * 512:(s8 + 1) * 512],
                                        start=(dc == 0), stop=(dc == 7)),
                                        r=[wall_b, XT_b], w=[mp_ps_b])
                                p.op("ve", lambda v, s8=s8, dst=dst: v.tensor_copy(
                                    dst[:, s8 * 512:(s8 + 1) * 512], mp_ps[0:64, :]),
                                    r=[mp_ps_b], w=[dstb])
                            for j in range(4):
                                jj = s8 * 4 + j
                                p.op("te", lambda t, jj=jj: t.transpose(
                                    vtr_ps[:], VTs[:, jj * 128:(jj + 1) * 128],
                                    identb[0:64, 0:64]),
                                    r=[VTs_b, identb_b], w=[vtr_ps_b])
                                p.op("se", lambda s, jj=jj, vh=vh: s.copy(
                                    vh[:, jj, 0:64], vtr_ps[:]),
                                    r=[vtr_ps_b], w=[vb2])
                        pieces.append(go)
                    return pieces

                def attend(h, next_pieces):
                    hb = h % 2
                    kth, ktb = KT[hb]
                    qth, qtb = QT[hb]
                    vh, vb2 = Vsb[hb]
                    pend_fin = [None]

                    def emit_qk(s8, g0, gn):
                        gp = (g0 // 2) % 2
                        Sp, Sp_b = S2[gp]
                        for k in range(gn):
                            c = g0 + k
                            p.op("te", lambda t, k=k, c=c, s8=s8, kth=kth, qth=qth, Sp=Sp: t.matmul(
                                Sp[:, k, :], kth[:, c * 128:(c + 1) * 128],
                                qth[:, s8 * 512:(s8 + 1) * 512], start=True, stop=True),
                                r=[ktb, qtb], w=[Sp_b])

                    def emit_absln(s8, g0, gn):
                        gp = (g0 // 2) % 2
                        Sp, Sp_b = S2[gp]
                        Ah, Ah_b = As2[gp]
                        Lh, Lh_b = Ls2[gp]
                        p.op("se", lambda s, gn=gn, Ah=Ah, Sp=Sp: s.activation(
                            Ah[:, 0:gn, :], Sp[:, 0:gn, :], AF.Abs),
                            r=[Sp_b], w=[Ah_b])
                        p.op("se", lambda s, gn=gn, Ah=Ah, Lh=Lh: s.activation(
                            Lh[:, 0:gn, :], Ah[:, 0:gn, :], AF.Ln, bias=EPS_ATT),
                            r=[Ah_b], w=[Lh_b])

                    def emit_elem_pv(s8, g0, gn, nch):
                        gp = (g0 // 2) % 2
                        Sp, Sp_b = S2[gp]
                        Ah, Ah_b = As2[gp]
                        Lh, Lh_b = Ls2[gp]
                        Ph, Ph_b = Ps2[gp]
                        Rsb, Rsb_b = Rsb2[s8 % 2]
                        for k in range(gn):
                            c = g0 + k
                            p.op("ve", lambda v, k=k, Ah=Ah, Lh=Lh, Rsb=Rsb: v.tensor_tensor(
                                Ah[:, k, :], Lh[:, k, :], Rsb[:], ALU.add),
                                r=[Lh_b, Rsb_b], w=[Ah_b])
                            if c >= 4 * s8:
                                p.op("ve", lambda v, k=k, c=c, s8=s8, Ah=Ah: v.tensor_tensor(
                                    Ah[:, k, :], Ah[:, k, :], msk4[:, c - 4 * s8, :],
                                    ALU.add),
                                    r=[msk4_b], w=[Ah_b])
                        for k in range(gn):
                            c = g0 + k
                            p.op("se", lambda s, k=k, c=c, h=h, Ah=Ah, Ph=Ph: s.activation(
                                Ph[:, k, :], Ah[:, k, :], AF.Exp, scale=2.0,
                                bias=lgGn[:, c, h:h + 1]),
                                r=[Ah_b, lgGn_b], w=[Ph_b])
                        for k in range(gn):
                            c = g0 + k
                            p.op("te", lambda t, k=k, c=c, vh=vh, Ph=Ph: t.matmul(
                                O_ps[:], vh[:, c, :], Ph[:, k, :],
                                start=(c == 0), stop=(c == nch - 1)),
                                r=[vb2, Ph_b], w=[O_ps_b])

                    def make_fin(s8, h):
                        def fin():
                            p.op("se", lambda s: s.copy(Oe[:], O_ps[:]), r=[O_ps_b], w=[Oe_b])
                            for j in range(4):
                                p.op("te", lambda t, j=j: t.transpose(
                                    ftr_ps[:], Oe[:, j * 128:(j + 1) * 128],
                                    identf[0:65, 0:65]),
                                    r=[Oe_b, identf_b], w=[ftr_ps_b])
                                p.op("ve", lambda v: v.reciprocal(rcp[:], ftr_ps[:, 64:65]),
                                     r=[ftr_ps_b], w=[rcp_b])
                                p.op("ve", lambda v, j=j, s8=s8, h=h: v.tensor_scalar_mul(
                                    y2[:, s8 * 4 + j, (h % 2) * 64:(h % 2) * 64 + 64],
                                    ftr_ps[:, 0:64], rcp[:]),
                                    r=[ftr_ps_b, rcp_b], w=[y2_b])
                        return fin

                    rh0, rb0 = rr02[0]
                    p.dma("qs", rh0[:], lgG8[h:h + 1, 0:512], r=[lgG8_b], w=[rb0])
                    for s8 in range(8):
                        rpar = s8 % 2
                        rr0h, rr0b = rr02[rpar]
                        Rsb, Rsb_b = Rsb2[rpar]
                        nch = 4 * (s8 + 1)
                        emit_qk(s8, 0, 2)
                        if pend_fin[0] is not None:
                            pend_fin[0]()
                            pend_fin[0] = None
                        p.flush("qs")
                        if s8 < 7:
                            nh_, nb_ = rr02[(s8 + 1) % 2]
                            p.dma("qs", nh_[:], lgG8[h:h + 1, (s8 + 1) * 512:(s8 + 2) * 512],
                                  r=[lgG8_b], w=[nb_])
                        p.op("ve", lambda v, rr0h=rr0h: v.tensor_scalar(
                            rr[:], rr0h[:], 0.5, LOGSCALE / 2.0, ALU.mult, ALU.add),
                            r=[rr0b], w=[rr_b])
                        p.op("te", lambda t: t.matmul(mp_ps[:], onesf[:], rr[:],
                                                      start=True, stop=True),
                             r=[onesf_b, rr_b], w=[mp_ps_b])
                        p.op("ve", lambda v, Rsb=Rsb: v.tensor_copy(Rsb[:], mp_ps[:]),
                             r=[mp_ps_b], w=[Rsb_b])
                        groups = list(range(0, nch, 2))
                        ng = len(groups)

                        def _gn(gi):
                            return min(2, nch - groups[gi])

                        if ng > 1:
                            emit_qk(s8, groups[1], _gn(1))
                        emit_absln(s8, 0, 2)
                        for gi, g0 in enumerate(groups):
                            gn = min(2, nch - g0)
                            if gi + 2 < ng:
                                emit_qk(s8, groups[gi + 2], _gn(gi + 2))
                            if gi + 1 < ng:
                                emit_absln(s8, groups[gi + 1], _gn(gi + 1))
                            elif s8 < 4 and next_pieces:
                                next_pieces.pop(0)()
                            emit_elem_pv(s8, g0, gn, nch)
                        if s8 >= 4 and next_pieces:
                            next_pieces.pop(0)()
                        pend_fin[0] = make_fin(s8, h)
                    pend_fin[0]()
                    while next_pieces:
                        next_pieces.pop(0)()
                    if h % 2 == 1:
                        h2 = h // 2
                        for q4 in range(4):
                            p.dma("qs", y_pub.ap().rearrange("(j p) c -> p j c", p=128)
                                  [:, q4 * 8:(q4 + 1) * 8, h2 * 128:(h2 + 1) * 128],
                                  y2[:, q4 * 8:(q4 + 1) * 8, :], r=[y2_b], w=[yb["y_pub"]])
                        p.flush("qs")

                for piece in setup_pieces(0):
                    piece()
                for h in range(NHL):
                    wcast_emit(17)
                    np_ = setup_pieces(h + 1) if h + 1 < NHL else []
                    attend(h, np_)
                wcast_emit(len(wc_steps))
                p.cc("AllGather", ALU.bypass, y_pub.ap().opt(), y_ag.ap().opt(),
                     r=[yb["y_pub"]], w=[yb["y_ag"]])
                if "yag" in dumps:
                    p.dma("qs", dumps["yag"], y_ag.ap(), r=[yb["y_ag"]])
                    p.flush("qs")
            p.barrier()

        with ExitStack() as sR:
            res, res_b = _sb(sR, nc, "res", [128, NB, D], F32)

            with ExitStack() as sRB:
                sRB.enter_context(nc.named_scope("res_build"))
                xs0, xs0_b = _sb(sRB, nc, "rbx", [128, D], F32)
                ya, ya_b = _sb(sRB, nc, "rbya", [128, 512], BF16)
                yB2, yB2_b = _sb(sRB, nc, "rbyb", [128, 512], BF16)
                for j in range(NB):
                    p.dma("qs", xs0[:], x_own[j * 128:(j + 1) * 128, :], w=[xs0_b])
                    p.flush("qs")
                    p.op("ve", lambda v, j=j: v.tensor_copy(res[:, j, :], xs0[:]),
                         r=[xs0_b], w=[res_b])
                    for slot in range(2):
                        p.dma("qs", ya[:], y_ag.ap()[slot * T + j * 128: slot * T + (j + 1) * 128, :],
                              r=[yb["y_ag"]], w=[ya_b])
                        p.dma("qs", yB2[:], y_ag.ap()[slot * T + TOK + j * 128: slot * T + TOK + (j + 1) * 128, :],
                              r=[yb["y_ag"]], w=[yB2_b])
                        p.flush("qs")
                        sl = slice(slot * 512, (slot + 1) * 512)
                        p.op("ve", lambda v, j=j, sl=sl: v.scalar_tensor_tensor(
                            res[:, j, sl], ya[:], mAs[:], res[:, j, sl], ALU.mult, ALU.add),
                            r=[ya_b, mAs_b], w=[res_b])
                        p.op("ve", lambda v, j=j, sl=sl: v.scalar_tensor_tensor(
                            res[:, j, sl], yB2[:], mBs[:], res[:, j, sl], ALU.mult, ALU.add),
                            r=[yB2_b, mBs_b], w=[res_b])
            if "res1" in dumps:
                p.dma("qs", dumps["res1"].rearrange("(j p) d -> p j d", p=128), res[:],
                      r=[res_b])
                p.flush("qs")
            p.barrier()

            def emit_ffn(tag, dump_sco=False):
                with ExitStack() as sF:
                    sF.enter_context(nc.named_scope(tag + "_ffn"))
                    sctm, sctm_b = _sb(sF, nc, tag + "sctm", [128, NB, 8], F32)
                    t8a, t8a_b = _sb(sF, nc, tag + "t8a", [128, 8], F32)
                    t8b, t8b_b = _sb(sF, nc, tag + "t8b", [128, 8], F32)
                    t8c, t8c_b = _sb(sF, nc, tag + "t8c", [128, 8], F32)
                    t1c, t1c_b = _sb(sF, nc, tag + "t1c", [128, 1], F32)
                    mkb, mkb_b = _sb(sF, nc, tag + "mkb", [128, NB, 8], BF16)
                    m1t, m1t_b = _sb(sF, nc, tag + "m1t", [128, NB, 8], F32)
                    m2t, m2t_b = _sb(sF, nc, tag + "m2t", [128, NB, 8], F32)
                    sr1, sr1_b = _sb(sF, nc, tag + "sr1", [128, NB], F32)
                    sr2, sr2_b = _sb(sF, nc, tag + "sr2", [128, NB], F32)
                    cntA, cntA_b = _sb(sF, nc, tag + "cntA", [1, 128], F32)
                    r8 = {}
                    for nm in ("cnt", "al", "incl", "starts", "run", "sbrow", "tmp8"):
                        r8[nm] = _sb(sF, nc, tag + nm, [1, 8], F32)
                    blk2e, blk2e_b = _sb(sF, nc, tag + "blk2e", [1, 40], F32)
                    widx = [_sb(sF, nc, tag + "widx%d" % i, [128, 40], I32) for i in range(2)]
                    slotf, slotf_b = _sb(sF, nc, tag + "slotf", [128, 8], F32)
                    sfa, sfa_b = _sb(sF, nc, tag + "sfa", [128, 8], F32)
                    sfb, sfb_b = _sb(sF, nc, tag + "sfb", [128, 8], F32)

                    inclT, inclT_b = _sb(sF, nc, tag + "inclT", [8, 1], F32)
                    cmpb, cmpb_b = _sb(sF, nc, tag + "cmpb", [8, 40], BF16)
                    with ExitStack() as sFP:
                        XTf, XTf_b = _sb(sFP, nc, tag + "XTf", [128, 8, TOK], BF16)
                        xlf, xlf_b = _sb(sFP, nc, tag + "xlf", [128, D], BF16)
                        sqf, sqf_b = _sb(sFP, nc, tag + "sqf", [128, D], F32)
                        stf, stf_b = _sb(sFP, nc, tag + "stf", [128, 8], F32)
                        lgt, lgt_b = _sb(sFP, nc, tag + "lgt", [8, 512], F32)
                        ltm, ltm_b = _sb(sFP, nc, tag + "ltm", [128, NB, 8], F32)
                        psbF, psbF_b = _ps(sFP, nc, tag + "psbF", [128, 128], BF16)
                        lgp, lgp_b = _ps(sFP, nc, tag + "lgp", [8, 512], F32)
                        tpf, tpf_b = _ps(sFP, nc, tag + "tpf", [128, 8], F32)

                        for j in range(NB):
                            emit_ln(p, res[:, j, :], [res_b], xlf[:], xlf_b,
                                    sqf, sqf_b, stf, stf_b)
                            p.dma("qs", xl_d.ap()[j * 128:(j + 1) * 128, :], xlf[:],
                                  r=[xlf_b], w=[yb["xl_d"]])
                            p.flush("qs")
                            for dc in range(8):
                                p.op("te", lambda t, dc=dc: t.transpose(
                                    psbF[:], xlf[:, dc * 128:(dc + 1) * 128], identb[:]),
                                    r=[xlf_b, identb_b], w=[psbF_b])
                                p.op("se", lambda s, dc=dc, j=j: s.copy(
                                    XTf[:, dc, j * 128:(j + 1) * 128], psbF[:]),
                                    r=[psbF_b], w=[XTf_b])
                        for s4 in range(4):
                            for dc in range(8):
                                p.op("te", lambda t, dc=dc, s4=s4: t.matmul(
                                    lgp[:], gwts[:, dc, :], XTf[:, dc, s4 * 512:(s4 + 1) * 512],
                                    start=(dc == 0), stop=(dc == 7)),
                                    r=[gwts_b, XTf_b], w=[lgp_b])
                            p.op("se", lambda s: s.copy(lgt[:], lgp[:]), r=[lgp_b], w=[lgt_b])
                            for jj in range(4):
                                j = s4 * 4 + jj
                                p.op("te", lambda t, jj=jj: t.transpose(
                                    tpf[:], lgt[:, jj * 128:(jj + 1) * 128], identf[0:8, 0:8]),
                                    r=[lgt_b, identf_b], w=[tpf_b])
                                p.op("ve", lambda v, j=j: v.tensor_copy(ltm[:, j, :], tpf[:]),
                                     r=[tpf_b], w=[ltm_b])
                        for j in range(NB):
                            jj = j
                            p.op("ve", lambda v: v.reduce_max(t1c[:], ltm[:, jj, :], axis=AX.X),
                                 r=[ltm_b], w=[t1c_b])
                            p.op("ve", lambda v: v.tensor_scalar_mul(t1c[:], t1c[:], -1.0),
                                 r=[t1c_b], w=[t1c_b])
                            p.op("se", lambda s: s.activation(t8a[:], ltm[:, jj, :], AF.Exp,
                                                              bias=t1c[:]),
                                 r=[ltm_b, t1c_b], w=[t8a_b])
                            p.op("ve", lambda v: v.reduce_sum(t1c[:], t8a[:], axis=AX.X),
                                 r=[t8a_b], w=[t1c_b])
                            p.op("ve", lambda v: v.reciprocal(t1c[:], t1c[:]),
                                 r=[t1c_b], w=[t1c_b])
                            p.op("ve", lambda v: v.tensor_scalar_mul(sctm[:, jj, :], t8a[:], t1c[:]),
                                 r=[t8a_b, t1c_b], w=[sctm_b])
                            p.op("ve", lambda v: v.tensor_tensor(t8a[:], sctm[:, jj, :], ebc[:],
                                                                 ALU.add),
                                 r=[sctm_b, ebc_b], w=[t8a_b])
                            p.op("ve", lambda v: v.reduce_max(t1c[:], t8a[:], axis=AX.X),
                                 r=[t8a_b], w=[t1c_b])
                            p.op("ve", lambda v: v.tensor_scalar(t8b[:], t8a[:], t1c[:], None,
                                                                 ALU.is_equal),
                                 r=[t8a_b, t1c_b], w=[t8b_b])
                            p.op("ve", lambda v: v.tensor_tensor(t8b[:], t8b[:], iotaB8[:],
                                                                 ALU.mult),
                                 r=[iotaB8_b], w=[t8b_b])
                            p.op("ve", lambda v: v.tensor_single_scalar(t8b[:], t8b[:], BIG,
                                                                        ALU.add),
                                 r=[t8b_b], w=[t8b_b])
                            p.op("ve", lambda v: v.tensor_reduce(t1c[:], t8b[:], axis=AX.X,
                                                                 op=ALU.min),
                                 r=[t8b_b], w=[t1c_b])
                            p.op("ve", lambda v: v.tensor_scalar(t8b[:], iota8[:], t1c[:], None,
                                                                 ALU.is_equal),
                                 r=[iota8_b, t1c_b], w=[t8b_b])
                            p.op("ve", lambda v: v.tensor_copy(m1t[:, jj, :], t8b[:]),
                                 r=[t8b_b], w=[m1t_b])
                            p.op("ve", lambda v: v.scalar_tensor_tensor(
                                t8c[:], t8b[:], -BIG, t8a[:], ALU.mult, ALU.add),
                                r=[t8b_b, t8a_b], w=[t8c_b])
                            p.op("ve", lambda v: v.reduce_max(t1c[:], t8c[:], axis=AX.X),
                                 r=[t8c_b], w=[t1c_b])
                            p.op("ve", lambda v: v.tensor_scalar(t8c[:], t8c[:], t1c[:], None,
                                                                 ALU.is_equal),
                                 r=[t1c_b], w=[t8c_b])
                            p.op("ve", lambda v: v.tensor_tensor(t8c[:], t8c[:], iotaB8[:],
                                                                 ALU.mult),
                                 r=[iotaB8_b], w=[t8c_b])
                            p.op("ve", lambda v: v.tensor_single_scalar(t8c[:], t8c[:], BIG,
                                                                        ALU.add),
                                 r=[t8c_b], w=[t8c_b])
                            p.op("ve", lambda v: v.tensor_reduce(t1c[:], t8c[:], axis=AX.X,
                                                                 op=ALU.min),
                                 r=[t8c_b], w=[t1c_b])
                            p.op("ve", lambda v: v.tensor_scalar(t8c[:], iota8[:], t1c[:], None,
                                                                 ALU.is_equal),
                                 r=[iota8_b, t1c_b], w=[t8c_b])
                            p.op("ve", lambda v: v.tensor_copy(m2t[:, jj, :], t8c[:]),
                                 r=[t8c_b], w=[m2t_b])
                            p.op("ve", lambda v: v.tensor_tensor(t8b[:], t8b[:], t8c[:], ALU.add),
                                 r=[t8c_b], w=[t8b_b])
                            p.op("ve", lambda v: v.tensor_copy(mkb[:, jj, :], t8b[:]),
                                 r=[t8b_b], w=[mkb_b])
                            p.op("ve", lambda v: v.tensor_tensor(sctm[:, jj, :], sctm[:, jj, :],
                                                                 t8b[:], ALU.mult),
                                 r=[t8b_b], w=[sctm_b])
                        if dump_sco and "sco" in dumps:
                            p.dma("qs", dumps["sco"], sctm[:], r=[sctm_b])
                            p.flush("qs")
                    p.barrier()

                    with ExitStack() as sRT:
                        idxs4 = [_sb(sRT, nc, tag + "idxs%d" % i, [128, 1], I32)
                                 for i in range(4)]
                        scs4 = [_sb(sRT, nc, tag + "scs%d" % i, [128, D], BF16)
                                for i in range(4)]
                        xbl, xbl_b = _sb(sRT, nc, tag + "xbl", [128, D], BF16)
                        cntp, cntp_b = _ps(sRT, nc, tag + "cntp", [1, 8], F32)
                        prefp, prefp_b = _ps(sRT, nc, tag + "prefp", [128, 8], F32)
                        sbbcp, sbbcp_b = _ps(sRT, nc, tag + "sbbcp", [128, 8], F32)
                        tixp, tixp_b = _ps(sRT, nc, tag + "tixp", [8, 1], F32)
                        b2ep, b2ep_b = _ps(sRT, nc, tag + "b2ep", [1, 40], F32)
                        e40p, e40p_b = _ps(sRT, nc, tag + "e40p", [128, 40], F32)

                        p.op("ve", lambda v: v.memset(r8["run"][0][:], 0.0), w=[r8["run"][1]])
                        for j in range(NB):
                            p.op("te", lambda t, j=j: t.matmul(
                                cntp[:], onescb[:], mkb[:, j, :], start=True, stop=True),
                                r=[onescb_b, mkb_b], w=[cntp_b])
                            p.op("ve", lambda v, j=j: v.tensor_copy(
                                cntA[0:1, j * 8:(j + 1) * 8], cntp[:]), r=[cntp_b], w=[cntA_b])
                            p.op("ve", lambda v, j=j: v.tensor_tensor(
                                r8["run"][0][:], r8["run"][0][:], cntA[0:1, j * 8:(j + 1) * 8],
                                ALU.add), r=[cntA_b], w=[r8["run"][1]])
                        p.op("ve", lambda v: v.tensor_copy(r8["cnt"][0][:], r8["run"][0][:]),
                             r=[r8["run"][1]], w=[r8["cnt"][1]])
                        p.op("ve", lambda v: v.memset(r8["al"][0][:], 0.0), w=[r8["al"][1]])
                        for m in range(16):
                            p.op("ve", lambda v, m=m: v.tensor_single_scalar(
                                r8["tmp8"][0][:], r8["cnt"][0][:], float(m * 128), ALU.is_gt),
                                r=[r8["cnt"][1]], w=[r8["tmp8"][1]])
                            p.op("ve", lambda v: v.scalar_tensor_tensor(
                                r8["al"][0][:], r8["tmp8"][0][:], 128.0, r8["al"][0][:],
                                ALU.mult, ALU.add), r=[r8["tmp8"][1]], w=[r8["al"][1]])
                        p.op("ve", lambda v: v.tensor_tensor_scan(
                            r8["incl"][0][:], os18[:], r8["al"][0][:], 0.0, ALU.mult, ALU.add),
                            r=[os18_b, r8["al"][1]], w=[r8["incl"][1]])
                        p.op("ve", lambda v: v.tensor_tensor(
                            r8["starts"][0][:], r8["incl"][0][:], r8["al"][0][:], ALU.subtract),
                            r=[r8["incl"][1], r8["al"][1]], w=[r8["starts"][1]])
                        p.op("te", lambda t: t.transpose(tixp[:], r8["incl"][0][:],
                                                         identf[0:1, 0:1]),
                             r=[r8["incl"][1], identf_b], w=[tixp_b])
                        p.op("ve", lambda v: v.tensor_copy(inclT[:], tixp[:]),
                             r=[tixp_b], w=[inclT_b])
                        p.op("ve", lambda v: v.tensor_scalar(
                            cmpb[:], iota40[:], inclT[:], None, ALU.is_ge),
                            r=[iota40_b, inclT_b], w=[cmpb_b])
                        p.op("te", lambda t: t.matmul(b2ep[:], onescb[0:8, :], cmpb[:],
                                                      start=True, stop=True),
                             r=[onescb_b, cmpb_b], w=[b2ep_b])
                        p.op("ve", lambda v: v.tensor_copy(blk2e[:], b2ep[:]),
                             r=[b2ep_b], w=[blk2e_b])
                        p.op("te", lambda t: t.matmul(e40p[:], onesf[:], blk2e[:],
                                                      start=True, stop=True),
                             r=[onesf_b, blk2e_b], w=[e40p_b])
                        for hh in range(2):
                            wih, wib = widx[hh]
                            p.op("ve", lambda v, wih=wih, hh=hh: v.tensor_scalar(
                                wih[:], e40p[:], 256.0, float(hh * 128), ALU.mult, ALU.add),
                                r=[e40p_b], w=[wib])
                            p.op("ve", lambda v, wih=wih: v.tensor_scalar(
                                wih[:], wih[:], iotac[:], None, ALU.add),
                                r=[iotac_b], w=[wib])

                        p.op("ve", lambda v: v.memset(r8["run"][0][:], 0.0), w=[r8["run"][1]])
                        for j in range(NB):
                            p.op("ve", lambda v, j=j: v.tensor_tensor(
                                r8["sbrow"][0][:], r8["starts"][0][:], r8["run"][0][:], ALU.add),
                                r=[r8["starts"][1], r8["run"][1]], w=[r8["sbrow"][1]])
                            p.op("ve", lambda v, j=j: v.tensor_tensor(
                                r8["run"][0][:], r8["run"][0][:], cntA[0:1, j * 8:(j + 1) * 8],
                                ALU.add), r=[cntA_b], w=[r8["run"][1]])
                            p.op("te", lambda t, j=j: t.matmul(
                                prefp[:], trix[:], mkb[:, j, :], start=True, stop=True),
                                r=[trix_b, mkb_b], w=[prefp_b])
                            p.op("te", lambda t: t.matmul(
                                sbbcp[:], onesf[:], r8["sbrow"][0][:], start=True, stop=True),
                                r=[onesf_b, r8["sbrow"][1]], w=[sbbcp_b])
                            p.op("ve", lambda v, j=j: v.tensor_tensor(
                                sfa[:], prefp[:], m1t[:, j, :], ALU.mult),
                                r=[prefp_b, m1t_b], w=[sfa_b])
                            p.op("ve", lambda v, j=j: v.tensor_tensor(
                                sfa[:], prefp[:], mkb[:, j, :], ALU.mult),
                                r=[prefp_b, mkb_b], w=[sfa_b])
                            p.op("ve", lambda v, j=j: v.tensor_tensor(
                                sfb[:], sbbcp[:], mkb[:, j, :], ALU.mult),
                                r=[sbbcp_b, mkb_b], w=[sfb_b])
                            p.op("ve", lambda v: v.tensor_tensor(
                                slotf[:], sfa[:], sfb[:], ALU.add),
                                r=[sfa_b, sfb_b], w=[slotf_b])
                            p.op("ve", lambda v, j=j: v.tensor_scalar(
                                sfa[:], mkb[:, j, :], -BIG, BIG, ALU.mult, ALU.add),
                                r=[mkb_b], w=[sfa_b])
                            p.op("ve", lambda v: v.tensor_tensor(
                                slotf[:], slotf[:], sfa[:], ALU.add),
                                r=[sfa_b], w=[slotf_b])
                            p.op("ve", lambda v, j=j: v.tensor_tensor(
                                sfa[:], slotf[:], m1t[:, j, :], ALU.mult),
                                r=[slotf_b, m1t_b], w=[sfa_b])
                            p.op("ve", lambda v, j=j: v.reduce_sum(
                                sr1[:, j:j + 1], sfa[:], axis=AX.X), r=[sfa_b], w=[sr1_b])
                            p.op("ve", lambda v, j=j: v.tensor_tensor(
                                sfa[:], slotf[:], m2t[:, j, :], ALU.mult),
                                r=[slotf_b, m2t_b], w=[sfa_b])
                            p.op("ve", lambda v, j=j: v.reduce_sum(
                                sr2[:, j:j + 1], sfa[:], axis=AX.X), r=[sfa_b], w=[sr2_b])
                            p.dma("qs", xbl[:], xl_d.ap()[j * 128:(j + 1) * 128, :],
                                  r=[yb["xl_d"]], w=[xbl_b])
                            p.flush("qs")
                            for e in range(NE):
                                sch, scb = scs4[e % 4]
                                ixh, ixb = idxs4[e % 4]
                                p.op("ve", lambda v, j=j, e=e, sch=sch: v.tensor_scalar_mul(
                                    sch[:], xbl[:], sctm[:, j, e:e + 1]),
                                    r=[xbl_b, sctm_b], w=[scb])
                                p.op("ve", lambda v, e=e, ixh=ixh: v.tensor_copy(
                                    ixh[:], slotf[:, e:e + 1]), r=[slotf_b], w=[ixb])
                                p.dma("ge", xsl_d.ap(), sch[:], r=[scb, ixb],
                                      w=[yb["xsl_d"]],
                                      indirect=(ixh, 0))
                                if e % 4 == 3:
                                    p.flush("ge")
                        p.flush("ge")
                    p.barrier()

                    with ExitStack() as sF2:
                        w13u = [_sb(sF2, nc, tag + "w13u%d" % i, [128, 2, 11, D], BF16)
                                for i in range(2)]
                        w2u, w2u_b = _sb(sF2, nc, tag + "w2u", [128, 11, D], BF16)
                        idxw = [_sb(sF2, nc, tag + "idxw%d" % i, [128, 1], I32) for i in range(2)]
                        xg = [_sb(sF2, nc, tag + "xg%d" % i, [128, D], BF16) for i in range(2)]
                        xbt, xbt_b = _sb(sF2, nc, tag + "xbt", [128, 8, 128], BF16)
                        hbl, hbl_b = _sb(sF2, nc, tag + "hbl", [128, 11, 128], BF16)
                        ofm, ofm_b = _sb(sF2, nc, tag + "ofm", [128, 8, 128], F32)
                        otm, otm_b = _sb(sF2, nc, tag + "otm", [128, D], BF16)
                        s1b, s1b_b = _sb(sF2, nc, tag + "s1b", [128, 128], F32)
                        g1b, g1b_b = xg[0]
                        g2b, g2b_b = xg[1]
                        ir1, ir1_b = _sb(sF2, nc, tag + "ir1", [128, 1], I32)
                        ir2, ir2_b = _sb(sF2, nc, tag + "ir2", [128, 1], I32)
                        u1p = [_ps(sF2, nc, tag + "u1p%d" % i, [128, 128], F32) for i in range(2)]
                        u3p = [_ps(sF2, nc, tag + "u3p%d" % i, [128, 128], F32) for i in range(2)]
                        o_p = [_ps(sF2, nc, tag + "o_p%d" % i, [128, 128], F32) for i in range(2)]
                        trxp, trxp_b = _ps(sF2, nc, tag + "trxp", [128, 128], BF16)
                        trop, trop_b = _ps(sF2, nc, tag + "trop", [128, 128], F32)

                        w13view = w13b.ap().rearrange("e h p m l w -> (e h p) (m l w)")
                        w2view = w2b.ap().rearrange("e h p l w -> (e h p) (l w)")

                        idxw2 = [_sb(sF2, nc, tag + "idxw2_%d" % i, [128, 1], I32)
                                 for i in range(2)]

                        def ld_w13(blk, half):
                            up = (blk * 2 + half) % 2
                            wh, whb = w13u[up]
                            iw, iwb = idxw[up]
                            p.op("ve", lambda v, blk=blk, half=half, iw=iw: v.tensor_copy(
                                iw[:], widx[half][0][:, blk:blk + 1]),
                                r=[widx[half][1]], w=[iwb])
                            p.dma("ge", wh[:].rearrange("p a l w -> p (a l w)"), w13view,
                                  r=[iwb], w=[whb], indirect_in=(iw, 0))

                        def ld_w2(blk, half):
                            up = (blk * 2 + half) % 2
                            iw, iwb = idxw2[up]
                            p.op("ve", lambda v, blk=blk, half=half, iw=iw: v.tensor_copy(
                                iw[:], widx[half][0][:, blk:blk + 1]),
                                r=[widx[half][1]], w=[iwb])
                            p.dma("ge", w2u[:].rearrange("p l w -> p (l w)"), w2view,
                                  r=[iwb], w=[w2u_b], indirect_in=(iw, 0))

                        def ld_x(blk):
                            bp = blk % 2
                            xh, xb = xg[bp]
                            p.dma("qs", xh[:], xsl_d.ap()[blk * 128:(blk + 1) * 128, :],
                                  r=[yb["xsl_d"]], w=[xb])
                            p.flush("qs")

                        ld_x(0)
                        ld_w13(0, 0)
                        units = [(b_, h_) for b_ in range(40) for h_ in (0, 1)]
                        for ui, (blk, half) in enumerate(units):
                            bp = blk % 2
                            xh, xb = xg[bp]
                            if True:
                                up = (blk * 2 + half) % 2
                                wh, whb = w13u[up]
                                if ui + 1 < len(units):
                                    ld_w13(*units[ui + 1])
                                ld_w2(blk, half)
                                if half == 0 and blk < 39:
                                    ld_x(blk + 1)
                                p.flush("ge")
                                if half == 0:
                                    for dc in range(8):
                                        p.op("te", lambda t, dc=dc, xh=xh: t.transpose(
                                            trxp[:], xh[:, dc * 128:(dc + 1) * 128], identb[:]),
                                            r=[xb, identb_b], w=[trxp_b])
                                        p.op("se", lambda s, dc=dc: s.copy(xbt[:, dc, :], trxp[:]),
                                             r=[trxp_b], w=[xbt_b])
                                for hl in range(11):
                                    pu = hl % 2
                                    up1, up1_b = u1p[pu]
                                    up3, up3_b = u3p[pu]
                                    for dc in range(8):
                                        p.op("te", lambda t, dc=dc, hl=hl, wh=wh, up1=up1: t.matmul(
                                            up1[:], wh[:, 0, hl, dc * 128:(dc + 1) * 128],
                                            xbt[:, dc, :], start=(dc == 0), stop=(dc == 7)),
                                            r=[whb, xbt_b], w=[up1_b])
                                    for dc in range(8):
                                        p.op("te", lambda t, dc=dc, hl=hl, wh=wh, up3=up3: t.matmul(
                                            up3[:], wh[:, 1, hl, dc * 128:(dc + 1) * 128],
                                            xbt[:, dc, :], start=(dc == 0), stop=(dc == 7)),
                                            r=[whb, xbt_b], w=[up3_b])
                                    p.op("se", lambda s, up1=up1: s.activation(
                                        s1b[:], up1[:], AF.Silu), r=[up1_b], w=[s1b_b])
                                    p.op("ve", lambda v, hl=hl, up3=up3: v.tensor_tensor(
                                        hbl[:, hl, :], up3[:], s1b[:], ALU.mult),
                                        r=[up3_b, s1b_b], w=[hbl_b])
                                for dc in range(8):
                                    po = dc % 2
                                    oph, opb = o_p[po]
                                    for hl in range(11):
                                        p.op("te", lambda t, hl=hl, dc=dc, oph=oph: t.matmul(
                                            oph[:], w2u[:, hl, dc * 128:(dc + 1) * 128],
                                            hbl[:, hl, :], start=(hl == 0), stop=(hl == 10)),
                                            r=[w2u_b, hbl_b], w=[opb])
                                    if half == 0:
                                        p.op("ve", lambda v, dc=dc, oph=oph: v.tensor_copy(
                                            ofm[:, dc, :], oph[:]), r=[opb], w=[ofm_b])
                                    else:
                                        p.op("ve", lambda v, dc=dc, oph=oph: v.tensor_tensor(
                                            ofm[:, dc, :], ofm[:, dc, :], oph[:], ALU.add),
                                            r=[opb], w=[ofm_b])
                            if half == 1:
                                for dc in range(8):
                                    p.op("te", lambda t, dc=dc: t.transpose(
                                        trop[:], ofm[:, dc, :], identf[:]),
                                        r=[ofm_b, identf_b], w=[trop_b])
                                    p.op("ve", lambda v, dc=dc: v.tensor_copy(
                                        otm[:, dc * 128:(dc + 1) * 128], trop[:]),
                                        r=[trop_b], w=[otm_b])
                                p.dma("qs", osl_d.ap()[blk * 128:(blk + 1) * 128, :], otm[:],
                                      r=[otm_b], w=[yb["osl_d"]])
                                p.flush("qs")
                        for j in range(NB):
                            p.op("ve", lambda v, j=j: v.tensor_copy(ir1[:], sr1[:, j:j + 1]),
                                 r=[sr1_b], w=[ir1_b])
                            p.op("ve", lambda v, j=j: v.tensor_copy(ir2[:], sr2[:, j:j + 1]),
                                 r=[sr2_b], w=[ir2_b])
                            p.dma("ge", g1b[:], osl_d.ap(), r=[yb["osl_d"], ir1_b], w=[g1b_b],
                                  indirect_in=(ir1, 0))
                            p.dma("ge", g2b[:], osl_d.ap(), r=[yb["osl_d"], ir2_b], w=[g2b_b],
                                  indirect_in=(ir2, 0))
                            p.flush("ge")
                            p.op("ve", lambda v, j=j: v.tensor_tensor(
                                res[:, j, :], res[:, j, :], g1b[:], ALU.add),
                                r=[g1b_b], w=[res_b])
                            p.op("ve", lambda v, j=j: v.tensor_tensor(
                                res[:, j, :], res[:, j, :], g2b[:], ALU.add),
                                r=[g2b_b], w=[res_b])
                p.barrier()

            emit_ffn("f1", dump_sco=True)
            if "res2" in dumps:
                p.dma("qs", dumps["res2"].rearrange("(j p) d -> p j d", p=128), res[:],
                      r=[res_b])
                p.flush("qs")
                p.barrier()

            with ExitStack() as sH:
                sH.enter_context(nc.named_scope("hgrn"))
                X3T, X3T_b = _sb(sH, nc, "X3T", [128, 8, TOK], BF16)
                ost, ost_b = _sb(sH, nc, "ost", [128, 8, TOK], BF16)
                fcs, fcs_b = _sb(sH, nc, "fcs", [128, 8, TOK], BF16)
                zrow, zrow_b = _sb(sH, nc, "zrow", [128, TOK], BF16)
                bnd, bnd_b = _sb(sH, nc, "bnd", [128, 8, 3], BF16)
                hend, hend_b = _sb(sH, nc, "hend", [128, 8], F32)
                carry, carry_b = _sb(sH, nc, "carry", [128, 8], F32)
                cg0, cg0_b = _sb(sH, nc, "cg0", [128, 8], F32)
                bg0, bg0_b = _sb(sH, nc, "bg0", [128, 8, 3], BF16)

                with ExitStack() as sH0:
                    xlh, xlh_b = _sb(sH0, nc, "h_xl", [128, D], BF16)
                    sqh, sqh_b = _sb(sH0, nc, "h_sq", [128, D], F32)
                    sth, sth_b = _sb(sH0, nc, "h_st", [128, 8], F32)
                    psbH, psbH_b = _ps(sH0, nc, "h_psb", [128, 128], BF16)
                    for j in range(NB):
                        emit_ln(p, res[:, j, :], [res_b], xlh[:], xlh_b, sqh, sqh_b, sth, sth_b)
                        for dc in range(8):
                            p.op("te", lambda t, dc=dc: t.transpose(
                                psbH[:], xlh[:, dc * 128:(dc + 1) * 128], identb[:]),
                                r=[xlh_b, identb_b], w=[psbH_b])
                            p.op("se", lambda s, dc=dc, j=j: s.copy(
                                X3T[:, dc, j * 128:(j + 1) * 128], psbH[:]),
                                r=[psbH_b], w=[X3T_b])
                    p.op("ve", lambda v: v.tensor_copy(bnd[:], X3T[:, :, TOK - 3:TOK]),
                         r=[X3T_b], w=[bnd_b])
                    p.dma("qs", b_pub.ap(), bnd[:].rearrange("p h w -> p (h w)"),
                          r=[bnd_b], w=[yb["b_pub"]])
                    p.flush("qs")
                    p.cc("AllGather", ALU.bypass, b_pub.ap().opt(), b_ag.ap().opt(),
                         r=[yb["b_pub"]], w=[yb["b_ag"]])
                    p.dma("qs", bg0[:].rearrange("p h w -> p (h w)"), b_ag.ap()[0:128, :],
                          r=[yb["b_ag"]], w=[bg0_b])
                    p.flush("qs")
                    p.op("ve", lambda v: v.tensor_scalar_mul(bnd[:], bg0[:], mBs[:]),
                         r=[bg0_b, mBs_b], w=[bnd_b])
                p.barrier()

                def hgrn_pass(pass2):
                    with ExitStack() as sP:
                        wic = [_sb(sP, nc, "hwi%d%d" % (pass2, i), [128, 8, 128], BF16)
                               for i in range(2)]
                        wfc = [_sb(sP, nc, "hwf%d%d" % (pass2, i), [128, 8, 128], BF16)
                               for i in range(2)]
                        ipd, ipd_b = _sb(sP, nc, "hip%d" % pass2, [128, TOK + 3], BF16)
                        fpd, fpd_b = _sb(sP, nc, "hfp%d" % pass2, [128, TOK + 3], BF16)
                        fgr, fgr_b = _sb(sP, nc, "hfg%d" % pass2, [128, TOK], F32)
                        iir, iir_b = _sb(sP, nc, "hii%d" % pass2, [128, TOK], BF16)
                        c1, c1_b = _sb(sP, nc, "hc1%d" % pass2, [128, 512], F32)
                        c2, c2_b = _sb(sP, nc, "hc2%d" % pass2, [128, 512], F32)
                        sgx, sgx_b = c2, c2_b
                        slx, slx_b = c1, c1_b
                        ipp = [_ps(sP, nc, "hipp%d%d" % (pass2, i), [128, 512], F32)
                               for i in range(2)]
                        fpp = [_ps(sP, nc, "hfpp%d%d" % (pass2, i), [128, 512], F32)
                               for i in range(2)]
                        bpp, bpp_b = _ps(sP, nc, "hbpp%d" % pass2, [128, 3], F32)

                        for oc in range(8):
                            par = oc % 2
                            wih, wib = wic[par]
                            wfh, wfb = wfc[par]
                            p.dma("ge", wih[:], wi[oc], w=[wib])
                            p.dma("ge", wfh[:], wf[oc], w=[wfb])
                            p.flush("ge")
                            for (wh, wb, pad, padb) in ((wih, wib, ipd, ipd_b),
                                                        (wfh, wfb, fpd, fpd_b)):
                                for dc in range(8):
                                    p.op("te", lambda t, dc=dc, wh=wh: t.matmul(
                                        bpp[:], wh[:, dc, :], bnd[:, dc, :],
                                        start=(dc == 0), stop=(dc == 7)),
                                        r=[wb, bnd_b], w=[bpp_b])
                                p.op("se", lambda s, pad=pad: s.copy(pad[:, 0:3], bpp[:]),
                                     r=[bpp_b], w=[padb])
                                pp = ipp if pad is ipd else fpp
                                for s4 in range(4):
                                    ph, pb = pp[s4 % 2]
                                    for dc in range(8):
                                        p.op("te", lambda t, dc=dc, s4=s4, wh=wh, ph=ph: t.matmul(
                                            ph[:], wh[:, dc, :],
                                            X3T[:, dc, s4 * 512:(s4 + 1) * 512],
                                            start=(dc == 0), stop=(dc == 7)),
                                            r=[wb, X3T_b], w=[pb])
                                    p.op("se", lambda s, s4=s4, ph=ph, pad=pad: s.copy(
                                        pad[:, 3 + s4 * 512: 3 + (s4 + 1) * 512], ph[:]),
                                        r=[pb], w=[padb])
                            for s4 in range(4):
                                sl = slice(s4 * 512, (s4 + 1) * 512)
                                p.op("ve", lambda v, s4=s4, oc=oc: v.tensor_scalar_mul(
                                    c1[:], ipd[:, s4 * 512:s4 * 512 + 512], cwi[:, oc, 0:1]),
                                    r=[ipd_b, cwi_b], w=[c1_b])
                                for k in range(1, 4):
                                    p.op("ve", lambda v, s4=s4, k=k, oc=oc: v.scalar_tensor_tensor(
                                        c1[:], ipd[:, s4 * 512 + k:s4 * 512 + k + 512],
                                        cwi[:, oc, k:k + 1], c1[:], ALU.mult, ALU.add),
                                        r=[ipd_b, cwi_b], w=[c1_b])
                                p.op("ve", lambda v, s4=s4, oc=oc: v.tensor_scalar_mul(
                                    c2[:], fpd[:, s4 * 512:s4 * 512 + 512], cwf[:, oc, 0:1]),
                                    r=[fpd_b, cwf_b], w=[c2_b])
                                for k in range(1, 4):
                                    p.op("ve", lambda v, s4=s4, k=k, oc=oc: v.scalar_tensor_tensor(
                                        c2[:], fpd[:, s4 * 512 + k:s4 * 512 + k + 512],
                                        cwf[:, oc, k:k + 1], c2[:], ALU.mult, ALU.add),
                                        r=[fpd_b, cwf_b], w=[c2_b])
                                p.op("se", lambda s, sl=sl: s.activation(
                                    fgr[:, sl], c2[:], AF.Sigmoid), r=[c2_b], w=[fgr_b])
                                p.op("se", lambda s: s.activation(
                                    sgx[:], c1[:], AF.Sigmoid), r=[c1_b], w=[sgx_b])
                                p.op("ve", lambda v: v.tensor_tensor(
                                    slx[:], c1[:], sgx[:], ALU.mult), r=[c1_b, sgx_b], w=[slx_b])
                                p.op("ve", lambda v, sl=sl: v.tensor_scalar(
                                    sgx[:], fgr[:, sl], -1.0, 1.0, ALU.mult, ALU.add),
                                    r=[fgr_b], w=[sgx_b])
                                p.op("ve", lambda v, sl=sl: v.tensor_tensor(
                                    iir[:, sl], slx[:], sgx[:], ALU.mult),
                                    r=[slx_b, sgx_b], w=[iir_b])
                            p.op("ve", lambda v, oc=oc: v.tensor_tensor_scan(
                                ost[:, oc, :], fgr[:], iir[:], 0.0, ALU.mult, ALU.add),
                                r=[fgr_b, iir_b], w=[ost_b])
                            p.op("ve", lambda v, oc=oc: v.tensor_tensor_scan(
                                fcs[:, oc, :], fgr[:], zrow[:], 1.0, ALU.mult, ALU.add),
                                r=[fgr_b, zrow_b], w=[fcs_b])
                            p.op("ve", lambda v, oc=oc: v.tensor_copy(
                                hend[:, oc:oc + 1], ost[:, oc, TOK - 1:TOK]),
                                r=[ost_b], w=[hend_b])

                p.op("ve", lambda v: v.memset(zrow[:], 0.0), w=[zrow_b])
                hgrn_pass(False)
                p.barrier()
                p.dma("qs", c_pub.ap(), hend[:], r=[hend_b], w=[yb["c_pub"]])
                p.flush("qs")
                p.cc("AllGather", ALU.bypass, c_pub.ap().opt(), c_ag.ap().opt(),
                     r=[yb["c_pub"]], w=[yb["c_ag"]])
                p.dma("qs", cg0[:], c_ag.ap()[0:128, :], r=[yb["c_ag"]], w=[cg0_b])
                p.flush("qs")
                p.op("ve", lambda v: v.tensor_scalar_mul(carry[:], cg0[:], mBs[:]),
                     r=[cg0_b, mBs_b], w=[carry_b])
                for oc in range(8):
                    p.op("ve", lambda v, oc=oc: v.scalar_tensor_tensor(
                        ost[:, oc, :], fcs[:, oc, :], carry[:, oc:oc + 1],
                        ost[:, oc, :], ALU.mult, ALU.add),
                        r=[fcs_b, carry_b], w=[ost_b])
                p.barrier()

                with ExitStack() as sN:
                    o2s, o2s_b = _sb(sN, nc, "n_o2", [128, TOK], BF16)
                    rrow, rrow_b = _sb(sN, nc, "n_rr", [1, TOK], F32)
                    gss, gss_b = _sb(sN, nc, "n_gs", [128, 512], F32)
                    gsl, gsl_b = _sb(sN, nc, "n_gl", [128, 512], F32)
                    ons, ons_b = _sb(sN, nc, "n_on", [128, 512], F32)
                    wgc = [_sb(sN, nc, "n_wg%d" % i, [128, 8, 128], BF16) for i in range(2)]
                    with ExitStack() as sN1:
                        rmp, rmp_b = _ps(sN1, nc, "n_rmp", [1, 4, 512], F32)
                        for oc in range(8):
                            p.op("ve", lambda v, oc=oc: v.tensor_tensor(
                                o2s[:], ost[:, oc, :], ost[:, oc, :], ALU.mult),
                                r=[ost_b], w=[o2s_b])
                            for s4 in range(4):
                                p.op("te", lambda t, s4=s4, oc=oc: t.matmul(
                                    rmp[:, s4, :], onescb[:],
                                    o2s[:, s4 * 512:(s4 + 1) * 512],
                                    start=(oc == 0), stop=(oc == 7)),
                                    r=[onescb_b, o2s_b], w=[rmp_b])
                        p.op("se", lambda s: s.copy(rrow[:], rmp[0:1, :, :].rearrange(
                            "x a b -> x (a b)")), r=[rmp_b], w=[rrow_b])
                    p.barrier()
                    p.op("se", lambda s: s.activation(rrow[:], rrow[:], AF.Ln,
                                                      scale=1.0 / D, bias=RMS_EPS),
                         r=[rrow_b], w=[rrow_b])
                    p.op("se", lambda s: s.activation(rrow[:], rrow[:], AF.Exp, scale=-0.5),
                         r=[rrow_b], w=[rrow_b])
                    with ExitStack() as sN2:
                        rbp, rbp_b = _ps(sN2, nc, "n_rbp", [128, 4, 512], F32)
                        gpp = [_ps(sN2, nc, "n_gp%d" % i, [128, 512], F32) for i in range(2)]
                        for s4 in range(4):
                            p.op("te", lambda t, s4=s4: t.matmul(
                                rbp[:, s4, :], onesf[:], rrow[0:1, s4 * 512:(s4 + 1) * 512],
                                start=True, stop=True),
                                r=[onesf_b, rrow_b], w=[rbp_b])
                        for oc in range(8):
                            par = oc % 2
                            wgh, wgb = wgc[par]
                            p.dma("ge", wgh[:], wg[oc], w=[wgb])
                            p.flush("ge")
                            for s4 in range(4):
                                gph, gpb = gpp[s4 % 2]
                                for dc in range(8):
                                    p.op("te", lambda t, dc=dc, s4=s4, wgh=wgh, gph=gph: t.matmul(
                                        gph[:], wgh[:, dc, :],
                                        X3T[:, dc, s4 * 512:(s4 + 1) * 512],
                                        start=(dc == 0), stop=(dc == 7)),
                                        r=[wgb, X3T_b], w=[gpb])
                                p.op("se", lambda s, gph=gph: s.activation(
                                    gss[:], gph[:], AF.Sigmoid), r=[gpb], w=[gss_b])
                                p.op("ve", lambda v, gph=gph: v.tensor_tensor(
                                    gsl[:], gph[:], gss[:], ALU.mult), r=[gpb, gss_b], w=[gsl_b])
                                sl = slice(s4 * 512, (s4 + 1) * 512)
                                p.op("ve", lambda v, oc=oc, s4=s4, sl=sl: v.scalar_tensor_tensor(
                                    ons[:], ost[:, oc, sl], gnws[:, oc:oc + 1], rbp[:, s4, :],
                                    ALU.mult, ALU.mult),
                                    r=[ost_b, gnws_b, rbp_b], w=[ons_b])
                                p.op("ve", lambda v, oc=oc, sl=sl: v.tensor_tensor(
                                    ost[:, oc, sl], ons[:], gsl[:], ALU.mult),
                                    r=[ons_b, gsl_b], w=[ost_b])
                    p.barrier()
                    with ExitStack() as sN3:
                        wop = [_ps(sN3, nc, "n_wo%d" % i, [128, 512], F32) for i in range(2)]
                        trh, trh_b = _ps(sN3, nc, "n_trh", [128, 4, 128], BF16)
                        owh, owh_b = _sb(sN3, nc, "n_ow", [128, 512], BF16)
                        woc = [_sb(sN3, nc, "n_wc%d" % i, [128, 8, 128], BF16) for i in range(2)]
                        for ooc in range(8):
                            par = ooc % 2
                            wch, wcb = woc[par]
                            p.dma("ge", wch[:], wo[ooc], w=[wcb])
                            p.flush("ge")
                            for tt in range(4):
                                oph, opb = wop[tt % 2]
                                for dc in range(8):
                                    p.op("te", lambda t, dc=dc, tt=tt, wch=wch, oph=oph: t.matmul(
                                        oph[:], wch[:, dc, :],
                                        ost[:, dc, tt * 512:(tt + 1) * 512],
                                        start=(dc == 0), stop=(dc == 7)),
                                        r=[wcb, ost_b], w=[opb])
                                p.op("ve", lambda v, oph=oph: v.tensor_copy(owh[:], oph[:]),
                                     r=[opb], w=[owh_b])
                                for k in range(4):
                                    p.op("te", lambda t, k=k: t.transpose(
                                        trh[:, k, :], owh[:, k * 128:(k + 1) * 128], identb[:]),
                                        r=[owh_b, identb_b], w=[trh_b])
                                p.op("ve", lambda v, tt=tt, ooc=ooc: v.tensor_tensor(
                                    res[:, tt * 4:(tt + 1) * 4, ooc * 128:(ooc + 1) * 128],
                                    res[:, tt * 4:(tt + 1) * 4, ooc * 128:(ooc + 1) * 128],
                                    trh[:], ALU.add),
                                    r=[trh_b], w=[res_b])
            p.barrier()
            if "res3" in dumps:
                p.dma("qs", dumps["res3"].rearrange("(j p) d -> p j d", p=128), res[:],
                      r=[res_b])
                p.flush("qs")
                p.barrier()

            emit_ffn("f2")

            for j in range(NB):
                p.dma("qs", y_out[j * 128:(j + 1) * 128, :], res[:, j, :], r=[res_b])
            p.flush("qs")
        p.barrier()
    return nc


def kernel(**inputs):
    x = np.ascontiguousarray(np.asarray(inputs["L0_attn_a_x"], dtype=np.float32))
    W = np.asarray(inputs["L0_attn_a_W"], dtype=np.float32)
    Wi = np.ascontiguousarray(np.asarray(inputs["L0_attn_b_Wi"], dtype=np.float32))
    Wf = np.ascontiguousarray(np.asarray(inputs["L0_attn_b_Wf"], dtype=np.float32))
    Wg = np.ascontiguousarray(np.asarray(inputs["L0_attn_b_Wg"], dtype=np.float32))
    Wo = np.ascontiguousarray(np.asarray(inputs["L0_attn_b_Wo"], dtype=np.float32))
    ci = np.asarray(inputs["L0_attn_b_conv_i"], dtype=np.float32)
    cf = np.asarray(inputs["L0_attn_b_conv_f"], dtype=np.float32)
    gn = np.asarray(inputs["L0_attn_b_g_norm_weight"], dtype=np.float32)
    gw = np.asarray(inputs["L0_ffn_gate_w"], dtype=np.float32)
    eb = np.asarray(inputs["L0_ffn_expert_bias"], dtype=np.float32)
    w1 = np.ascontiguousarray(np.asarray(inputs["L0_ffn_w1"], dtype=np.float32))
    w2 = np.ascontiguousarray(np.asarray(inputs["L0_ffn_w2"], dtype=np.float32))
    w3 = np.ascontiguousarray(np.asarray(inputs["L0_ffn_w3"], dtype=np.float32))

    w1r = np.ascontiguousarray(w1.reshape(8, 8, 128, 22, 128).transpose(0, 3, 2, 1, 4)
                               .reshape(8, 22, 128, 1024))
    w3r = np.ascontiguousarray(w3.reshape(8, 8, 128, 22, 128).transpose(0, 3, 2, 1, 4)
                               .reshape(8, 22, 128, 1024))
    w2r = np.ascontiguousarray(w2.reshape(8, 2, 11, 128, 1024).transpose(0, 1, 3, 2, 4))
    def colmajor(m):
        return np.ascontiguousarray(m.reshape(8, 128, 8, 128).transpose(2, 1, 0, 3))
    wir, wfr, wgr_, wor = colmajor(Wi), colmajor(Wf), colmajor(Wg), colmajor(Wo)
    ci_l = np.ascontiguousarray(ci.reshape(8, 128, 4).transpose(1, 0, 2))
    cf_l = np.ascontiguousarray(cf.reshape(8, 128, 4).transpose(1, 0, 2))
    gn_l = np.ascontiguousarray(gn.reshape(8, 128).T)
    gwt_l = np.ascontiguousarray(gw.T.reshape(8, 128, 8).transpose(1, 0, 2))
    eb_l = np.ascontiguousarray(eb.reshape(1, 8))

    nc = build_nc()
    in_maps = []
    for c in range(8):
        b, hf = c // 2, c % 2
        wq = W[hf * 512:(hf + 1) * 512]
        wk = W[1024 + hf * 512:1024 + (hf + 1) * 512]
        wv = W[2048 + hf * 512:2048 + (hf + 1) * 512]
        wgr = W[3072 + hf * 8:3072 + (hf + 1) * 8]
        wqkvT = np.ascontiguousarray(np.concatenate([wq, wk, wv, wgr], axis=0).T)
        im = {
            "x_own": np.ascontiguousarray(x[b, hf * TOK:(hf + 1) * TOK]),
            "x_full": np.ascontiguousarray(x[b]),
            "w_qkvT": wqkvT,
            "wi": wir, "wf": wfr, "wg": wgr_, "wo": wor,
            "conv_i": ci_l, "conv_f": cf_l, "gnw": gn_l, "gwt": gwt_l, "ebias": eb_l,
            "w1": w1r, "w2": w2r, "w3": w3r,
            "mA": np.full((128, 1), 1.0 - hf, np.float32),
            "mB": np.full((128, 1), float(hf), np.float32),
        }
        in_maps.append(im)
    res = run_bass_kernel_spmd(nc, in_maps, core_ids=list(range(8)))
    out = np.zeros((4, T, D), np.float32)
    for c in range(8):
        b, hf = c // 2, c % 2
        out[b, hf * TOK:(hf + 1) * TOK] = res.results[c]["y_out"]
    return out
